# version 11
# baseline (speedup 1.0000x reference)
import contextlib
import numpy as np
import concourse.bass as bass
import concourse.mybir as mybir

F32 = mybir.dt.float32
BF16 = mybir.dt.bfloat16
AF = mybir.ActivationFunctionType
ALU = mybir.AluOpType
AX = mybir.AxisListType

ENGINES = ("sync", "scalar", "vector", "gpsimd", "tensor")


class Prog:
    def __init__(self):
        self.nc = bass.Bass("TRN2", target_bir_lowering=False, debug=False)
        self.es = contextlib.ExitStack()
        self.sched = {e: [] for e in ENGINES}
        self.waited = {e: {} for e in ENGINES}
        self.sems = {}
        self.boot = self.es.enter_context(self.nc.semaphore("boot"))
        self.jka = self.es.enter_context(self.nc.sbuf_tensor("jka", [128, 64], mybir.dt.float32))
        self.jkb = self.es.enter_context(self.nc.sbuf_tensor("jkb", [128, 64], mybir.dt.float32))

    def sem(self, name):
        s = self.es.enter_context(self.nc.semaphore(name))
        self.sems[name] = [s, 0]
        return name

    def sb(self, stack, name, shape, dtype):
        return stack.enter_context(self.nc.sbuf_tensor(name, list(shape), dtype))

    def ps(self, name, shape, dtype=F32):
        return self.es.enter_context(self.nc.psum_tensor(name, list(shape), dtype))

    def junk(self, eng, engine):
        if eng == "vector":
            return engine.tensor_copy(self.jkb[:], self.jka[:])
        if eng == "scalar":
            return engine.activation(out=self.jkb[:], in_=self.jka[:],
                                     func=AF.Identity)
        if eng == "gpsimd":
            return engine.tensor_copy(self.jkb[:], self.jka[:])
        raise ValueError(eng)

    def emit(self, eng, fn, deps=(), inc=None, inc_amt=1, pad=False):
        deps = [(s, c) for (s, c) in deps if c > self.waited[eng].get(s, 0)]
        for s, c in deps:
            self.waited[eng][s] = c
        if inc is not None:
            self.sems[inc][1] += inc_amt
        def run(engine, deps=deps, fn=fn, inc=inc, inc_amt=inc_amt, pad=pad):
            for s, c in deps:
                engine.wait_ge(self.sems[s][0], c)
            ret = fn(engine)
            if pad:
                ret = self.junk(eng, engine)
            if inc is not None:
                assert ret is not None
                ret.then_inc(self.sems[inc][0], inc_amt)
        self.sched[eng].append(run)

    def count(self, sem):
        return self.sems[sem][1]

    def bar(self, sem):
        return (sem, self.sems[sem][1])

    def build(self):
        nums = sorted([h.num for h, _ in self.sems.values()] + [self.boot.num])
        lo, hi = min(nums), max(nums) + 1
        rng = range(lo, hi)
        finals = [(h, c) for (h, c) in self.sems.values()]
        block = self.es.enter_context(self.nc.Block())
        for eng in ENGINES:
            def body(engine, eng=eng, rng=rng, finals=finals):
                if eng == "gpsimd":
                    engine.dma_reset(rng)
                    engine.sem_clear(rng)
                    engine.engine_nop().then_inc(self.boot, 1)
                else:
                    engine.wait_ge(self.boot, 1)
                for fn in self.sched[eng]:
                    fn(engine)
                if eng == "gpsimd":
                    engine.wait_ge(self.boot, 5)
                    for h, c in finals:
                        if c > 0:
                            engine.wait_ge(h, c)
                    engine.dma_reset(rng)
                    engine.sem_clear(rng)
                else:
                    engine.nop().then_inc(self.boot, 1)
            getattr(block, eng)(body)
        return self.nc


class BankPipe:

    def __init__(self, p, name, shape=(128, 512), n=2, dtype=F32):
        self.p = p
        self.banks = [p.ps(f"{name}{i}", shape, dtype) for i in range(n)]
        self.pe_done = p.sem(f"{name}_pd")
        self.ev_done = p.sem(f"{name}_ed")
        self.n = n
        self.uses = 0

    def produce(self, fn, deps=()):
        i = self.uses % self.n
        bank = self.banks[i]
        d = list(deps)
        if self.uses >= self.n:
            d.append((self.ev_done, self.uses - self.n + 1))
        self.uses += 1
        self.p.emit("tensor", lambda t, b=bank, fn=fn: fn(t, b), deps=d,
                    inc=self.pe_done)
        return bank

    def consume(self, eng, fn, deps=(), pad=False):
        i = (self.uses - 1) % self.n
        bank = self.banks[i]
        d = list(deps) + [(self.pe_done, self.uses)]
        self.p.emit(eng, lambda e, b=bank, fn=fn: fn(e, b), deps=d,
                    inc=self.ev_done, pad=pad)


def build_program(dbg_ffn_input=False, dbg_taps=False, T=8192):
    DBG = dbg_taps
    """Returns (nc, names). If dbg_ffn_input, rs_out is an ExternalInput."""
    p = Prog()
    nc = p.nc

    TC = 256
    NCH = T // TC
    NSUB = TC // 128
    IC, IF, NST, KC = 1024, 2048, 16, 4
    NIT = IC // 128
    NCT = IF // 128
    TOKF = T // 2
    NSL = TOKF // 512
    EPS_LN, EPS_RMS = 1e-5, 1e-6
    EE, FFD = 8, 4096

    def din(name, shape, dtype=F32):
        return nc.dram_tensor(name, list(shape), dtype, kind="ExternalInput").ap()

    x = din("x", [T, 1024])
    xr = din("xr", [TOKF, 1024])
    in_w = din("in_w", [1024, IF + IC])
    in_b = din("in_b", [128, 24])
    conv_w = din("conv_w", [128, 16, KC])
    conv_b = din("conv_b", [128, 16])
    xproj_w = din("xproj_w", [IF, 96])
    dt_w = din("dt_w", [64, IC])
    dt_b = din("dt_b", [128, 8])
    A_log = din("A_log", [IC, NST])
    Dvec = din("D", [128, 8])
    out_w = din("out_w", [IC, 1024])
    out_b = din("out_b", [1, 1024])
    norm_scale = din("norm_scale", [128, 8])
    router_wT = din("router_wT", [1024, EE])
    w1T = din("w1T", [EE, 1024, FFD])
    b1 = din("b1", [128, 32, EE])
    w2T = din("w2T", [EE, FFD, 1024])
    b2 = din("b2", [128, 8, EE])

    out_hbm = nc.dram_tensor("out", [TOKF, 1024], F32, kind="ExternalOutput").ap()

    cc_in = nc.dram_tensor("cc_in", [2048, TOKF], F32)
    if dbg_ffn_input:
        rs_out = nc.dram_tensor("rs_out", [1024, TOKF], F32, kind="ExternalInput")
    else:
        rs_out = nc.dram_tensor("rs_out", [1024, TOKF], F32)
    h2n_dram = nc.dram_tensor("h2n_dram", [1024, TOKF], BF16)
    h2_dram = nc.dram_tensor("h2_dram", [TOKF, 1024], F32)
    acc_fm = nc.dram_tensor("acc_fm", [1024, TOKF], F32)

    s_dx = p.sem("s_dx")
    s_dw = p.sem("s_dw")
    s_v = p.sem("s_v")
    s_a = p.sem("s_a")
    s_g = p.sem("s_g")
    s_do = p.sem("s_do")
    s_cc = p.sem("s_cc")
    s_dh2 = p.sem("s_dh2")
    s_dhn = p.sem("s_dhn")
    s_acc = p.sem("s_acc")
    s_dwf = p.sem("s_dwf")
    s_gp = p.sem("s_gp")
    s_out = p.sem("s_out")

    pipeT = BankPipe(p, "pT", n=2)
    pipeM = BankPipe(p, "pM", n=2)
    pipeB = BankPipe(p, "pB", n=2)
    pipeC = BankPipe(p, "pC", n=2)

    ident = p.sb(p.es, "ident", [128, 128], F32)
    iotf = p.sb(p.es, "iotf", [128, 128], F32)
    iotc = p.sb(p.es, "iotc", [128, 1], F32)
    stat = p.sb(p.es, "stat", [128, 4, 8], F32)
    CVALS = [EPS_LN, EPS_RMS, 1.0, -0.5, 0.5, 1.0 / 1024,
             (1.0 + EPS_RMS) / 1024, 1e-11]
    cst = p.sb(p.es, "cst", [128, len(CVALS)], F32)
    for j, v in enumerate(CVALS):
        nc.const_aps.register(F32, v, cst[:, j:j + 1])

    def g_dma(dst, src, inc="s_dw", deps=()):
        p.emit("gpsimd", lambda g, d=dst, s=src: g.dma_start(out=d, in_=s),
               deps=deps, inc=inc, inc_amt=16)

    def s_dma(dst, src, inc="s_dw", deps=()):
        p.emit("sync", lambda s_, d=dst, s=src: s_.dma_start(out=d, in_=s),
               deps=deps, inc=inc, inc_amt=16)

    def emit_ln(xin, xout, ns, rms, xdep, junk, jdeps=()):
        st_ = lambda c: stat[:, 0:ns, c]
        for s in range(ns):
            p.emit("vector", lambda v, s=s: v.reduce_sum(
                stat[:, s, 0:1], xin[:, s, :], axis=AX.X),
                deps=[xdep, G_INIT], inc=s_v, pad=(s == ns - 1))
        SM = p.bar(s_v)
        for s in range(ns):
            p.emit("scalar", lambda a, s=s, junk=junk: a.activation(
                out=junk, in_=xin[:, s, :], func=AF.Square,
                accum_out=stat[:, s, 1:2]), deps=[xdep] + list(jdeps),
                inc=s_a, pad=(s == ns - 1))
        SQ = p.bar(s_a)
        def v_u(v):
            v.tensor_tensor(out=st_(3), in0=st_(0), in1=st_(0), op=ALU.mult)
            v.tensor_scalar(out=st_(2), in0=st_(0), scalar1=1.0 / 1024,
                            scalar2=None, op0=ALU.mult)
            return None
        p.emit("vector", v_u, deps=[SM])
        p.emit("vector", lambda v: p.junk("vector", v))
        def v_w(v):
            return v.scalar_tensor_tensor(out=st_(4), in0=st_(3),
                                          scalar=-1.0 / 1024, in1=st_(1),
                                          op0=ALU.mult, op1=ALU.add)
        p.emit("vector", v_w, deps=[SQ], inc=s_v, pad=True)
        VW = p.bar(s_v)
        if rms:
            sc, bi = (1.0 + EPS_RMS) / 1024, 1e-11
        else:
            sc, bi = 1.0 / 1024, EPS_LN
        def a_sc(a, sc=sc, bi=bi):
            a.activation(out=st_(5), in_=st_(4), func=AF.Ln, scale=sc, bias=bi)
            return a.activation(out=st_(5), in_=st_(5), func=AF.Exp, scale=-0.5)
        p.emit("scalar", a_sc, deps=[VW, G_INIT], inc=s_a, pad=True)
        SCL = p.bar(s_a)
        for s in range(ns):
            p.emit("vector", lambda v, s=s: v.tensor_scalar(
                out=xout[:, s, :], in0=xin[:, s, :], scalar1=stat[:, s, 2:3],
                scalar2=stat[:, s, 5:6], op0=ALU.subtract, op1=ALU.mult),
                deps=[SCL], inc=s_v)
        return p.bar(s_v)

    es_attn = contextlib.ExitStack()
    sel16 = p.sb(es_attn, "sel16", [16, 16 * 128], F32)
    state = p.sb(es_attn, "state", [128, NIT, NST], F32)
    in_w_sb = p.sb(es_attn, "in_w_sb", [128, 8, IF + IC], BF16)
    xproj_w_sb = p.sb(es_attn, "xproj_w_sb", [128, 16, 96], BF16)
    dt_w_sb = p.sb(es_attn, "dt_w_sb", [64, IC], BF16)
    out_w_sb = p.sb(es_attn, "out_w_sb", [128, 8, 1024], BF16)
    in_b_sb = p.sb(es_attn, "in_b_sb", [128, 24], F32)
    conv_b_sb = p.sb(es_attn, "conv_b_sb", [128, 16], F32)
    conv_w_sb = p.sb(es_attn, "conv_w_sb", [128, 16, KC], F32)
    dt_b_sb = p.sb(es_attn, "dt_b_sb", [128, 8], F32)
    A_sb = p.sb(es_attn, "A_sb", [128, 8, NST], F32)
    D_sb = p.sb(es_attn, "D_sb", [128, 8], F32)
    ns_sb = p.sb(es_attn, "ns_sb", [128, 8], F32)
    xt = p.sb(es_attn, "xt", [128, NSUB, 1024], F32)
    xnt = xt
    xn_fm = p.sb(es_attn, "xn_fm", [128, 8, TC], BF16)
    hid = p.sb(es_attn, "hid", [128, NCT, TC + 3], BF16)
    cacc = p.sb(es_attn, "cacc", [128, 2, TC], F32)
    conv_s = p.sb(es_attn, "conv_s", [128, NIT, TC], F32)
    conv_bf = p.sb(es_attn, "conv_bf", [128, NCT, TC], BF16)
    gate_s = p.sb(es_attn, "gate_s", [128, NIT, TC], F32)
    ts_bf = p.sb(es_attn, "ts_bf", [64, TC], BF16)
    B_sbuf = p.sb(es_attn, "B_sbuf", [16, TC], F32)
    C_sbuf = p.sb(es_attn, "C_sbuf", [16, TC], F32)
    delta = p.sb(es_attn, "delta", [128, NIT, TC], F32)
    dh = p.sb(es_attn, "dh", [128, NIT, TC], F32)
    dA_b = [p.sb(es_attn, f"dA_b{i}", [128, NIT, TC + 1], F32)
            for i in range(2)]
    X_b = [p.sb(es_attn, f"X_b{i}", [128, NIT, TC + 1], F32)
           for i in range(2)]
    sout = p.sb(es_attn, "sout", [128, NIT, TC + 1], F32)
    tmpy = p.sb(es_attn, "tmpy", [128, NIT, TC], F32)
    y_acc = p.sb(es_attn, "y_acc", [128, NIT, TC], F32)
    ygate = p.sb(es_attn, "ygate", [128, NIT, TC], BF16)
    Brep_sb = p.sb(es_attn, "Brep_sb", [128, TC], F32)
    Crep_sb = p.sb(es_attn, "Crep_sb", [128, TC], F32)
    o_sb = cacc
    junk = conv_bf[:, 0:4, :].rearrange("p a t -> p (a t)")

    def g_init(g):
        g.iota(iotf[:], pattern=[[1, 128]], base=0, channel_multiplier=0,
               allow_small_or_imprecise_dtypes=True)
        g.iota(iotc[:], pattern=[[0, 1]], base=0, channel_multiplier=1,
               allow_small_or_imprecise_dtypes=True)
        g.iota(sel16[:], pattern=[[1, 16], [0, 128]], base=0, channel_multiplier=0,
               allow_small_or_imprecise_dtypes=True)
        g.memset(state[:], 0.0)
        for j, v in enumerate(CVALS):
            g.memset(cst[:, j:j + 1], v)
        return g.memset(stat[:], 0.0)
    p.emit("gpsimd", g_init, inc=s_g)
    G_INIT = p.bar(s_g)

    def v_init(v):
        v.tensor_scalar(out=ident[:], in0=iotf[:], scalar1=iotc[:], scalar2=None,
                        op0=ALU.is_equal)
        return v.tensor_scalar(out=sel16[:], in0=sel16[:], scalar1=iotc[0:16, :],
                               scalar2=None, op0=ALU.is_equal)
    p.emit("vector", v_init, deps=[G_INIT], inc=s_v)
    V_INIT = p.bar(s_v)


    g_dma(in_w_sb[:], in_w.rearrange("(a p) c -> p a c", p=128))
    g_dma(xproj_w_sb[:], xproj_w.rearrange("(a p) c -> p a c", p=128))
    g_dma(dt_w_sb[:], dt_w)
    g_dma(out_w_sb[:], out_w.rearrange("(a p) c -> p a c", p=128))
    s_dma(in_b_sb[:], in_b)
    s_dma(conv_b_sb[:], conv_b)
    s_dma(conv_w_sb[:], conv_w)
    s_dma(dt_b_sb[:], dt_b)
    s_dma(D_sb[:], Dvec)
    s_dma(ns_sb[:], norm_scale)
    s_dma(A_sb[:], A_log.rearrange("(a p) n -> p a n", p=128))
    W_INIT = p.bar(s_dw)

    def a_initA(a):
        ap = A_sb[:].rearrange("p a n -> p (a n)")
        return a.activation(out=ap, in_=ap, func=AF.Exp)
    p.emit("scalar", a_initA, deps=[W_INIT], inc=s_a)
    def v_negA(v):
        ap = A_sb[:].rearrange("p a n -> p (a n)")
        return v.tensor_scalar(out=ap, in0=ap, scalar1=-1.0, scalar2=None,
                               op0=ALU.mult)
    p.emit("vector", v_negA, deps=[p.bar(s_a)], inc=s_v, pad=True)
    A_READY = p.bar(s_v)

    def g_zeros(g):
        g.memset(dA_b[0][:, :, 0:1], 0.0)
        g.memset(dA_b[1][:, :, 0:1], 0.0)
        return g.memset(hid[:, :, 0:3], 0.0)
    p.emit("gpsimd", g_zeros, inc=s_g)
    HALO0 = p.bar(s_g)

    prev = {}
    for k in range(NCH):
        half = (k * TC) // TOKF
        tloc = k * TC - half * TOKF

        d0 = [prev["xn_v"], prev["sq_a"]] if k else []
        s_dma(xt[:], x[k * TC:(k + 1) * TC, :].rearrange("(a p) h -> p a h", p=128),
              inc=s_dx, deps=d0)
        XRDY = (s_dx, 16 * (k + 1))

        lnw = [prev["xnt_pe"]] if k else []
        if k:
            p.emit("vector", lambda v: v.engine_nop(), deps=[prev["xnt_pe"]],
                   inc=s_v)
        XN = emit_ln(xt, xnt, NSUB, True, XRDY, junk,
                     jdeps=[(s_do, 64 * k)] if k else [])
        prev["xn_v"] = XN
        prev["sq_a"] = p.bar(s_a)

        inproj_pe_guard = prev.get("inproj_pe")
        for ht in range(8):
            def mk_t(t, bank, ht=ht):
                last = None
                for s in range(NSUB):
                    last = t.transpose(bank[:, s * 128:(s + 1) * 128],
                                       xnt[:, s, ht * 128:(ht + 1) * 128], ident[:])
                return last
            pipeT.produce(mk_t, deps=[XN, V_INIT])
            evdeps = [W_INIT] + ([inproj_pe_guard] if inproj_pe_guard else [])
            def ev_t(v, bank, ht=ht):
                return v.tensor_scalar(out=xn_fm[:, ht, :], in0=bank[:, 0:TC],
                                       scalar1=ns_sb[:, ht:ht + 1], scalar2=None,
                                       op0=ALU.mult)
            pipeT.consume("vector", ev_t, deps=evdeps)
        XNFM = p.bar(pipeT.ev_done)
        prev["xnt_pe"] = (pipeT.pe_done, pipeT.uses)

        hide_dep = [prev["conv_v"]] if k else [HALO0]
        gate_dep = [prev["yg_v"]] if k else []
        for ct in range(24):
            def mk_mm(t, bank, ct=ct):
                last = None
                for ht in range(8):
                    last = t.matmul(bank[:, 0:TC],
                                    in_w_sb[:, ht, ct * 128:(ct + 1) * 128],
                                    xn_fm[:, ht, :], start=(ht == 0), stop=(ht == 7))
                return last
            pipeM.produce(mk_mm, deps=[XNFM, W_INIT])
            if ct < 16:
                def ev_h(a, bank, ct=ct):
                    return a.activation(out=hid[:, ct, 3:3 + TC], in_=bank[:, 0:TC],
                                        func=AF.Identity, bias=in_b_sb[:, ct:ct + 1])
                pipeM.consume("scalar", ev_h, deps=hide_dep + [W_INIT])
            else:
                def ev_g(a, bank, ct=ct):
                    return a.activation(out=gate_s[:, ct - 16, :], in_=bank[:, 0:TC],
                                        func=AF.Silu, bias=in_b_sb[:, ct:ct + 1])
                pipeM.consume("scalar", ev_g, deps=gate_dep + [W_INIT])
        HID = p.bar(pipeM.ev_done)
        prev["inproj_pe"] = (pipeM.pe_done, pipeM.uses)

        silu_guard = [prev["xproj_pe"]] if k else []
        dh_guard = [prev["yd_v"]] if k else []
        silu_ct = {}
        for ct in range(16):
            pr = ct % 2
            def v_conv(v, ct=ct, pr=pr):
                v.tensor_scalar(out=cacc[:, pr, :], in0=hid[:, ct, 0:TC],
                                scalar1=conv_w_sb[:, ct, 0:1], scalar2=None,
                                op0=ALU.mult)
                last = None
                for kk in range(1, KC):
                    last = v.scalar_tensor_tensor(
                        out=cacc[:, pr, :], in0=hid[:, ct, kk:kk + TC],
                        scalar=conv_w_sb[:, ct, kk:kk + 1],
                        in1=cacc[:, pr, :], op0=ALU.mult, op1=ALU.add)
                return last
            wdeps = [HID, W_INIT]
            if k:
                wdeps.append((s_do, 64 * k))
            if ct >= 2:
                wdeps.append(silu_ct[ct - 2])
            p.emit("vector", v_conv, deps=wdeps, inc=s_v)
            CONVD = p.bar(s_v)
            def a_silu(a, ct=ct, pr=pr):
                last = a.activation(out=conv_bf[:, ct, :], in_=cacc[:, pr, :],
                                    func=AF.Silu, bias=conv_b_sb[:, ct:ct + 1])
                if ct < 8:
                    last = a.activation(out=conv_s[:, ct, :], in_=cacc[:, pr, :],
                                        func=AF.Silu, bias=conv_b_sb[:, ct:ct + 1])
                return last
            p.emit("scalar", a_silu,
                   deps=[CONVD, W_INIT] + silu_guard + (dh_guard if ct < 8 else []),
                   inc=s_a)
            silu_ct[ct] = p.bar(s_a)
        CONV = p.bar(s_a)
        def v_halo(v):
            return v.tensor_copy(hid[:, :, 0:3], hid[:, :, TC:TC + 3])
        p.emit("vector", v_halo, deps=[CONV, p.bar(s_v)], inc=s_v)
        prev["conv_v"] = p.bar(s_v)

        tsev_guard = [prev["dt_pe"]] if k else []
        bcev_guard = [prev["brep_pe"], prev["crep_pe"]] if k else []
        def mk_ts(t, bank):
            last = None
            for ct in range(16):
                last = t.matmul(bank[0:64, 0:TC], xproj_w_sb[:, ct, 0:64],
                                conv_bf[:, ct, :], start=(ct == 0), stop=(ct == 15))
            return last
        pipeM.produce(mk_ts, deps=[CONV])
        def ev_ts(v, bank):
            return v.tensor_copy(ts_bf[:], bank[0:64, 0:TC])
        pipeM.consume("vector", ev_ts, deps=tsev_guard)
        def mk_bc(t, bank):
            last = None
            for ct in range(16):
                t.matmul(bank[0:16, 0:TC], xproj_w_sb[:, ct, 64:80],
                         conv_bf[:, ct, :], start=(ct == 0), stop=(ct == 15))
            for ct in range(16):
                last = t.matmul(bank[32:48, 0:TC], xproj_w_sb[:, ct, 80:96],
                                conv_bf[:, ct, :], start=(ct == 0), stop=(ct == 15))
            return last
        pipeM.produce(mk_bc, deps=[CONV])
        def ev_bc(v, bank):
            v.tensor_copy(B_sbuf[:], bank[0:16, 0:TC])
            return v.tensor_copy(C_sbuf[:], bank[32:48, 0:TC])
        pipeM.consume("vector", ev_bc, deps=bcev_guard, pad=True)
        TSBC = p.bar(pipeM.ev_done)
        prev["xproj_pe"] = (pipeM.pe_done, pipeM.uses)

        dA_guard = [prev["dh_v"]] if k else []
        for it in range(NIT):
            def mk_dt(t, bank, it=it):
                return t.matmul(bank[:, 0:TC], dt_w_sb[:, it * 128:(it + 1) * 128],
                                ts_bf[:], start=True, stop=True)
            pipeM.produce(mk_dt, deps=[TSBC])
            def ev_dt(a, bank, it=it):
                a.activation(out=delta[:, it, :], in_=bank[:, 0:TC], func=AF.Exp,
                             bias=dt_b_sb[:, it:it + 1])
                return a.activation(out=delta[:, it, :], in_=delta[:, it, :],
                                    func=AF.Ln, bias=1.0)
            pipeM.consume("scalar", ev_dt, deps=[W_INIT, G_INIT] + dA_guard)
        DELTA = p.bar(pipeM.ev_done)
        prev["dt_pe"] = (pipeM.pe_done, pipeM.uses)

        dh_war = [prev["gx_m2"][1]] if k else []
        for it in range(NIT):
            p.emit("vector", lambda v, it=it: v.tensor_tensor(
                out=dh[:, it, :], in0=delta[:, it, :], in1=conv_s[:, it, :],
                op=ALU.mult), deps=[DELTA, CONV] + dh_war, inc=s_v)
        DH = p.bar(s_v)
        prev["dh_v"] = DH

        scan_done = {}
        gx_cnt = {}
        gy_cnt = {}
        acc_cnt = {}
        cb_cnt = {}
        cc_cnt = {}
        for n in range(16):
            q = n % 2
            bdep = [TSBC, V_INIT]
            if n >= 2:
                bdep.append((s_a, cb_cnt[n - 2]))
            elif k:
                bdep.append(prev["cb_m2"][n])
            def mk_brep(t, bank, n=n):
                return t.matmul(bank[:, 0:TC], sel16[:, n * 128:(n + 1) * 128],
                                B_sbuf[:], start=True, stop=True)
            pipeB.produce(mk_brep, deps=bdep)
            bB = pipeB.banks[(pipeB.uses - 1) % 2]
            BRDY = (pipeB.pe_done, pipeB.uses)
            cdep = [TSBC, V_INIT]
            if n >= 2:
                cdep.append((s_a, cc_cnt[n - 2]))
            elif k:
                cdep.append(prev["cc_m2"][n])
            def mk_crep(t, bank, n=n):
                return t.matmul(bank[:, 0:TC], sel16[:, n * 128:(n + 1) * 128],
                                C_sbuf[:], start=True, stop=True)
            pipeC.produce(mk_crep, deps=cdep)
            bC = pipeC.banks[(pipeC.uses - 1) % 2]
            CRDY = (pipeC.pe_done, pipeC.uses)
            cbdep = [BRDY]
            if n >= 1:
                cbdep.append((s_gp, gx_cnt[n - 1]))
            elif k:
                cbdep.append(prev["gx_m2"][1])
            p.emit("scalar", lambda a, bB=bB: a.activation(
                out=Brep_sb[:], in_=bB[:, 0:TC], func=AF.Copy),
                deps=cbdep, inc=s_a)
            cb_cnt[n] = p.count(s_a)
            CB = (s_a, cb_cnt[n])
            dadeps = [DELTA, A_READY, G_INIT]
            if n >= 2:
                dadeps.append(scan_done[n - 2])
            elif k:
                dadeps.append(prev["scan_m2"][n])
            for it in range(NIT):
                p.emit("scalar", lambda a, it=it, n=n, q=q: a.activation(
                    out=dA_b[q][:, it, 1:TC + 1], in_=delta[:, it, :],
                    func=AF.Exp, scale=A_sb[:, it, n:n + 1]),
                    deps=dadeps, inc=s_a)
            DA = p.bar(s_a)
            gxdep = [CB, DH, G_INIT]
            if n >= 2:
                gxdep.append(scan_done[n - 2])
            elif k:
                gxdep.append(prev["scan_m2"][n])
            def g_X(g, q=q):
                last = None
                for sl in range(4):
                    last = g.tensor_tensor(
                        out=X_b[q][:, sl, 1:TC + 1], in0=dh[:, sl, :],
                        in1=Brep_sb[:], op=ALU.mult)
                return last
            p.emit("gpsimd", g_X, deps=gxdep, inc=s_gp)
            gx_cnt[n] = p.count(s_gp)
            GPX = (s_gp, gx_cnt[n])
            def v_x0(v, n=n, q=q):
                return v.tensor_copy(X_b[q][:, :, 0], state[:, :, n])
            p.emit("vector", v_x0, deps=[DH, G_INIT], inc=s_v)
            def v_X(v, bB=bB, q=q):
                return v.tensor_tensor(
                    out=X_b[q][:, 4:8, 1:TC + 1], in0=dh[:, 4:8, :],
                    in1=bB[:, 0:TC].rearrange("p (o t) -> p o t", o=1)
                    .broadcast_to([128, 4, TC]), op=ALU.mult)
            p.emit("vector", v_X, deps=[BRDY], inc=pipeB.ev_done)
            sodep = [DA, GPX]
            if n >= 1:
                sodep.append((s_gp, gy_cnt[n - 1]))
            elif k:
                sodep.append(prev["gy_m2"][1])
            def v_scan(v, q=q):
                return v.tensor_tensor_scan(
                    out=sout[:].rearrange("p a t -> p (a t)"),
                    data0=dA_b[q][:].rearrange("p a t -> p (a t)"),
                    data1=X_b[q][:].rearrange("p a t -> p (a t)"),
                    initial=0.0, op0=ALU.mult, op1=ALU.add)
            p.emit("vector", v_scan, deps=sodep, inc=s_v)
            scan_done[n] = p.bar(s_v)
            tgt = y_acc if n == 0 else tmpy
            ymdeps = [CRDY]
            if n >= 2:
                ymdeps.append((s_gp, acc_cnt[n - 1]))
            elif k and n == 1:
                ymdeps.append(prev["acc_m2"][1])
            def v_ym(v, bC=bC, tgt=tgt):
                return v.tensor_tensor(
                    out=tgt[:, 3:8, :], in0=sout[:, 3:8, 1:TC + 1],
                    in1=bC[:, 0:TC].rearrange("p (o t) -> p o t", o=1)
                    .broadcast_to([128, 5, TC]), op=ALU.mult)
            p.emit("vector", v_ym, deps=ymdeps, inc=pipeC.ev_done)
            YMV = (pipeC.ev_done, p.count(pipeC.ev_done))
            ccdep = [CRDY]
            if n >= 1:
                ccdep.append((s_gp, gy_cnt[n - 1]))
            elif k:
                ccdep.append(prev["gy_m2"][1])
            p.emit("scalar", lambda a, bC=bC: a.activation(
                out=Crep_sb[:], in_=bC[:, 0:TC], func=AF.Copy),
                deps=ccdep, inc=s_a)
            cc_cnt[n] = p.count(s_a)
            gydep = [(s_a, cc_cnt[n]), scan_done[n]] + ymdeps[1:]
            if n == 0 and k:
                gydep.append(prev["yg_v"])
            def g_ym(g, tgt=tgt):
                last = None
                for sl in range(3):
                    last = g.tensor_tensor(
                        out=tgt[:, sl, :], in0=sout[:, sl, 1:TC + 1],
                        in1=Crep_sb[:], op=ALU.mult)
                return last
            p.emit("gpsimd", g_ym, deps=gydep, inc=s_gp)
            gy_cnt[n] = p.count(s_gp)
            if n > 0:
                def g_ya(g):
                    return g.tensor_tensor(out=y_acc[:], in0=y_acc[:],
                                           in1=tmpy[:], op=ALU.add)
                p.emit("gpsimd", g_ya, deps=[YMV], inc=s_gp)
            acc_cnt[n] = p.count(s_gp)
            def v_st(v, n=n):
                return v.tensor_copy(state[:, :, n], sout[:, :, TC])
            p.emit("vector", v_st, inc=s_v)
        prev["scan_m2"] = [scan_done[14], scan_done[15]]
        prev["gx_m2"] = [(s_gp, gx_cnt[14]), (s_gp, gx_cnt[15])]
        prev["gy_m2"] = [(s_gp, gy_cnt[14]), (s_gp, gy_cnt[15])]
        prev["acc_m2"] = [(s_gp, acc_cnt[14]), (s_gp, acc_cnt[15])]
        prev["cb_m2"] = [(s_a, cb_cnt[14]), (s_a, cb_cnt[15])]
        prev["cc_m2"] = [(s_a, cc_cnt[14]), (s_a, cc_cnt[15])]
        prev["brep_pe"] = (pipeB.pe_done, pipeB.uses)
        prev["crep_pe"] = (pipeC.pe_done, pipeC.uses)
        YADD_G = (s_gp, acc_cnt[15])
        SCAN = p.bar(s_v)

        for it in range(NIT):
            def v_yd(v, it=it):
                return v.scalar_tensor_tensor(out=y_acc[:, it, :],
                                              in0=conv_s[:, it, :],
                                              scalar=D_sb[:, it:it + 1],
                                              in1=y_acc[:, it, :],
                                              op0=ALU.mult, op1=ALU.add)
            p.emit("vector", v_yd, deps=[SCAN, W_INIT, YADD_G], inc=s_v)
            ygdeps = [prev["op_pe"]] if k and it == 0 else []
            def v_yg(v, it=it):
                return v.tensor_tensor(out=ygate[:, it, :], in0=y_acc[:, it, :],
                                       in1=gate_s[:, it, :], op=ALU.mult)
            p.emit("vector", v_yg, deps=ygdeps, inc=s_v)
        YG = p.bar(s_v)
        prev["yd_v"] = YG
        prev["yg_v"] = YG

        for hg in range(4):
            oev_guard = [(s_do, 64 * k + 16 * hg)] if (k or hg) else []
            for ho in range(2 * hg, 2 * hg + 2):
                def mk_op(t, bank, ho=ho):
                    last = None
                    for it in range(NIT):
                        last = t.matmul(bank[:, 0:TC],
                                        out_w_sb[:, it, ho * 128:(ho + 1) * 128],
                                        ygate[:, it, :], start=(it == 0),
                                        stop=(it == 7))
                    return last
                pipeM.produce(mk_op, deps=[YG])
                def ev_op(a, bank, ho=ho):
                    return a.activation(out=o_sb[:, ho % 2, :], in_=bank[:, 0:TC],
                                        func=AF.Copy)
                pipeM.consume("scalar", ev_op, deps=oev_guard)
            OUTP = p.bar(pipeM.ev_done)
            def d_out(s_, half=half, tloc=tloc, hg=hg):
                return s_.dma_start(
                    out=cc_in.ap()[half * 1024 + hg * 256:
                                   half * 1024 + (hg + 1) * 256, tloc:tloc + TC]
                        .rearrange("(a p) t -> p a t", p=128),
                    in_=o_sb[:])
            p.emit("sync", d_out, deps=[OUTP], inc=s_do, inc_amt=16)
        prev["op_pe"] = (pipeM.pe_done, pipeM.uses)

        if DBG and k == 0:
            taps = [("stat", stat), ("xt", xt), ("xnt", xnt), ("xn_fm", xn_fm), ("hid", hid),
                    ("conv_bf", conv_bf), ("conv_s", conv_s), ("gate_s", gate_s),
                    ("ts_bf", ts_bf), ("B_sbuf", B_sbuf), ("C_sbuf", C_sbuf),
                    ("delta", delta), ("dh", dh), ("y_acc", y_acc),
                    ("ygate", ygate), ("o_sb", o_sb), ("state", state)]
            s_tap = p.sem("s_tap")
            alldeps = [p.bar(s_v), p.bar(s_a), (s_do, 64), p.bar(pipeM.ev_done),
                       p.bar(s_gp)]
            for nm, tile in taps:
                ap = tile[:]
                od = nc.dram_tensor("dbg_" + nm, list(ap.shape), ap.dtype,
                                    kind="ExternalOutput").ap()
                p.emit("sync", lambda s_, od=od, ap=ap: s_.dma_start(
                    out=od, in_=ap), deps=alldeps, inc="s_tap", inc_amt=16)
            TAPS = p.bar(s_tap)
            for eng in ("vector", "scalar", "gpsimd", "sync"):
                p.emit(eng, lambda en: None, deps=[TAPS])
            p.emit("tensor", lambda t: None, deps=[TAPS])

    ALLOUT = (s_do, 64 * NCH)

    if not dbg_ffn_input:
        def g_rs(g):
            return g.collective_compute(
                "ReduceScatter", ALU.add,
                replica_groups=[[0, 1], [2, 3], [4, 5], [6, 7]],
                ins=[cc_in.ap().opt()], outs=[rs_out.ap().opt()])
        p.emit("gpsimd", g_rs, deps=[ALLOUT], inc=s_cc)
    else:
        p.emit("gpsimd", lambda g: g.engine_nop(), deps=[ALLOUT], inc=s_cc)
    RS = p.bar(s_cc)

    PHB = [RS, ALLOUT, p.bar(s_v), p.bar(s_a), p.bar(pipeM.ev_done),
           p.bar(pipeT.ev_done), p.bar(pipeB.ev_done), p.bar(pipeC.ev_done)]
    es_attn.close()

    es_fp = contextlib.ExitStack()
    w_eT = p.sb(es_fp, "w_eT", [8, TOKF], F32)
    sel8 = p.sb(es_fp, "sel8", [8, 8 * 128], F32)
    b1_sb = p.sb(es_fp, "b1_sb", [128, 32, EE], F32)
    b2_sb = p.sb(es_fp, "b2_sb", [128, 8, EE], F32)

    es_pro = contextlib.ExitStack()
    rw_sb = p.sb(es_pro, "rw_sb", [128, 8, EE], BF16)
    outb_sb = p.sb(es_pro, "outb_sb", [128, 1024], F32)
    obrow = p.sb(es_pro, "obrow", [1, 1024], F32)
    ones1 = p.sb(es_pro, "ones1", [1, 128], F32)
    lgt = p.sb(es_pro, "lgt", [128, 4, EE], F32)
    r8 = p.sb(es_pro, "r8", [128, 12, EE], F32)
    wsc = p.sb(es_pro, "wsc", [128, 24], F32)
    io0 = p.sb(es_pro, "io0", [128, 4, EE], F32)
    io1k = p.sb(es_pro, "io1k", [128, 4, EE], F32)
    wdense = p.sb(es_pro, "wdense", [128, 4, EE], F32)
    xrt = p.sb(es_pro, "xrt", [128, 4, 1024], F32)
    h2t = p.sb(es_pro, "h2t", [128, 4, 1024], F32)
    h2nt = p.sb(es_pro, "h2nt", [128, 4, 1024], F32)
    at_fm = p.sb(es_pro, "at_fm", [128, 8, 512], F32)
    hn_fm = p.sb(es_pro, "hn_fm", [128, 8, 512], BF16)
    junkf = p.sb(es_pro, "junkf", [128, 1024], BF16)

    g_dma(rw_sb[:], router_wT.rearrange("(a p) e -> p a e", p=128), deps=PHB)
    s_dma(b1_sb[:], b1, deps=PHB)
    s_dma(b2_sb[:], b2, deps=PHB)
    s_dma(obrow[:], out_b, deps=PHB)
    FW0 = p.bar(s_dw)

    def g_initf(g):
        g.memset(ones1[:], 1.0)
        g.iota(io0[:], pattern=[[0, 4], [1, 8]], base=0,
               channel_multiplier=0, allow_small_or_imprecise_dtypes=True)
        g.iota(io1k[:], pattern=[[0, 4], [1, 8]], base=1000,
               channel_multiplier=0, allow_small_or_imprecise_dtypes=True)
        return g.iota(sel8[:], pattern=[[1, 8], [0, 128]], base=0,
                      channel_multiplier=0, allow_small_or_imprecise_dtypes=True)
    p.emit("gpsimd", g_initf, deps=PHB, inc=s_g)
    GF = p.bar(s_g)
    def v_self8(v):
        return v.tensor_scalar(out=sel8[:], in0=sel8[:], scalar1=iotc[0:8, :],
                               scalar2=None, op0=ALU.is_equal)
    p.emit("vector", v_self8, deps=[GF] + PHB, inc=s_v)
    SEL8 = p.bar(s_v)

    for hhalf in range(2):
        def mk_ob(t, bank, hh=hhalf):
            return t.matmul(bank[:, 0:512], ones1[:],
                            obrow[0:1, hh * 512:(hh + 1) * 512],
                            start=True, stop=True)
        pipeM.produce(mk_ob, deps=[FW0, GF] + PHB)
        def ev_ob(v, bank, hh=hhalf):
            return v.tensor_copy(outb_sb[:, hh * 512:(hh + 1) * 512],
                                 bank[:, 0:512])
        pipeM.consume("vector", ev_ob, deps=PHB)
    OBREP = p.bar(pipeM.ev_done)

    fprev = {}
    for sl in range(NSL):
        c0 = sl * 512
        xdep = [fprev["h2_v"]] if sl else PHB
        s_dma(xrt[:], xr[c0:c0 + 512, :].rearrange("(a p) h -> p a h", p=128),
              inc=s_dx, deps=xdep)
        adep = [fprev["at_pe"]] if sl else PHB
        s_dma(at_fm[:], rs_out.ap()[:, c0:c0 + 512]
              .rearrange("(a p) t -> p a t", p=128), inc=s_dx, deps=adep + [RS])
        XA = (s_dx, p.count(s_dx))

        for s in range(4):
            p.emit("vector", lambda v, s=s: v.tensor_tensor(
                out=xrt[:, s, :], in0=xrt[:, s, :], in1=outb_sb[:], op=ALU.add),
                deps=[XA, OBREP], inc=s_v)
        XB = p.bar(s_v)

        for tt in range(4):
            for hh in range(2):
                def mk_at(t, bank, tt=tt, hh=hh):
                    last = None
                    for ho in range(4 * hh, 4 * hh + 4):
                        last = t.transpose(
                            bank[:, (ho - 4 * hh) * 128:(ho - 4 * hh + 1) * 128],
                            at_fm[:, ho, tt * 128:(tt + 1) * 128], ident[:])
                    return last
                pipeT.produce(mk_at, deps=[XA, V_INIT] + (PHB if sl == 0 else []))
                def ev_at(v, bank, tt=tt, hh=hh):
                    return v.tensor_tensor(out=h2t[:, tt, hh * 512:(hh + 1) * 512],
                                           in0=bank[:, 0:512],
                                           in1=xrt[:, tt, hh * 512:(hh + 1) * 512],
                                           op=ALU.add)
                h2t_guard = []
                if "h2_dma" in fprev:
                    h2t_guard = [fprev["h2_dma"], fprev["sq2"]]
                pipeT.consume("vector", ev_at, deps=[XB] + h2t_guard)
        fprev["at_pe"] = (pipeT.pe_done, pipeT.uses)
        H2 = p.bar(pipeT.ev_done)

        s_dma(h2_dram.ap()[c0:c0 + 512, :].rearrange("(a p) h -> p a h", p=128),
              h2t[:], inc=s_dh2, deps=[H2])
        fprev["h2_dma"] = (s_dh2, p.count(s_dh2))

        if sl:
            p.emit("vector", lambda v: v.engine_nop(),
                   deps=[fprev["hnt_pe"]], inc=s_v)
        H2N = emit_ln(h2t, h2nt, 4, False, H2, junkf[:])
        fprev["h2_v"] = H2N
        fprev["sq2"] = p.bar(s_a)

        for ho in range(8):
            def mk_hn(t, bank, ho=ho):
                last = None
                for tt in range(4):
                    last = t.transpose(bank[:, tt * 128:(tt + 1) * 128],
                                       h2nt[:, tt, ho * 128:(ho + 1) * 128],
                                       ident[:])
                return last
            pipeT.produce(mk_hn, deps=[H2N])
            fprev["hnt_pe"] = (pipeT.pe_done, pipeT.uses)
            def ev_hn(v, bank, ho=ho):
                return v.tensor_copy(hn_fm[:, ho, :], bank[:, 0:512])
            hn_guard = []
            if "r_pe" in fprev:
                hn_guard = [fprev["r_pe"], fprev["hn_dma"]]
            pipeT.consume("vector", ev_hn, deps=hn_guard)
        HNF = p.bar(pipeT.ev_done)
        s_dma(h2n_dram.ap()[:, c0:c0 + 512].rearrange("(a p) t -> p a t", p=128),
              hn_fm[:], inc=s_dh2, deps=[HNF])
        fprev["hn_dma"] = (s_dh2, p.count(s_dh2))

        def mk_r(t, bank):
            last = None
            for tt in range(4):
                for ht in range(8):
                    last = t.matmul(bank[:, tt * 8:(tt + 1) * 8],
                                    hn_fm[:, ht, tt * 128:(tt + 1) * 128],
                                    rw_sb[:, ht, :], start=(ht == 0),
                                    stop=(ht == 7))
            return last
        pipeM.produce(mk_r, deps=[HNF, FW0])
        fprev["r_pe"] = (pipeM.pe_done, pipeM.uses)
        def ev_r(v, bank):
            return v.tensor_copy(lgt[:].rearrange("p a e -> p (a e)"),
                                 bank[:, 0:32])
        pipeM.consume("vector", ev_r,
                      deps=[fprev["wd_pe"]] if "wd_pe" in fprev else [],
                      pad=True)
        LG = p.bar(pipeM.ev_done)

        def bc(ap):
            return ap.broadcast_to([128, 4, EE])
        lg3 = lgt[:]
        m1 = wsc[:, 0:4].rearrange("p (a o) -> p a o", o=1)
        ix1 = wsc[:, 4:8].rearrange("p (a o) -> p a o", o=1)
        m2 = wsc[:, 8:12].rearrange("p (a o) -> p a o", o=1)
        ix2 = wsc[:, 12:16].rearrange("p (a o) -> p a o", o=1)
        w1c = wsc[:, 16:20].rearrange("p (a o) -> p a o", o=1)
        w2c = wsc[:, 20:24].rearrange("p (a o) -> p a o", o=1)
        r8a = r8[:, 0:4, :]
        r8b = r8[:, 4:8, :]
        r8c = r8[:, 8:12, :]
        RV = [LG, GF]
        def step(fn, eng="vector", deps=()):
            p.emit(eng, fn, deps=list(deps) + RV, inc=s_v if eng == "vector"
                   else s_a, pad=True)
        step(lambda v: v.tensor_reduce(m1, lg3, axis=AX.X, op=ALU.max))
        step(lambda v: v.tensor_tensor(out=r8a, in0=lg3, in1=bc(m1),
                                       op=ALU.is_equal))
        step(lambda v: v.scalar_tensor_tensor(out=r8b, in0=r8a, scalar=-1000.0,
                                              in1=io1k[:], op0=ALU.mult,
                                              op1=ALU.add))
        step(lambda v: v.tensor_reduce(ix1, r8b, axis=AX.X, op=ALU.min))
        step(lambda v: v.tensor_tensor(out=r8a, in0=io0[:], in1=bc(ix1),
                                       op=ALU.is_equal))
        step(lambda v: v.scalar_tensor_tensor(out=r8b, in0=r8a, scalar=-1e30,
                                              in1=lg3, op0=ALU.mult,
                                              op1=ALU.add))
        step(lambda v: v.tensor_reduce(m2, r8b, axis=AX.X, op=ALU.max))
        step(lambda v: v.tensor_tensor(out=r8c, in0=r8b, in1=bc(m2),
                                       op=ALU.is_equal))
        step(lambda v: v.scalar_tensor_tensor(out=r8b, in0=r8c, scalar=-1000.0,
                                              in1=io1k[:], op0=ALU.mult,
                                              op1=ALU.add))
        step(lambda v: v.tensor_reduce(ix2, r8b, axis=AX.X, op=ALU.min))
        step(lambda v: v.tensor_tensor(out=r8c, in0=io0[:], in1=bc(ix2),
                                       op=ALU.is_equal))
        step(lambda v: v.tensor_tensor(out=w1c, in0=m1, in1=m2,
                                       op=ALU.subtract))
        W1D = p.bar(s_v)
        def a_w1(a):
            a.activation(out=w1c, in_=w1c, func=AF.Tanh, scale=0.5)
            return a.activation(out=w1c, in_=w1c, func=AF.Identity, scale=0.5,
                                bias=0.5)
        step(a_w1, eng="scalar", deps=[W1D, G_INIT])
        W1S = p.bar(s_a)
        step(lambda v: v.tensor_scalar(out=w2c, in0=w1c, scalar1=-1.0,
                                       scalar2=1.0, op0=ALU.mult, op1=ALU.add),
             deps=[W1S])
        step(lambda v: v.tensor_tensor(out=r8a, in0=r8a, in1=bc(w1c),
                                       op=ALU.mult))
        step(lambda v: v.tensor_tensor(out=r8c, in0=r8c, in1=bc(w2c),
                                       op=ALU.mult))
        step(lambda v: v.tensor_tensor(out=wdense[:], in0=r8a, in1=r8c,
                                       op=ALU.add))
        WD = p.bar(s_v)

        def mk_wt(t, bank):
            last = None
            for tt in range(4):
                last = t.transpose(bank[0:8, tt * 128:(tt + 1) * 128],
                                   wdense[:, tt, :], ident[:])
            return last
        pipeM.produce(mk_wt, deps=[WD])
        def ev_wt(v, bank, sl=sl):
            return v.tensor_copy(w_eT[:, sl * 512:(sl + 1) * 512], bank[0:8, 0:512])
        pipeM.consume("vector", ev_wt)
        fprev["wd_pe"] = (pipeM.pe_done, pipeM.uses)
    fprev["hn_pe"] = (pipeT.pe_done, pipeT.uses)
    PRO = [p.bar(pipeM.ev_done), p.bar(s_dh2), p.bar(s_v), p.bar(s_a),
           (pipeM.pe_done, pipeM.uses), (pipeT.pe_done, pipeT.uses),
           p.bar(pipeT.ev_done), p.bar(s_dx), p.bar(s_g)]
    WET = p.bar(pipeM.ev_done)
    es_pro.close()
    es_exp = contextlib.ExitStack()
    s_ac2 = p.sem("s_ac2")
    w1_sb = [p.sb(es_exp, f"w1_sb{i}", [128, 8, FFD // 2], BF16)
             for i in range(2)]
    w2_sb = [p.sb(es_exp, f"w2_sb{i}", [128, 16, 1024], BF16)
             for i in range(2)]
    h1_sb = p.sb(es_exp, "h1_sb", [128, 16, 512], BF16)
    h2n_sb = [p.sb(es_exp, f"h2n_sb{i}", [128, 8, 512], BF16)
              for i in range(2)]
    tmp_sb = p.sb(es_exp, "tmp_sb", [128, 8, 512], F32)
    wrep_sb = [p.sb(es_exp, f"wrep_sb{i}", [128, 512], F32) for i in range(2)]

    NU = 2 * EE

    def emit_wload(u, deps):
        e, fh = u // 2, u % 2
        pu = u % 2
        g_dma(w1_sb[pu][:],
              w1T[e][:, fh * (FFD // 2):(fh + 1) * (FFD // 2)]
              .rearrange("(a p) f -> p a f", p=128), inc="s_dwf", deps=deps)
        g_dma(w2_sb[pu][:],
              w2T[e][fh * (FFD // 2):(fh + 1) * (FFD // 2), :]
              .rearrange("(a p) h -> p a h", p=128), inc="s_dwf", deps=deps)

    def emit_hnload(u, tc, deps):
        s_dma(h2n_sb[tc % 2][:],
              h2n_dram.ap()[:, tc * 512:(tc + 1) * 512]
              .rearrange("(a p) t -> p a t", p=128),
              inc="s_dhn", deps=deps)

    emit_wload(0, PRO)
    emit_wload(1, PRO)
    emit_hnload(0, 0, PRO + [p.bar(s_dh2)])
    nacc = [0, 0]
    wlast = {}
    for u in range(NU):
        e, fh, pu = u // 2, u % 2, u % 2
        WF = (s_dwf, 32 * (u + 1))
        for tc in range(NSL):
            if not (u == NU - 1 and tc == NSL - 1):
                ntc = (tc + 1) % NSL
                hdep = [fprev["h1_pe"]] if "h1_pe" in fprev else PRO
                emit_hnload(u, ntc, hdep)
            def mk_wr(t, bank, e=e, tc=tc):
                return t.matmul(bank[:, 0:512], sel8[:, e * 128:(e + 1) * 128],
                                w_eT[:, tc * 512:(tc + 1) * 512],
                                start=True, stop=True)
            pipeC.produce(mk_wr, deps=[WET, SEL8])
            def ev_wr(v, bank, tc=tc):
                return v.tensor_copy(wrep_sb[tc % 2][:], bank[:, 0:512])
            pipeC.consume("vector", ev_wr,
                          deps=[fprev["wr_used"]] if "wr_used" in fprev else [])
            WR = p.bar(pipeC.ev_done)
            HN = (s_dhn, 16 * (u * NSL + tc + 1))

            for ft in range(16):
                def mk_h1(t, bank, ft=ft, pu=pu, tc=tc):
                    last = None
                    for ht in range(8):
                        last = t.matmul(bank[:, 0:512],
                                        w1_sb[pu][:, ht, ft * 128:(ft + 1) * 128],
                                        h2n_sb[tc % 2][:, ht, :],
                                        start=(ht == 0), stop=(ht == 7))
                    return last
                pipeM.produce(mk_h1, deps=[WF, HN])
                def ev_h1(a, bank, ft=ft, e=e, fh=fh):
                    return a.activation(out=h1_sb[:, ft, :], in_=bank[:, 0:512],
                                        func=AF.Gelu,
                                        bias=b1_sb[:, fh * 16 + ft, e:e + 1])
                pipeM.consume("scalar", ev_h1,
                              deps=([fprev["o2_pe"]] if "o2_pe" in fprev else [])
                              + [FW0])
            H1 = p.bar(pipeM.ev_done)
            fprev["h1_pe"] = (pipeM.pe_done, pipeM.uses)

            op = None if u == 0 else ALU.add
            for hg in range(2):
                sacc = s_acc if hg == 0 else s_ac2
                for ho in range(hg * 4, hg * 4 + 4):
                    def mk_o2(t, bank, ho=ho, pu=pu):
                        last = None
                        for ft in range(16):
                            last = t.matmul(bank[:, 0:512],
                                            w2_sb[pu][:, ft, ho * 128:(ho + 1) * 128],
                                            h1_sb[:, ft, :], start=(ft == 0),
                                            stop=(ft == 15))
                        return last
                    pipeB.produce(mk_o2, deps=[H1, WF])
                    tmpdep = [(sacc, nacc[hg] * 16)]
                    if fh == 0:
                        def ev_o2(v, bank, ho=ho, tc=tc):
                            return v.tensor_tensor(
                                out=tmp_sb[:, ho, :], in0=bank[:, 0:512],
                                in1=wrep_sb[tc % 2][:], op=ALU.mult)
                    else:
                        def ev_o2(v, bank, ho=ho, e=e, tc=tc):
                            return v.scalar_tensor_tensor(
                                out=tmp_sb[:, ho, :], in0=bank[:, 0:512],
                                scalar=b2_sb[:, ho, e:e + 1],
                                in1=wrep_sb[tc % 2][:],
                                op0=ALU.add, op1=ALU.mult)
                    pipeB.consume("vector", ev_o2, deps=tmpdep + [WR, FW0])
                O2 = p.bar(pipeB.ev_done)
                def d_acc(g, tc=tc, op=op, hg=hg):
                    kw = {} if op is None else dict(accum_op=op)
                    return g.dma_start(
                        out=acc_fm.ap()[hg * 512:(hg + 1) * 512,
                                        tc * 512:(tc + 1) * 512]
                        .rearrange("(a p) t -> p a t", p=128),
                        in_=tmp_sb[:, hg * 4:hg * 4 + 4, :], **kw)
                p.emit("gpsimd", d_acc, deps=[O2, (sacc, nacc[hg] * 16)],
                       inc="s_acc" if hg == 0 else "s_ac2", inc_amt=16)
                nacc[hg] += 1
            fprev["o2_pe"] = (pipeB.pe_done, pipeB.uses)
            fprev["wr_used"] = p.bar(pipeB.ev_done)
        wlast[pu] = [(pipeB.pe_done, pipeB.uses), (pipeM.pe_done, pipeM.uses)]
        if u + 2 < NU:
            emit_wload(u + 2, wlast[pu])

    ACC = [(s_acc, nacc[0] * 16), (s_ac2, nacc[1] * 16)]
    EXP = ACC + [p.bar(pipeB.ev_done), p.bar(pipeC.ev_done), p.bar(s_v),
                 p.bar(s_a), (pipeB.pe_done, pipeB.uses),
                 (pipeM.pe_done, pipeM.uses), p.bar(s_dwf), p.bar(s_dhn)]
    es_exp.close()
    es_epi = contextlib.ExitStack()
    at2 = p.sb(es_epi, "at2", [128, 8, 512], F32)
    ht2 = p.sb(es_epi, "ht2", [128, 4, 1024], F32)
    ot_sb = p.sb(es_epi, "ot_sb", [128, 4, 1024], F32)

    for sl in range(NSL):
        c0 = sl * 512
        adep = [fprev["ep_pe"]] if "ep_pe" in fprev else EXP
        s_dma(at2[:], acc_fm.ap()[:, c0:c0 + 512]
              .rearrange("(a p) t -> p a t", p=128), inc=s_dx,
              deps=adep + ACC)
        hdep = [fprev["ep_v"]] if "ep_v" in fprev else EXP
        s_dma(ht2[:], h2_dram.ap()[c0:c0 + 512, :]
              .rearrange("(a p) h -> p a h", p=128), inc=s_dx,
              deps=hdep + [p.bar(s_dh2)])
        EXA = (s_dx, p.count(s_dx))
        for tt in range(4):
            for hh in range(2):
                def mk_ep(t, bank, tt=tt, hh=hh):
                    last = None
                    for ho in range(4 * hh, 4 * hh + 4):
                        last = t.transpose(
                            bank[:, (ho - 4 * hh) * 128:(ho - 4 * hh + 1) * 128],
                            at2[:, ho, tt * 128:(tt + 1) * 128], ident[:])
                    return last
                pipeT.produce(mk_ep, deps=[EXA] + (EXP if sl == 0 else []))
                def ev_ep(v, bank, tt=tt, hh=hh):
                    return v.tensor_tensor(out=ot_sb[:, tt, hh * 512:(hh + 1) * 512],
                                           in0=bank[:, 0:512],
                                           in1=ht2[:, tt, hh * 512:(hh + 1) * 512],
                                           op=ALU.add)
                pipeT.consume("vector", ev_ep,
                              deps=[EXA, (s_out, 16 * sl)] if sl else [EXA])
        fprev["ep_pe"] = (pipeT.pe_done, pipeT.uses)
        fprev["ep_v"] = p.bar(pipeT.ev_done)
        EOUT = p.bar(pipeT.ev_done)
        s_dma(out_hbm[c0:c0 + 512, :].rearrange("(a p) h -> p a h", p=128),
              ot_sb[:], inc="s_out", deps=[EOUT])

    if DBG:
        for nm, dr in [("cc_in", cc_in), ("rs_out", rs_out), ("h2", h2_dram),
                       ("h2n", h2n_dram), ("acc", acc_fm)]:
            if dbg_ffn_input and nm == "rs_out":
                continue
            ap = dr.ap()
            od = nc.dram_tensor("dbgd_" + nm, list(ap.shape), ap.dtype,
                                kind="ExternalOutput").ap()
            p.emit("sync", lambda s_, od=od, ap=ap: s_.dma_start(out=od, in_=ap),
                   deps=[(s_out, 16 * NSL)], inc="s_out", inc_amt=16)
        p.emit("sync", lambda s_: s_.nop(),
               deps=[("s_out", p.count("s_out"))])

    p.emit("sync", lambda s_: s_.nop(), deps=[(s_out, 16 * NSL)])
    p.emit("gpsimd", lambda g: g.engine_nop(), deps=[(s_out, 16 * NSL)])

    es_epi.close()
    es_fp.close()
    return p


def core_inputs(inputs, c):
    b, half = c // 2, c % 2
    f = np.ascontiguousarray
    own = slice(half * 1024, (half + 1) * 1024)
    oth = slice((1 - half) * 1024, (2 - half) * 1024)
    perm = np.r_[np.arange(own.start, own.stop), np.arange(oth.start, oth.stop)]
    in_w = inputs["L0_attn_in_w"]
    d = {
        "x": f(inputs["L0_attn_x"][b]),
        "xr": f(inputs["L0_attn_x"][b][half * 4096:(half + 1) * 4096]),
        "in_w": f(np.concatenate([in_w[:, perm], in_w[:, 2048 + own.start:2048 + own.stop]], 1)),
        "in_b": col128(np.concatenate([inputs["L0_attn_in_b"][perm],
                                       inputs["L0_attn_in_b"][2048:][own]])),
        "conv_w": f(inputs["L0_attn_conv_w"][:, perm].T.reshape(16, 128, 4)
                    .transpose(1, 0, 2)),
        "conv_b": col128(inputs["L0_attn_conv_b"][perm]),
        "xproj_w": f(inputs["L0_attn_xproj_w"][perm]),
        "dt_w": f(inputs["L0_attn_dt_w"][:, own]),
        "dt_b": col128(inputs["L0_attn_dt_b"][own]),
        "A_log": f(inputs["L0_attn_A_log"][own]),
        "D": col128(inputs["L0_attn_D"][own]),
        "out_w": f(inputs["L0_attn_out_w"][own]),
        "out_b": f(inputs["L0_attn_out_b"][None]),
        "norm_scale": col128(inputs["L0_attn_norm_scale"]),
        "router_wT": f(inputs["L0_ffn_router_w"].T),
        "w1T": f(inputs["L0_ffn_w1"].transpose(0, 2, 1)),
        "b1": f(inputs["L0_ffn_b1"].T.reshape(32, 128, 8).transpose(1, 0, 2)),
        "w2T": f(inputs["L0_ffn_w2"].transpose(0, 2, 1)),
        "b2": f(inputs["L0_ffn_b2"].T.reshape(8, 128, 8).transpose(1, 0, 2)),
    }
    return d


def col128(v):
    v = np.asarray(v)
    return np.ascontiguousarray(v.reshape(-1, 128).T)


_CACHE = {}


def kernel(**inputs):
    from concourse.bass_utils import run_bass_kernel_spmd
    if "nc" not in _CACHE:
        p = build_program(T=8192)
        _CACHE["nc"] = p.build()
    nc = _CACHE["nc"]
    in_maps = [core_inputs(inputs, c) for c in range(8)]
    res = run_bass_kernel_spmd(nc, in_maps, core_ids=list(range(8)))
    out = np.zeros((4, 8192, 1024), np.float32)
    for c in range(8):
        b, half = c // 2, c % 2
        out[b, half * 4096:(half + 1) * 4096] = res.results[c]["out"]
    return out


# version 12 (final)
# speedup vs baseline: 1.1800x; 1.1800x over previous
import contextlib
import numpy as np
import concourse.bass as bass
import concourse.mybir as mybir

F32 = mybir.dt.float32
BF16 = mybir.dt.bfloat16
AF = mybir.ActivationFunctionType
ALU = mybir.AluOpType
AX = mybir.AxisListType

ENGINES = ("sync", "scalar", "vector", "gpsimd", "tensor")


class Prog:
    def __init__(self):
        self.nc = bass.Bass("TRN2", target_bir_lowering=False, debug=False)
        self.es = contextlib.ExitStack()
        self.sched = {e: [] for e in ENGINES}
        self.waited = {e: {} for e in ENGINES}
        self.sems = {}
        self.boot = self.es.enter_context(self.nc.semaphore("boot"))
        self.jka = self.es.enter_context(self.nc.sbuf_tensor("jka", [128, 64], mybir.dt.float32))
        self.jkb = self.es.enter_context(self.nc.sbuf_tensor("jkb", [128, 64], mybir.dt.float32))

    def sem(self, name):
        s = self.es.enter_context(self.nc.semaphore(name))
        self.sems[name] = [s, 0]
        return name

    def sb(self, stack, name, shape, dtype):
        return stack.enter_context(self.nc.sbuf_tensor(name, list(shape), dtype))

    def ps(self, name, shape, dtype=F32):
        return self.es.enter_context(self.nc.psum_tensor(name, list(shape), dtype))

    def junk(self, eng, engine):
        if eng == "vector":
            return engine.tensor_copy(self.jkb[:], self.jka[:])
        if eng == "scalar":
            return engine.activation(out=self.jkb[:], in_=self.jka[:],
                                     func=AF.Identity)
        if eng == "gpsimd":
            return engine.tensor_copy(self.jkb[:], self.jka[:])
        raise ValueError(eng)

    def emit(self, eng, fn, deps=(), inc=None, inc_amt=1, pad=False):
        deps = [(s, c) for (s, c) in deps if c > self.waited[eng].get(s, 0)]
        for s, c in deps:
            self.waited[eng][s] = c
        if inc is not None:
            self.sems[inc][1] += inc_amt
        def run(engine, deps=deps, fn=fn, inc=inc, inc_amt=inc_amt, pad=pad):
            for s, c in deps:
                engine.wait_ge(self.sems[s][0], c)
            ret = fn(engine)
            if pad:
                ret = self.junk(eng, engine)
            if inc is not None:
                assert ret is not None
                ret.then_inc(self.sems[inc][0], inc_amt)
        self.sched[eng].append(run)

    def count(self, sem):
        return self.sems[sem][1]

    def bar(self, sem):
        return (sem, self.sems[sem][1])

    def build(self):
        nums = sorted([h.num for h, _ in self.sems.values()] + [self.boot.num])
        lo, hi = min(nums), max(nums) + 1
        rng = range(lo, hi)
        finals = [(h, c) for (h, c) in self.sems.values()]
        block = self.es.enter_context(self.nc.Block())
        for eng in ENGINES:
            def body(engine, eng=eng, rng=rng, finals=finals):
                if eng == "gpsimd":
                    engine.dma_reset(rng)
                    engine.sem_clear(rng)
                    engine.engine_nop().then_inc(self.boot, 1)
                else:
                    engine.wait_ge(self.boot, 1)
                for fn in self.sched[eng]:
                    fn(engine)
                if eng == "gpsimd":
                    engine.wait_ge(self.boot, 5)
                    for h, c in finals:
                        if c > 0:
                            engine.wait_ge(h, c)
                    engine.dma_reset(rng)
                    engine.sem_clear(rng)
                else:
                    engine.nop().then_inc(self.boot, 1)
            getattr(block, eng)(body)
        return self.nc


class BankPipe:

    def __init__(self, p, name, shape=(128, 512), n=2, dtype=F32):
        self.p = p
        self.banks = [p.ps(f"{name}{i}", shape, dtype) for i in range(n)]
        self.pe_done = p.sem(f"{name}_pd")
        self.ev_done = p.sem(f"{name}_ed")
        self.n = n
        self.uses = 0

    def produce(self, fn, deps=()):
        i = self.uses % self.n
        bank = self.banks[i]
        d = list(deps)
        if self.uses >= self.n:
            d.append((self.ev_done, self.uses - self.n + 1))
        self.uses += 1
        self.p.emit("tensor", lambda t, b=bank, fn=fn: fn(t, b), deps=d,
                    inc=self.pe_done)
        return bank

    def consume(self, eng, fn, deps=(), pad=False):
        i = (self.uses - 1) % self.n
        bank = self.banks[i]
        d = list(deps) + [(self.pe_done, self.uses)]
        self.p.emit(eng, lambda e, b=bank, fn=fn: fn(e, b), deps=d,
                    inc=self.ev_done, pad=pad)


def build_program(dbg_ffn_input=False, dbg_taps=False, T=8192):
    DBG = dbg_taps
    """Returns (nc, names). If dbg_ffn_input, rs_out is an ExternalInput."""
    p = Prog()
    nc = p.nc

    TC = 256
    NCH = T // TC
    NSUB = TC // 128
    IC, IF, NST, KC = 1024, 2048, 16, 4
    NIT = IC // 128
    NCT = IF // 128
    TOKF = T // 2
    NSL = TOKF // 512
    EPS_LN, EPS_RMS = 1e-5, 1e-6
    EE, FFD = 8, 4096

    def din(name, shape, dtype=F32):
        return nc.dram_tensor(name, list(shape), dtype, kind="ExternalInput").ap()

    x = din("x", [T, 1024])
    xr = din("xr", [TOKF, 1024])
    in_w = din("in_w", [1024, IF + IC])
    in_b = din("in_b", [128, 24])
    conv_w = din("conv_w", [128, 16, KC])
    conv_b = din("conv_b", [128, 16])
    xproj_w = din("xproj_w", [IF, 96])
    dt_w = din("dt_w", [64, IC])
    dt_b = din("dt_b", [128, 8])
    A_log = din("A_log", [IC, NST])
    Dvec = din("D", [128, 8])
    out_w = din("out_w", [IC, 1024])
    out_b = din("out_b", [1, 1024])
    norm_scale = din("norm_scale", [128, 8])
    router_wT = din("router_wT", [1024, EE])
    w1T = din("w1T", [EE, 1024, FFD])
    b1 = din("b1", [128, 32, EE])
    w2T = din("w2T", [EE, FFD, 1024])
    b2 = din("b2", [128, 8, EE])

    out_hbm = nc.dram_tensor("out", [TOKF, 1024], F32, kind="ExternalOutput").ap()

    cc_in = nc.dram_tensor("cc_in", [2048, TOKF], F32)
    if dbg_ffn_input:
        rs_out = nc.dram_tensor("rs_out", [1024, TOKF], F32, kind="ExternalInput")
    else:
        rs_out = nc.dram_tensor("rs_out", [1024, TOKF], F32)
    h2n_dram = nc.dram_tensor("h2n_dram", [1024, TOKF], BF16)
    h2_dram = nc.dram_tensor("h2_dram", [TOKF, 1024], F32)
    acc_fm = nc.dram_tensor("acc_fm", [1024, TOKF], F32)

    s_dx = p.sem("s_dx")
    s_dw = p.sem("s_dw")
    s_v = p.sem("s_v")
    s_a = p.sem("s_a")
    s_g = p.sem("s_g")
    s_do = p.sem("s_do")
    s_cc = p.sem("s_cc")
    s_dh2 = p.sem("s_dh2")
    s_dhn = p.sem("s_dhn")
    s_acc = p.sem("s_acc")
    s_dwf = p.sem("s_dwf")
    s_gp = p.sem("s_gp")
    s_out = p.sem("s_out")

    pipeT = BankPipe(p, "pT", n=2)
    pipeM = BankPipe(p, "pM", n=2)
    pipeB = BankPipe(p, "pB", n=2)
    pipeC = BankPipe(p, "pC", n=2)

    ident = p.sb(p.es, "ident", [128, 128], F32)
    iotf = p.sb(p.es, "iotf", [128, 128], F32)
    iotc = p.sb(p.es, "iotc", [128, 1], F32)
    stat = p.sb(p.es, "stat", [128, 4, 8], F32)
    CVALS = [EPS_LN, EPS_RMS, 1.0, -0.5, 0.5, 1.0 / 1024,
             (1.0 + EPS_RMS) / 1024, 1e-11]
    cst = p.sb(p.es, "cst", [128, len(CVALS)], F32)
    for j, v in enumerate(CVALS):
        nc.const_aps.register(F32, v, cst[:, j:j + 1])

    def g_dma(dst, src, inc="s_dw", deps=()):
        p.emit("gpsimd", lambda g, d=dst, s=src: g.dma_start(out=d, in_=s),
               deps=deps, inc=inc, inc_amt=16)

    def s_dma(dst, src, inc="s_dw", deps=()):
        p.emit("sync", lambda s_, d=dst, s=src: s_.dma_start(out=d, in_=s),
               deps=deps, inc=inc, inc_amt=16)

    def emit_ln(xin, xout, ns, rms, xdep, junk, jdeps=()):
        st_ = lambda c: stat[:, 0:ns, c]
        for s in range(ns):
            p.emit("vector", lambda v, s=s: v.reduce_sum(
                stat[:, s, 0:1], xin[:, s, :], axis=AX.X),
                deps=[xdep, G_INIT], inc=s_v, pad=(s == ns - 1))
        SM = p.bar(s_v)
        for s in range(ns):
            p.emit("scalar", lambda a, s=s, junk=junk: a.activation(
                out=junk, in_=xin[:, s, :], func=AF.Square,
                accum_out=stat[:, s, 1:2]), deps=[xdep] + list(jdeps),
                inc=s_a, pad=(s == ns - 1))
        SQ = p.bar(s_a)
        def v_u(v):
            v.tensor_tensor(out=st_(3), in0=st_(0), in1=st_(0), op=ALU.mult)
            v.tensor_scalar(out=st_(2), in0=st_(0), scalar1=1.0 / 1024,
                            scalar2=None, op0=ALU.mult)
            return None
        p.emit("vector", v_u, deps=[SM])
        p.emit("vector", lambda v: p.junk("vector", v))
        def v_w(v):
            return v.scalar_tensor_tensor(out=st_(4), in0=st_(3),
                                          scalar=-1.0 / 1024, in1=st_(1),
                                          op0=ALU.mult, op1=ALU.add)
        p.emit("vector", v_w, deps=[SQ], inc=s_v, pad=True)
        VW = p.bar(s_v)
        if rms:
            sc, bi = (1.0 + EPS_RMS) / 1024, 1e-11
        else:
            sc, bi = 1.0 / 1024, EPS_LN
        def a_sc(a, sc=sc, bi=bi):
            a.activation(out=st_(5), in_=st_(4), func=AF.Ln, scale=sc, bias=bi)
            return a.activation(out=st_(5), in_=st_(5), func=AF.Exp, scale=-0.5)
        p.emit("scalar", a_sc, deps=[VW, G_INIT], inc=s_a, pad=True)
        SCL = p.bar(s_a)
        for s in range(ns):
            p.emit("vector", lambda v, s=s: v.tensor_scalar(
                out=xout[:, s, :], in0=xin[:, s, :], scalar1=stat[:, s, 2:3],
                scalar2=stat[:, s, 5:6], op0=ALU.subtract, op1=ALU.mult),
                deps=[SCL], inc=s_v)
        return p.bar(s_v)

    es_attn = contextlib.ExitStack()
    sel16 = p.sb(es_attn, "sel16", [16, 16 * 128], F32)
    state = p.sb(es_attn, "state", [128, NIT, NST], F32)
    in_w_sb = p.sb(es_attn, "in_w_sb", [128, 8, IF + IC], BF16)
    xproj_w_sb = p.sb(es_attn, "xproj_w_sb", [128, 16, 96], BF16)
    dt_w_sb = p.sb(es_attn, "dt_w_sb", [64, IC], BF16)
    out_w_sb = p.sb(es_attn, "out_w_sb", [128, 8, 1024], BF16)
    in_b_sb = p.sb(es_attn, "in_b_sb", [128, 24], F32)
    conv_b_sb = p.sb(es_attn, "conv_b_sb", [128, 16], F32)
    conv_w_sb = p.sb(es_attn, "conv_w_sb", [128, 16, KC], F32)
    dt_b_sb = p.sb(es_attn, "dt_b_sb", [128, 8], F32)
    A_sb = p.sb(es_attn, "A_sb", [128, 8, NST], F32)
    D_sb = p.sb(es_attn, "D_sb", [128, 8], F32)
    ns_sb = p.sb(es_attn, "ns_sb", [128, 8], F32)
    xt = p.sb(es_attn, "xt", [128, NSUB, 1024], F32)
    xnt = xt
    xn_fm = p.sb(es_attn, "xn_fm", [128, 8, TC], BF16)
    hid = p.sb(es_attn, "hid", [128, NCT, TC + 3], BF16)
    cacc = p.sb(es_attn, "cacc", [128, 2, TC], F32)
    conv_s = p.sb(es_attn, "conv_s", [128, NIT, TC], F32)
    conv_bf = p.sb(es_attn, "conv_bf", [128, NCT, TC], BF16)
    gate_s = p.sb(es_attn, "gate_s", [128, NIT, TC], F32)
    ts_bf = p.sb(es_attn, "ts_bf", [64, TC], BF16)
    B_sbuf = p.sb(es_attn, "B_sbuf", [16, TC], F32)
    C_sbuf = p.sb(es_attn, "C_sbuf", [16, TC], F32)
    delta = p.sb(es_attn, "delta", [128, NIT, TC], F32)
    dh = p.sb(es_attn, "dh", [128, NIT, TC], F32)
    dA_b = [p.sb(es_attn, f"dA_b{i}", [128, NIT, TC + 1], F32)
            for i in range(2)]
    X_b = [p.sb(es_attn, f"X_b{i}", [128, NIT, TC + 1], F32)
           for i in range(2)]
    sout = p.sb(es_attn, "sout", [128, NIT, TC + 1], F32)
    tmpy = p.sb(es_attn, "tmpy", [128, NIT, TC], F32)
    y_acc = p.sb(es_attn, "y_acc", [128, NIT, TC], F32)
    ygate = p.sb(es_attn, "ygate", [128, NIT, TC], BF16)
    o_sb = cacc
    junk = conv_bf[:, 0:4, :].rearrange("p a t -> p (a t)")

    def g_init(g):
        g.iota(iotf[:], pattern=[[1, 128]], base=0, channel_multiplier=0,
               allow_small_or_imprecise_dtypes=True)
        g.iota(iotc[:], pattern=[[0, 1]], base=0, channel_multiplier=1,
               allow_small_or_imprecise_dtypes=True)
        g.iota(sel16[:], pattern=[[1, 16], [0, 128]], base=0, channel_multiplier=0,
               allow_small_or_imprecise_dtypes=True)
        g.memset(state[:], 0.0)
        for j, v in enumerate(CVALS):
            g.memset(cst[:, j:j + 1], v)
        return g.memset(stat[:], 0.0)
    p.emit("gpsimd", g_init, inc=s_g)
    G_INIT = p.bar(s_g)

    def v_init(v):
        v.tensor_scalar(out=ident[:], in0=iotf[:], scalar1=iotc[:], scalar2=None,
                        op0=ALU.is_equal)
        return v.tensor_scalar(out=sel16[:], in0=sel16[:], scalar1=iotc[0:16, :],
                               scalar2=None, op0=ALU.is_equal)
    p.emit("vector", v_init, deps=[G_INIT], inc=s_v)
    V_INIT = p.bar(s_v)


    g_dma(in_w_sb[:], in_w.rearrange("(a p) c -> p a c", p=128))
    g_dma(xproj_w_sb[:], xproj_w.rearrange("(a p) c -> p a c", p=128))
    g_dma(dt_w_sb[:], dt_w)
    g_dma(out_w_sb[:], out_w.rearrange("(a p) c -> p a c", p=128))
    s_dma(in_b_sb[:], in_b)
    s_dma(conv_b_sb[:], conv_b)
    s_dma(conv_w_sb[:], conv_w)
    s_dma(dt_b_sb[:], dt_b)
    s_dma(D_sb[:], Dvec)
    s_dma(ns_sb[:], norm_scale)
    s_dma(A_sb[:], A_log.rearrange("(a p) n -> p a n", p=128))
    W_INIT = p.bar(s_dw)

    def a_initA(a):
        ap = A_sb[:].rearrange("p a n -> p (a n)")
        return a.activation(out=ap, in_=ap, func=AF.Exp)
    p.emit("scalar", a_initA, deps=[W_INIT], inc=s_a)
    def v_negA(v):
        ap = A_sb[:].rearrange("p a n -> p (a n)")
        return v.tensor_scalar(out=ap, in0=ap, scalar1=-1.0, scalar2=None,
                               op0=ALU.mult)
    p.emit("vector", v_negA, deps=[p.bar(s_a)], inc=s_v, pad=True)
    A_READY = p.bar(s_v)

    def g_zeros(g):
        g.memset(dA_b[0][:, :, 0:1], 0.0)
        g.memset(dA_b[1][:, :, 0:1], 0.0)
        return g.memset(hid[:, :, 0:3], 0.0)
    p.emit("gpsimd", g_zeros, inc=s_g)
    HALO0 = p.bar(s_g)

    prev = {}
    for k in range(NCH):
        half = (k * TC) // TOKF
        tloc = k * TC - half * TOKF

        d0 = [prev["xn_v"], prev["sq_a"]] if k else []
        s_dma(xt[:], x[k * TC:(k + 1) * TC, :].rearrange("(a p) h -> p a h", p=128),
              inc=s_dx, deps=d0)
        XRDY = (s_dx, 16 * (k + 1))

        lnw = [prev["xnt_pe"]] if k else []
        if k:
            p.emit("vector", lambda v: v.engine_nop(), deps=[prev["xnt_pe"]],
                   inc=s_v)
        XN = emit_ln(xt, xnt, NSUB, True, XRDY, junk,
                     jdeps=[(s_do, 64 * k)] if k else [])
        prev["xn_v"] = XN
        prev["sq_a"] = p.bar(s_a)

        inproj_pe_guard = prev.get("inproj_pe")
        for ht in range(8):
            def mk_t(t, bank, ht=ht):
                last = None
                for s in range(NSUB):
                    last = t.transpose(bank[:, s * 128:(s + 1) * 128],
                                       xnt[:, s, ht * 128:(ht + 1) * 128], ident[:])
                return last
            pipeT.produce(mk_t, deps=[XN, V_INIT])
            evdeps = [W_INIT] + ([inproj_pe_guard] if inproj_pe_guard else [])
            def ev_t(v, bank, ht=ht):
                return v.tensor_scalar(out=xn_fm[:, ht, :], in0=bank[:, 0:TC],
                                       scalar1=ns_sb[:, ht:ht + 1], scalar2=None,
                                       op0=ALU.mult)
            pipeT.consume("vector", ev_t, deps=evdeps)
        XNFM = p.bar(pipeT.ev_done)
        prev["xnt_pe"] = (pipeT.pe_done, pipeT.uses)

        hide_dep = [prev["conv_v"]] if k else [HALO0]
        gate_dep = [prev["yg_v"]] if k else []
        for ct in range(24):
            def mk_mm(t, bank, ct=ct):
                last = None
                for ht in range(8):
                    last = t.matmul(bank[:, 0:TC],
                                    in_w_sb[:, ht, ct * 128:(ct + 1) * 128],
                                    xn_fm[:, ht, :], start=(ht == 0), stop=(ht == 7))
                return last
            pipeM.produce(mk_mm, deps=[XNFM, W_INIT])
            if ct < 16:
                def ev_h(a, bank, ct=ct):
                    return a.activation(out=hid[:, ct, 3:3 + TC], in_=bank[:, 0:TC],
                                        func=AF.Identity, bias=in_b_sb[:, ct:ct + 1])
                pipeM.consume("scalar", ev_h, deps=hide_dep + [W_INIT])
            else:
                def ev_g(a, bank, ct=ct):
                    return a.activation(out=gate_s[:, ct - 16, :], in_=bank[:, 0:TC],
                                        func=AF.Silu, bias=in_b_sb[:, ct:ct + 1])
                pipeM.consume("scalar", ev_g, deps=gate_dep + [W_INIT])
        HID = p.bar(pipeM.ev_done)
        prev["inproj_pe"] = (pipeM.pe_done, pipeM.uses)

        silu_guard = [prev["xproj_pe"]] if k else []
        dh_guard = [prev["yd_v"]] if k else []
        silu_ct = {}
        for ct in range(16):
            pr = ct % 2
            def v_conv(v, ct=ct, pr=pr):
                v.tensor_scalar(out=cacc[:, pr, :], in0=hid[:, ct, 0:TC],
                                scalar1=conv_w_sb[:, ct, 0:1], scalar2=None,
                                op0=ALU.mult)
                last = None
                for kk in range(1, KC):
                    last = v.scalar_tensor_tensor(
                        out=cacc[:, pr, :], in0=hid[:, ct, kk:kk + TC],
                        scalar=conv_w_sb[:, ct, kk:kk + 1],
                        in1=cacc[:, pr, :], op0=ALU.mult, op1=ALU.add)
                return last
            wdeps = [HID, W_INIT]
            if k:
                wdeps.append((s_do, 64 * k))
            if ct >= 2:
                wdeps.append(silu_ct[ct - 2])
            p.emit("vector", v_conv, deps=wdeps, inc=s_v)
            CONVD = p.bar(s_v)
            def a_silu(a, ct=ct, pr=pr):
                last = a.activation(out=conv_bf[:, ct, :], in_=cacc[:, pr, :],
                                    func=AF.Silu, bias=conv_b_sb[:, ct:ct + 1])
                if ct < 8:
                    last = a.activation(out=conv_s[:, ct, :], in_=cacc[:, pr, :],
                                        func=AF.Silu, bias=conv_b_sb[:, ct:ct + 1])
                return last
            p.emit("scalar", a_silu,
                   deps=[CONVD, W_INIT] + silu_guard + (dh_guard if ct < 8 else []),
                   inc=s_a)
            silu_ct[ct] = p.bar(s_a)
        CONV = p.bar(s_a)
        def v_halo(v):
            return v.tensor_copy(hid[:, :, 0:3], hid[:, :, TC:TC + 3])
        p.emit("vector", v_halo, deps=[CONV, p.bar(s_v)], inc=s_v)
        prev["conv_v"] = p.bar(s_v)

        tsev_guard = [prev["dt_pe"]] if k else []
        bcev_guard = [prev["brep_pe"], prev["crep_pe"]] if k else []
        def mk_ts(t, bank):
            last = None
            for ct in range(16):
                last = t.matmul(bank[0:64, 0:TC], xproj_w_sb[:, ct, 0:64],
                                conv_bf[:, ct, :], start=(ct == 0), stop=(ct == 15))
            return last
        pipeM.produce(mk_ts, deps=[CONV])
        def ev_ts(v, bank):
            return v.tensor_copy(ts_bf[:], bank[0:64, 0:TC])
        pipeM.consume("vector", ev_ts, deps=tsev_guard)
        def mk_bc(t, bank):
            last = None
            for ct in range(16):
                t.matmul(bank[0:16, 0:TC], xproj_w_sb[:, ct, 64:80],
                         conv_bf[:, ct, :], start=(ct == 0), stop=(ct == 15))
            for ct in range(16):
                last = t.matmul(bank[32:48, 0:TC], xproj_w_sb[:, ct, 80:96],
                                conv_bf[:, ct, :], start=(ct == 0), stop=(ct == 15))
            return last
        pipeM.produce(mk_bc, deps=[CONV])
        def ev_bc(v, bank):
            v.tensor_copy(B_sbuf[:], bank[0:16, 0:TC])
            return v.tensor_copy(C_sbuf[:], bank[32:48, 0:TC])
        pipeM.consume("vector", ev_bc, deps=bcev_guard, pad=True)
        TSBC = p.bar(pipeM.ev_done)
        prev["xproj_pe"] = (pipeM.pe_done, pipeM.uses)

        dA_guard = [prev["dh_v"]] if k else []
        for it in range(NIT):
            def mk_dt(t, bank, it=it):
                return t.matmul(bank[:, 0:TC], dt_w_sb[:, it * 128:(it + 1) * 128],
                                ts_bf[:], start=True, stop=True)
            pipeM.produce(mk_dt, deps=[TSBC])
            def ev_dt(a, bank, it=it):
                a.activation(out=delta[:, it, :], in_=bank[:, 0:TC], func=AF.Exp,
                             bias=dt_b_sb[:, it:it + 1])
                return a.activation(out=delta[:, it, :], in_=delta[:, it, :],
                                    func=AF.Ln, bias=1.0)
            pipeM.consume("scalar", ev_dt, deps=[W_INIT, G_INIT] + dA_guard)
        DELTA = p.bar(pipeM.ev_done)
        prev["dt_pe"] = (pipeM.pe_done, pipeM.uses)

        for it in range(NIT):
            p.emit("vector", lambda v, it=it: v.tensor_tensor(
                out=dh[:, it, :], in0=delta[:, it, :], in1=conv_s[:, it, :],
                op=ALU.mult), deps=[DELTA, CONV], inc=s_v)
        DH = p.bar(s_v)
        prev["dh_v"] = DH

        scan_done = {}
        gx_cnt = {}
        gy_cnt = {}
        acc_cnt = {}
        cb_cnt = {}
        cc_cnt = {}
        for n in range(16):
            q = n % 2
            def mk_brep(t, bank, n=n):
                return t.matmul(bank[:, 0:TC], sel16[:, n * 128:(n + 1) * 128],
                                B_sbuf[:], start=True, stop=True)
            pipeB.produce(mk_brep, deps=[TSBC, V_INIT])
            bB = pipeB.banks[(pipeB.uses - 1) % 2]
            BRDY = (pipeB.pe_done, pipeB.uses)
            def mk_crep(t, bank, n=n):
                return t.matmul(bank[:, 0:TC], sel16[:, n * 128:(n + 1) * 128],
                                C_sbuf[:], start=True, stop=True)
            pipeC.produce(mk_crep, deps=[TSBC, V_INIT])
            bC = pipeC.banks[(pipeC.uses - 1) % 2]
            CRDY = (pipeC.pe_done, pipeC.uses)
            dadeps = [DELTA, A_READY, G_INIT]
            if n >= 2:
                dadeps.append(scan_done[n - 2])
            elif k:
                dadeps.append(prev["scan_m2"][n])
            for it in range(NIT):
                p.emit("scalar", lambda a, it=it, n=n, q=q: a.activation(
                    out=dA_b[q][:, it, 1:TC + 1], in_=delta[:, it, :],
                    func=AF.Exp, scale=A_sb[:, it, n:n + 1]),
                    deps=dadeps, inc=s_a)
            DA = p.bar(s_a)
            def v_x0(v, n=n, q=q):
                return v.tensor_copy(X_b[q][:, :, 0], state[:, :, n])
            p.emit("vector", v_x0, deps=[DH, G_INIT], inc=s_v)
            def v_X(v, bB=bB, q=q):
                return v.tensor_tensor(
                    out=X_b[q][:, :, 1:TC + 1], in0=dh[:],
                    in1=bB[:, 0:TC].rearrange("p (o t) -> p o t", o=1)
                    .broadcast_to([128, NIT, TC]), op=ALU.mult)
            p.emit("vector", v_X, deps=[BRDY], inc=pipeB.ev_done)
            def v_scan(v, q=q):
                return v.tensor_tensor_scan(
                    out=sout[:].rearrange("p a t -> p (a t)"),
                    data0=dA_b[q][:].rearrange("p a t -> p (a t)"),
                    data1=X_b[q][:].rearrange("p a t -> p (a t)"),
                    initial=0.0, op0=ALU.mult, op1=ALU.add)
            p.emit("vector", v_scan, deps=[DA], inc=s_v)
            scan_done[n] = p.bar(s_v)
            tgt = y_acc if n == 0 else tmpy
            ymdeps = [CRDY]
            if n >= 2:
                ymdeps.append((s_gp, acc_cnt[n - 1]))
            elif k:
                ymdeps.append(prev["acc_m2"][1])
            def v_ym(v, bC=bC, tgt=tgt):
                return v.tensor_tensor(
                    out=tgt[:], in0=sout[:, :, 1:TC + 1],
                    in1=bC[:, 0:TC].rearrange("p (o t) -> p o t", o=1)
                    .broadcast_to([128, NIT, TC]), op=ALU.mult)
            p.emit("vector", v_ym, deps=ymdeps, inc=pipeC.ev_done)
            YMV = (pipeC.ev_done, p.count(pipeC.ev_done))
            if n > 0:
                def g_ya(g):
                    return g.tensor_tensor(out=y_acc[:], in0=y_acc[:],
                                           in1=tmpy[:], op=ALU.add)
                p.emit("gpsimd", g_ya, deps=[YMV], inc=s_gp)
            acc_cnt[n] = p.count(s_gp)
            def v_st(v, n=n):
                return v.tensor_copy(state[:, :, n], sout[:, :, TC])
            p.emit("vector", v_st, inc=s_v)
        prev["scan_m2"] = [scan_done[14], scan_done[15]]
        prev["acc_m2"] = [(s_gp, acc_cnt[14]), (s_gp, acc_cnt[15])]
        prev["brep_pe"] = (pipeB.pe_done, pipeB.uses)
        prev["crep_pe"] = (pipeC.pe_done, pipeC.uses)
        YADD_G = (s_gp, acc_cnt[15])
        SCAN = p.bar(s_v)

        for it in range(NIT):
            def v_yd(v, it=it):
                return v.scalar_tensor_tensor(out=y_acc[:, it, :],
                                              in0=conv_s[:, it, :],
                                              scalar=D_sb[:, it:it + 1],
                                              in1=y_acc[:, it, :],
                                              op0=ALU.mult, op1=ALU.add)
            p.emit("vector", v_yd, deps=[SCAN, W_INIT, YADD_G], inc=s_v)
            ygdeps = [prev["op_pe"]] if k and it == 0 else []
            def v_yg(v, it=it):
                return v.tensor_tensor(out=ygate[:, it, :], in0=y_acc[:, it, :],
                                       in1=gate_s[:, it, :], op=ALU.mult)
            p.emit("vector", v_yg, deps=ygdeps, inc=s_v)
        YG = p.bar(s_v)
        prev["yd_v"] = YG
        prev["yg_v"] = YG

        for hg in range(4):
            oev_guard = [(s_do, 64 * k + 16 * hg)] if (k or hg) else []
            for ho in range(2 * hg, 2 * hg + 2):
                def mk_op(t, bank, ho=ho):
                    last = None
                    for it in range(NIT):
                        last = t.matmul(bank[:, 0:TC],
                                        out_w_sb[:, it, ho * 128:(ho + 1) * 128],
                                        ygate[:, it, :], start=(it == 0),
                                        stop=(it == 7))
                    return last
                pipeM.produce(mk_op, deps=[YG])
                def ev_op(a, bank, ho=ho):
                    return a.activation(out=o_sb[:, ho % 2, :], in_=bank[:, 0:TC],
                                        func=AF.Copy)
                pipeM.consume("scalar", ev_op, deps=oev_guard)
            OUTP = p.bar(pipeM.ev_done)
            def d_out(s_, half=half, tloc=tloc, hg=hg):
                return s_.dma_start(
                    out=cc_in.ap()[half * 1024 + hg * 256:
                                   half * 1024 + (hg + 1) * 256, tloc:tloc + TC]
                        .rearrange("(a p) t -> p a t", p=128),
                    in_=o_sb[:])
            p.emit("sync", d_out, deps=[OUTP], inc=s_do, inc_amt=16)
        prev["op_pe"] = (pipeM.pe_done, pipeM.uses)

        if DBG and k == 0:
            taps = [("stat", stat), ("xt", xt), ("xnt", xnt), ("xn_fm", xn_fm), ("hid", hid),
                    ("conv_bf", conv_bf), ("conv_s", conv_s), ("gate_s", gate_s),
                    ("ts_bf", ts_bf), ("B_sbuf", B_sbuf), ("C_sbuf", C_sbuf),
                    ("delta", delta), ("dh", dh), ("y_acc", y_acc),
                    ("ygate", ygate), ("o_sb", o_sb), ("state", state)]
            s_tap = p.sem("s_tap")
            alldeps = [p.bar(s_v), p.bar(s_a), (s_do, 64), p.bar(pipeM.ev_done),
                       p.bar(s_gp)]
            for nm, tile in taps:
                ap = tile[:]
                od = nc.dram_tensor("dbg_" + nm, list(ap.shape), ap.dtype,
                                    kind="ExternalOutput").ap()
                p.emit("sync", lambda s_, od=od, ap=ap: s_.dma_start(
                    out=od, in_=ap), deps=alldeps, inc="s_tap", inc_amt=16)
            TAPS = p.bar(s_tap)
            for eng in ("vector", "scalar", "gpsimd", "sync"):
                p.emit(eng, lambda en: None, deps=[TAPS])
            p.emit("tensor", lambda t: None, deps=[TAPS])

    ALLOUT = (s_do, 64 * NCH)

    if not dbg_ffn_input:
        def g_rs(g):
            return g.collective_compute(
                "ReduceScatter", ALU.add,
                replica_groups=[[0, 1], [2, 3], [4, 5], [6, 7]],
                ins=[cc_in.ap().opt()], outs=[rs_out.ap().opt()])
        p.emit("gpsimd", g_rs, deps=[ALLOUT], inc=s_cc)
    else:
        p.emit("gpsimd", lambda g: g.engine_nop(), deps=[ALLOUT], inc=s_cc)
    RS = p.bar(s_cc)

    PHB = [RS, ALLOUT, p.bar(s_v), p.bar(s_a), p.bar(pipeM.ev_done),
           p.bar(pipeT.ev_done), p.bar(pipeB.ev_done), p.bar(pipeC.ev_done)]
    es_attn.close()

    es_fp = contextlib.ExitStack()
    w_eT = p.sb(es_fp, "w_eT", [8, TOKF], F32)
    sel8 = p.sb(es_fp, "sel8", [8, 8 * 128], F32)
    b1_sb = p.sb(es_fp, "b1_sb", [128, 32, EE], F32)
    b2_sb = p.sb(es_fp, "b2_sb", [128, 8, EE], F32)

    es_pro = contextlib.ExitStack()
    rw_sb = p.sb(es_pro, "rw_sb", [128, 8, EE], BF16)
    outb_sb = p.sb(es_pro, "outb_sb", [128, 1024], F32)
    obrow = p.sb(es_pro, "obrow", [1, 1024], F32)
    ones1 = p.sb(es_pro, "ones1", [1, 128], F32)
    lgt = p.sb(es_pro, "lgt", [128, 4, EE], F32)
    r8 = p.sb(es_pro, "r8", [128, 12, EE], F32)
    wsc = p.sb(es_pro, "wsc", [128, 24], F32)
    io0 = p.sb(es_pro, "io0", [128, 4, EE], F32)
    io1k = p.sb(es_pro, "io1k", [128, 4, EE], F32)
    wdense = p.sb(es_pro, "wdense", [128, 4, EE], F32)
    xrt = p.sb(es_pro, "xrt", [128, 4, 1024], F32)
    h2t = p.sb(es_pro, "h2t", [128, 4, 1024], F32)
    h2nt = p.sb(es_pro, "h2nt", [128, 4, 1024], F32)
    at_fm = p.sb(es_pro, "at_fm", [128, 8, 512], F32)
    hn_fm = p.sb(es_pro, "hn_fm", [128, 8, 512], BF16)
    junkf = p.sb(es_pro, "junkf", [128, 1024], BF16)

    g_dma(rw_sb[:], router_wT.rearrange("(a p) e -> p a e", p=128), deps=PHB)
    s_dma(b1_sb[:], b1, deps=PHB)
    s_dma(b2_sb[:], b2, deps=PHB)
    s_dma(obrow[:], out_b, deps=PHB)
    FW0 = p.bar(s_dw)

    def g_initf(g):
        g.memset(ones1[:], 1.0)
        g.iota(io0[:], pattern=[[0, 4], [1, 8]], base=0,
               channel_multiplier=0, allow_small_or_imprecise_dtypes=True)
        g.iota(io1k[:], pattern=[[0, 4], [1, 8]], base=1000,
               channel_multiplier=0, allow_small_or_imprecise_dtypes=True)
        return g.iota(sel8[:], pattern=[[1, 8], [0, 128]], base=0,
                      channel_multiplier=0, allow_small_or_imprecise_dtypes=True)
    p.emit("gpsimd", g_initf, deps=PHB, inc=s_g)
    GF = p.bar(s_g)
    def v_self8(v):
        return v.tensor_scalar(out=sel8[:], in0=sel8[:], scalar1=iotc[0:8, :],
                               scalar2=None, op0=ALU.is_equal)
    p.emit("vector", v_self8, deps=[GF] + PHB, inc=s_v)
    SEL8 = p.bar(s_v)

    for hhalf in range(2):
        def mk_ob(t, bank, hh=hhalf):
            return t.matmul(bank[:, 0:512], ones1[:],
                            obrow[0:1, hh * 512:(hh + 1) * 512],
                            start=True, stop=True)
        pipeM.produce(mk_ob, deps=[FW0, GF] + PHB)
        def ev_ob(v, bank, hh=hhalf):
            return v.tensor_copy(outb_sb[:, hh * 512:(hh + 1) * 512],
                                 bank[:, 0:512])
        pipeM.consume("vector", ev_ob, deps=PHB)
    OBREP = p.bar(pipeM.ev_done)

    fprev = {}
    for sl in range(NSL):
        c0 = sl * 512
        xdep = [fprev["h2_v"]] if sl else PHB
        s_dma(xrt[:], xr[c0:c0 + 512, :].rearrange("(a p) h -> p a h", p=128),
              inc=s_dx, deps=xdep)
        adep = [fprev["at_pe"]] if sl else PHB
        s_dma(at_fm[:], rs_out.ap()[:, c0:c0 + 512]
              .rearrange("(a p) t -> p a t", p=128), inc=s_dx, deps=adep + [RS])
        XA = (s_dx, p.count(s_dx))

        for s in range(4):
            p.emit("vector", lambda v, s=s: v.tensor_tensor(
                out=xrt[:, s, :], in0=xrt[:, s, :], in1=outb_sb[:], op=ALU.add),
                deps=[XA, OBREP], inc=s_v)
        XB = p.bar(s_v)

        for tt in range(4):
            for hh in range(2):
                def mk_at(t, bank, tt=tt, hh=hh):
                    last = None
                    for ho in range(4 * hh, 4 * hh + 4):
                        last = t.transpose(
                            bank[:, (ho - 4 * hh) * 128:(ho - 4 * hh + 1) * 128],
                            at_fm[:, ho, tt * 128:(tt + 1) * 128], ident[:])
                    return last
                pipeT.produce(mk_at, deps=[XA, V_INIT] + (PHB if sl == 0 else []))
                def ev_at(v, bank, tt=tt, hh=hh):
                    return v.tensor_tensor(out=h2t[:, tt, hh * 512:(hh + 1) * 512],
                                           in0=bank[:, 0:512],
                                           in1=xrt[:, tt, hh * 512:(hh + 1) * 512],
                                           op=ALU.add)
                h2t_guard = []
                if "h2_dma" in fprev:
                    h2t_guard = [fprev["h2_dma"], fprev["sq2"]]
                pipeT.consume("vector", ev_at, deps=[XB] + h2t_guard)
        fprev["at_pe"] = (pipeT.pe_done, pipeT.uses)
        H2 = p.bar(pipeT.ev_done)

        s_dma(h2_dram.ap()[c0:c0 + 512, :].rearrange("(a p) h -> p a h", p=128),
              h2t[:], inc=s_dh2, deps=[H2])
        fprev["h2_dma"] = (s_dh2, p.count(s_dh2))

        if sl:
            p.emit("vector", lambda v: v.engine_nop(),
                   deps=[fprev["hnt_pe"]], inc=s_v)
        H2N = emit_ln(h2t, h2nt, 4, False, H2, junkf[:])
        fprev["h2_v"] = H2N
        fprev["sq2"] = p.bar(s_a)

        for ho in range(8):
            def mk_hn(t, bank, ho=ho):
                last = None
                for tt in range(4):
                    last = t.transpose(bank[:, tt * 128:(tt + 1) * 128],
                                       h2nt[:, tt, ho * 128:(ho + 1) * 128],
                                       ident[:])
                return last
            pipeT.produce(mk_hn, deps=[H2N])
            fprev["hnt_pe"] = (pipeT.pe_done, pipeT.uses)
            def ev_hn(v, bank, ho=ho):
                return v.tensor_copy(hn_fm[:, ho, :], bank[:, 0:512])
            hn_guard = []
            if "r_pe" in fprev:
                hn_guard = [fprev["r_pe"], fprev["hn_dma"]]
            pipeT.consume("vector", ev_hn, deps=hn_guard)
        HNF = p.bar(pipeT.ev_done)
        s_dma(h2n_dram.ap()[:, c0:c0 + 512].rearrange("(a p) t -> p a t", p=128),
              hn_fm[:], inc=s_dh2, deps=[HNF])
        fprev["hn_dma"] = (s_dh2, p.count(s_dh2))

        def mk_r(t, bank):
            last = None
            for tt in range(4):
                for ht in range(8):
                    last = t.matmul(bank[:, tt * 8:(tt + 1) * 8],
                                    hn_fm[:, ht, tt * 128:(tt + 1) * 128],
                                    rw_sb[:, ht, :], start=(ht == 0),
                                    stop=(ht == 7))
            return last
        pipeM.produce(mk_r, deps=[HNF, FW0])
        fprev["r_pe"] = (pipeM.pe_done, pipeM.uses)
        def ev_r(v, bank):
            return v.tensor_copy(lgt[:].rearrange("p a e -> p (a e)"),
                                 bank[:, 0:32])
        pipeM.consume("vector", ev_r,
                      deps=[fprev["wd_pe"]] if "wd_pe" in fprev else [],
                      pad=True)
        LG = p.bar(pipeM.ev_done)

        def bc(ap):
            return ap.broadcast_to([128, 4, EE])
        lg3 = lgt[:]
        m1 = wsc[:, 0:4].rearrange("p (a o) -> p a o", o=1)
        ix1 = wsc[:, 4:8].rearrange("p (a o) -> p a o", o=1)
        m2 = wsc[:, 8:12].rearrange("p (a o) -> p a o", o=1)
        ix2 = wsc[:, 12:16].rearrange("p (a o) -> p a o", o=1)
        w1c = wsc[:, 16:20].rearrange("p (a o) -> p a o", o=1)
        w2c = wsc[:, 20:24].rearrange("p (a o) -> p a o", o=1)
        r8a = r8[:, 0:4, :]
        r8b = r8[:, 4:8, :]
        r8c = r8[:, 8:12, :]
        RV = [LG, GF]
        def step(fn, eng="vector", deps=()):
            p.emit(eng, fn, deps=list(deps) + RV, inc=s_v if eng == "vector"
                   else s_a, pad=True)
        step(lambda v: v.tensor_reduce(m1, lg3, axis=AX.X, op=ALU.max))
        step(lambda v: v.tensor_tensor(out=r8a, in0=lg3, in1=bc(m1),
                                       op=ALU.is_equal))
        step(lambda v: v.scalar_tensor_tensor(out=r8b, in0=r8a, scalar=-1000.0,
                                              in1=io1k[:], op0=ALU.mult,
                                              op1=ALU.add))
        step(lambda v: v.tensor_reduce(ix1, r8b, axis=AX.X, op=ALU.min))
        step(lambda v: v.tensor_tensor(out=r8a, in0=io0[:], in1=bc(ix1),
                                       op=ALU.is_equal))
        step(lambda v: v.scalar_tensor_tensor(out=r8b, in0=r8a, scalar=-1e30,
                                              in1=lg3, op0=ALU.mult,
                                              op1=ALU.add))
        step(lambda v: v.tensor_reduce(m2, r8b, axis=AX.X, op=ALU.max))
        step(lambda v: v.tensor_tensor(out=r8c, in0=r8b, in1=bc(m2),
                                       op=ALU.is_equal))
        step(lambda v: v.scalar_tensor_tensor(out=r8b, in0=r8c, scalar=-1000.0,
                                              in1=io1k[:], op0=ALU.mult,
                                              op1=ALU.add))
        step(lambda v: v.tensor_reduce(ix2, r8b, axis=AX.X, op=ALU.min))
        step(lambda v: v.tensor_tensor(out=r8c, in0=io0[:], in1=bc(ix2),
                                       op=ALU.is_equal))
        step(lambda v: v.tensor_tensor(out=w1c, in0=m1, in1=m2,
                                       op=ALU.subtract))
        W1D = p.bar(s_v)
        def a_w1(a):
            a.activation(out=w1c, in_=w1c, func=AF.Tanh, scale=0.5)
            return a.activation(out=w1c, in_=w1c, func=AF.Identity, scale=0.5,
                                bias=0.5)
        step(a_w1, eng="scalar", deps=[W1D, G_INIT])
        W1S = p.bar(s_a)
        step(lambda v: v.tensor_scalar(out=w2c, in0=w1c, scalar1=-1.0,
                                       scalar2=1.0, op0=ALU.mult, op1=ALU.add),
             deps=[W1S])
        step(lambda v: v.tensor_tensor(out=r8a, in0=r8a, in1=bc(w1c),
                                       op=ALU.mult))
        step(lambda v: v.tensor_tensor(out=r8c, in0=r8c, in1=bc(w2c),
                                       op=ALU.mult))
        step(lambda v: v.tensor_tensor(out=wdense[:], in0=r8a, in1=r8c,
                                       op=ALU.add))
        WD = p.bar(s_v)

        def mk_wt(t, bank):
            last = None
            for tt in range(4):
                last = t.transpose(bank[0:8, tt * 128:(tt + 1) * 128],
                                   wdense[:, tt, :], ident[:])
            return last
        pipeM.produce(mk_wt, deps=[WD])
        def ev_wt(v, bank, sl=sl):
            return v.tensor_copy(w_eT[:, sl * 512:(sl + 1) * 512], bank[0:8, 0:512])
        pipeM.consume("vector", ev_wt)
        fprev["wd_pe"] = (pipeM.pe_done, pipeM.uses)
    fprev["hn_pe"] = (pipeT.pe_done, pipeT.uses)
    PRO = [p.bar(pipeM.ev_done), p.bar(s_dh2), p.bar(s_v), p.bar(s_a),
           (pipeM.pe_done, pipeM.uses), (pipeT.pe_done, pipeT.uses),
           p.bar(pipeT.ev_done), p.bar(s_dx), p.bar(s_g)]
    WET = p.bar(pipeM.ev_done)
    es_pro.close()
    es_exp = contextlib.ExitStack()
    s_ac2 = p.sem("s_ac2")
    w1_sb = [p.sb(es_exp, f"w1_sb{i}", [128, 8, FFD // 2], BF16)
             for i in range(2)]
    w2_sb = [p.sb(es_exp, f"w2_sb{i}", [128, 16, 1024], BF16)
             for i in range(2)]
    h1_sb = p.sb(es_exp, "h1_sb", [128, 16, 512], BF16)
    h2n_sb = [p.sb(es_exp, f"h2n_sb{i}", [128, 8, 512], BF16)
              for i in range(2)]
    tmp_sb = p.sb(es_exp, "tmp_sb", [128, 8, 512], F32)
    wrep_sb = [p.sb(es_exp, f"wrep_sb{i}", [128, 512], F32) for i in range(2)]

    NU = 2 * EE

    def emit_wload(u, deps):
        e, fh = u // 2, u % 2
        pu = u % 2
        g_dma(w1_sb[pu][:],
              w1T[e][:, fh * (FFD // 2):(fh + 1) * (FFD // 2)]
              .rearrange("(a p) f -> p a f", p=128), inc="s_dwf", deps=deps)
        g_dma(w2_sb[pu][:],
              w2T[e][fh * (FFD // 2):(fh + 1) * (FFD // 2), :]
              .rearrange("(a p) h -> p a h", p=128), inc="s_dwf", deps=deps)

    def emit_hnload(u, tc, deps):
        s_dma(h2n_sb[tc % 2][:],
              h2n_dram.ap()[:, tc * 512:(tc + 1) * 512]
              .rearrange("(a p) t -> p a t", p=128),
              inc="s_dhn", deps=deps)

    emit_wload(0, PRO)
    emit_wload(1, PRO)
    emit_hnload(0, 0, PRO + [p.bar(s_dh2)])
    nacc = [0, 0]
    wlast = {}
    for u in range(NU):
        e, fh, pu = u // 2, u % 2, u % 2
        WF = (s_dwf, 32 * (u + 1))
        for tc in range(NSL):
            if not (u == NU - 1 and tc == NSL - 1):
                ntc = (tc + 1) % NSL
                hdep = [fprev["h1_pe"]] if "h1_pe" in fprev else PRO
                emit_hnload(u, ntc, hdep)
            def mk_wr(t, bank, e=e, tc=tc):
                return t.matmul(bank[:, 0:512], sel8[:, e * 128:(e + 1) * 128],
                                w_eT[:, tc * 512:(tc + 1) * 512],
                                start=True, stop=True)
            pipeC.produce(mk_wr, deps=[WET, SEL8])
            def ev_wr(v, bank, tc=tc):
                return v.tensor_copy(wrep_sb[tc % 2][:], bank[:, 0:512])
            pipeC.consume("vector", ev_wr,
                          deps=[fprev["wr_used"]] if "wr_used" in fprev else [])
            WR = p.bar(pipeC.ev_done)
            HN = (s_dhn, 16 * (u * NSL + tc + 1))

            for ft in range(16):
                def mk_h1(t, bank, ft=ft, pu=pu, tc=tc):
                    last = None
                    for ht in range(8):
                        last = t.matmul(bank[:, 0:512],
                                        w1_sb[pu][:, ht, ft * 128:(ft + 1) * 128],
                                        h2n_sb[tc % 2][:, ht, :],
                                        start=(ht == 0), stop=(ht == 7))
                    return last
                pipeM.produce(mk_h1, deps=[WF, HN])
                def ev_h1(a, bank, ft=ft, e=e, fh=fh):
                    return a.activation(out=h1_sb[:, ft, :], in_=bank[:, 0:512],
                                        func=AF.Gelu,
                                        bias=b1_sb[:, fh * 16 + ft, e:e + 1])
                pipeM.consume("scalar", ev_h1,
                              deps=([fprev["o2_pe"]] if "o2_pe" in fprev else [])
                              + [FW0])
            H1 = p.bar(pipeM.ev_done)
            fprev["h1_pe"] = (pipeM.pe_done, pipeM.uses)

            op = None if u == 0 else ALU.add
            for hg in range(2):
                sacc = s_acc if hg == 0 else s_ac2
                for ho in range(hg * 4, hg * 4 + 4):
                    def mk_o2(t, bank, ho=ho, pu=pu):
                        last = None
                        for ft in range(16):
                            last = t.matmul(bank[:, 0:512],
                                            w2_sb[pu][:, ft, ho * 128:(ho + 1) * 128],
                                            h1_sb[:, ft, :], start=(ft == 0),
                                            stop=(ft == 15))
                        return last
                    pipeB.produce(mk_o2, deps=[H1, WF])
                    tmpdep = [(sacc, nacc[hg] * 16)]
                    if fh == 0:
                        def ev_o2(v, bank, ho=ho, tc=tc):
                            return v.tensor_tensor(
                                out=tmp_sb[:, ho, :], in0=bank[:, 0:512],
                                in1=wrep_sb[tc % 2][:], op=ALU.mult)
                    else:
                        def ev_o2(v, bank, ho=ho, e=e, tc=tc):
                            return v.scalar_tensor_tensor(
                                out=tmp_sb[:, ho, :], in0=bank[:, 0:512],
                                scalar=b2_sb[:, ho, e:e + 1],
                                in1=wrep_sb[tc % 2][:],
                                op0=ALU.add, op1=ALU.mult)
                    pipeB.consume("vector", ev_o2, deps=tmpdep + [WR, FW0])
                O2 = p.bar(pipeB.ev_done)
                def d_acc(g, tc=tc, op=op, hg=hg):
                    kw = {} if op is None else dict(accum_op=op)
                    return g.dma_start(
                        out=acc_fm.ap()[hg * 512:(hg + 1) * 512,
                                        tc * 512:(tc + 1) * 512]
                        .rearrange("(a p) t -> p a t", p=128),
                        in_=tmp_sb[:, hg * 4:hg * 4 + 4, :], **kw)
                p.emit("gpsimd", d_acc, deps=[O2, (sacc, nacc[hg] * 16)],
                       inc="s_acc" if hg == 0 else "s_ac2", inc_amt=16)
                nacc[hg] += 1
            fprev["o2_pe"] = (pipeB.pe_done, pipeB.uses)
            fprev["wr_used"] = p.bar(pipeB.ev_done)
        wlast[pu] = [(pipeB.pe_done, pipeB.uses), (pipeM.pe_done, pipeM.uses)]
        if u + 2 < NU:
            emit_wload(u + 2, wlast[pu])

    ACC = [(s_acc, nacc[0] * 16), (s_ac2, nacc[1] * 16)]
    EXP = ACC + [p.bar(pipeB.ev_done), p.bar(pipeC.ev_done), p.bar(s_v),
                 p.bar(s_a), (pipeB.pe_done, pipeB.uses),
                 (pipeM.pe_done, pipeM.uses), p.bar(s_dwf), p.bar(s_dhn)]
    es_exp.close()
    es_epi = contextlib.ExitStack()
    at2 = p.sb(es_epi, "at2", [128, 8, 512], F32)
    ht2 = p.sb(es_epi, "ht2", [128, 4, 1024], F32)
    ot_sb = p.sb(es_epi, "ot_sb", [128, 4, 1024], F32)

    for sl in range(NSL):
        c0 = sl * 512
        adep = [fprev["ep_pe"]] if "ep_pe" in fprev else EXP
        s_dma(at2[:], acc_fm.ap()[:, c0:c0 + 512]
              .rearrange("(a p) t -> p a t", p=128), inc=s_dx,
              deps=adep + ACC)
        hdep = [fprev["ep_v"]] if "ep_v" in fprev else EXP
        s_dma(ht2[:], h2_dram.ap()[c0:c0 + 512, :]
              .rearrange("(a p) h -> p a h", p=128), inc=s_dx,
              deps=hdep + [p.bar(s_dh2)])
        EXA = (s_dx, p.count(s_dx))
        for tt in range(4):
            for hh in range(2):
                def mk_ep(t, bank, tt=tt, hh=hh):
                    last = None
                    for ho in range(4 * hh, 4 * hh + 4):
                        last = t.transpose(
                            bank[:, (ho - 4 * hh) * 128:(ho - 4 * hh + 1) * 128],
                            at2[:, ho, tt * 128:(tt + 1) * 128], ident[:])
                    return last
                pipeT.produce(mk_ep, deps=[EXA] + (EXP if sl == 0 else []))
                def ev_ep(v, bank, tt=tt, hh=hh):
                    return v.tensor_tensor(out=ot_sb[:, tt, hh * 512:(hh + 1) * 512],
                                           in0=bank[:, 0:512],
                                           in1=ht2[:, tt, hh * 512:(hh + 1) * 512],
                                           op=ALU.add)
                pipeT.consume("vector", ev_ep,
                              deps=[EXA, (s_out, 16 * sl)] if sl else [EXA])
        fprev["ep_pe"] = (pipeT.pe_done, pipeT.uses)
        fprev["ep_v"] = p.bar(pipeT.ev_done)
        EOUT = p.bar(pipeT.ev_done)
        s_dma(out_hbm[c0:c0 + 512, :].rearrange("(a p) h -> p a h", p=128),
              ot_sb[:], inc="s_out", deps=[EOUT])

    if DBG:
        for nm, dr in [("cc_in", cc_in), ("rs_out", rs_out), ("h2", h2_dram),
                       ("h2n", h2n_dram), ("acc", acc_fm)]:
            if dbg_ffn_input and nm == "rs_out":
                continue
            ap = dr.ap()
            od = nc.dram_tensor("dbgd_" + nm, list(ap.shape), ap.dtype,
                                kind="ExternalOutput").ap()
            p.emit("sync", lambda s_, od=od, ap=ap: s_.dma_start(out=od, in_=ap),
                   deps=[(s_out, 16 * NSL)], inc="s_out", inc_amt=16)
        p.emit("sync", lambda s_: s_.nop(),
               deps=[("s_out", p.count("s_out"))])

    p.emit("sync", lambda s_: s_.nop(), deps=[(s_out, 16 * NSL)])
    p.emit("gpsimd", lambda g: g.engine_nop(), deps=[(s_out, 16 * NSL)])

    es_epi.close()
    es_fp.close()
    return p


def core_inputs(inputs, c):
    b, half = c // 2, c % 2
    f = np.ascontiguousarray
    own = slice(half * 1024, (half + 1) * 1024)
    oth = slice((1 - half) * 1024, (2 - half) * 1024)
    perm = np.r_[np.arange(own.start, own.stop), np.arange(oth.start, oth.stop)]
    in_w = inputs["L0_attn_in_w"]
    d = {
        "x": f(inputs["L0_attn_x"][b]),
        "xr": f(inputs["L0_attn_x"][b][half * 4096:(half + 1) * 4096]),
        "in_w": f(np.concatenate([in_w[:, perm], in_w[:, 2048 + own.start:2048 + own.stop]], 1)),
        "in_b": col128(np.concatenate([inputs["L0_attn_in_b"][perm],
                                       inputs["L0_attn_in_b"][2048:][own]])),
        "conv_w": f(inputs["L0_attn_conv_w"][:, perm].T.reshape(16, 128, 4)
                    .transpose(1, 0, 2)),
        "conv_b": col128(inputs["L0_attn_conv_b"][perm]),
        "xproj_w": f(inputs["L0_attn_xproj_w"][perm]),
        "dt_w": f(inputs["L0_attn_dt_w"][:, own]),
        "dt_b": col128(inputs["L0_attn_dt_b"][own]),
        "A_log": f(inputs["L0_attn_A_log"][own]),
        "D": col128(inputs["L0_attn_D"][own]),
        "out_w": f(inputs["L0_attn_out_w"][own]),
        "out_b": f(inputs["L0_attn_out_b"][None]),
        "norm_scale": col128(inputs["L0_attn_norm_scale"]),
        "router_wT": f(inputs["L0_ffn_router_w"].T),
        "w1T": f(inputs["L0_ffn_w1"].transpose(0, 2, 1)),
        "b1": f(inputs["L0_ffn_b1"].T.reshape(32, 128, 8).transpose(1, 0, 2)),
        "w2T": f(inputs["L0_ffn_w2"].transpose(0, 2, 1)),
        "b2": f(inputs["L0_ffn_b2"].T.reshape(8, 128, 8).transpose(1, 0, 2)),
    }
    return d


def col128(v):
    v = np.asarray(v)
    return np.ascontiguousarray(v.reshape(-1, 128).T)


_CACHE = {}


def kernel(**inputs):
    from concourse.bass_utils import run_bass_kernel_spmd
    if "nc" not in _CACHE:
        p = build_program(T=8192)
        _CACHE["nc"] = p.build()
    nc = _CACHE["nc"]
    in_maps = [core_inputs(inputs, c) for c in range(8)]
    res = run_bass_kernel_spmd(nc, in_maps, core_ids=list(range(8)))
    out = np.zeros((4, 8192, 1024), np.float32)
    for c in range(8):
        b, half = c // 2, c % 2
        out[b, half * 4096:(half + 1) * 4096] = res.results[c]["out"]
    return out


# version 13
# speedup vs baseline: 1.2071x; 1.0230x over previous
import contextlib
import numpy as np
import concourse.bass as bass
import concourse.mybir as mybir

F32 = mybir.dt.float32
BF16 = mybir.dt.bfloat16
AF = mybir.ActivationFunctionType
ALU = mybir.AluOpType
AX = mybir.AxisListType

ENGINES = ("sync", "scalar", "vector", "gpsimd", "tensor")


class Prog:
    def __init__(self):
        self.nc = bass.Bass("TRN2", target_bir_lowering=False, debug=False)
        self.es = contextlib.ExitStack()
        self.sched = {e: [] for e in ENGINES}
        self.waited = {e: {} for e in ENGINES}
        self.sems = {}
        self.boot = self.es.enter_context(self.nc.semaphore("boot"))
        self.jka = self.es.enter_context(self.nc.sbuf_tensor("jka", [128, 64], mybir.dt.float32))
        self.jkb = self.es.enter_context(self.nc.sbuf_tensor("jkb", [128, 64], mybir.dt.float32))

    def sem(self, name):
        s = self.es.enter_context(self.nc.semaphore(name))
        self.sems[name] = [s, 0]
        return name

    def sb(self, stack, name, shape, dtype):
        return stack.enter_context(self.nc.sbuf_tensor(name, list(shape), dtype))

    def ps(self, name, shape, dtype=F32):
        return self.es.enter_context(self.nc.psum_tensor(name, list(shape), dtype))

    def junk(self, eng, engine):
        if eng == "vector":
            return engine.tensor_copy(self.jkb[:], self.jka[:])
        if eng == "scalar":
            return engine.activation(out=self.jkb[:], in_=self.jka[:],
                                     func=AF.Identity)
        if eng == "gpsimd":
            return engine.tensor_copy(self.jkb[:], self.jka[:])
        raise ValueError(eng)

    def emit(self, eng, fn, deps=(), inc=None, inc_amt=1, pad=False):
        deps = [(s, c) for (s, c) in deps if c > self.waited[eng].get(s, 0)]
        for s, c in deps:
            self.waited[eng][s] = c
        if inc is not None:
            self.sems[inc][1] += inc_amt
        def run(engine, deps=deps, fn=fn, inc=inc, inc_amt=inc_amt, pad=pad):
            for s, c in deps:
                engine.wait_ge(self.sems[s][0], c)
            ret = fn(engine)
            if pad:
                ret = self.junk(eng, engine)
            if inc is not None:
                assert ret is not None
                ret.then_inc(self.sems[inc][0], inc_amt)
        self.sched[eng].append(run)

    def count(self, sem):
        return self.sems[sem][1]

    def bar(self, sem):
        return (sem, self.sems[sem][1])

    def build(self):
        nums = sorted([h.num for h, _ in self.sems.values()] + [self.boot.num])
        lo, hi = min(nums), max(nums) + 1
        rng = range(lo, hi)
        finals = [(h, c) for (h, c) in self.sems.values()]
        block = self.es.enter_context(self.nc.Block())
        for eng in ENGINES:
            def body(engine, eng=eng, rng=rng, finals=finals):
                if eng == "gpsimd":
                    engine.dma_reset(rng)
                    engine.sem_clear(rng)
                    engine.engine_nop().then_inc(self.boot, 1)
                else:
                    engine.wait_ge(self.boot, 1)
                for fn in self.sched[eng]:
                    fn(engine)
                if eng == "gpsimd":
                    engine.wait_ge(self.boot, 5)
                    for h, c in finals:
                        if c > 0:
                            engine.wait_ge(h, c)
                    engine.dma_reset(rng)
                    engine.sem_clear(rng)
                else:
                    engine.nop().then_inc(self.boot, 1)
            getattr(block, eng)(body)
        return self.nc


class BankPipe:

    def __init__(self, p, name, shape=(128, 512), n=2, dtype=F32):
        self.p = p
        self.banks = [p.ps(f"{name}{i}", shape, dtype) for i in range(n)]
        self.pe_done = p.sem(f"{name}_pd")
        self.ev_done = p.sem(f"{name}_ed")
        self.n = n
        self.uses = 0

    def produce(self, fn, deps=()):
        i = self.uses % self.n
        bank = self.banks[i]
        d = list(deps)
        if self.uses >= self.n:
            d.append((self.ev_done, self.uses - self.n + 1))
        self.uses += 1
        self.p.emit("tensor", lambda t, b=bank, fn=fn: fn(t, b), deps=d,
                    inc=self.pe_done)
        return bank

    def consume(self, eng, fn, deps=(), pad=False):
        i = (self.uses - 1) % self.n
        bank = self.banks[i]
        d = list(deps) + [(self.pe_done, self.uses)]
        self.p.emit(eng, lambda e, b=bank, fn=fn: fn(e, b), deps=d,
                    inc=self.ev_done, pad=pad)


def build_program(dbg_ffn_input=False, dbg_taps=False, T=8192):
    DBG = dbg_taps
    """Returns (nc, names). If dbg_ffn_input, rs_out is an ExternalInput."""
    p = Prog()
    nc = p.nc

    TC = 256
    NCH = T // TC
    NSUB = TC // 128
    IC, IF, NST, KC = 1024, 2048, 16, 4
    NIT = IC // 128
    NCT = IF // 128
    TOKF = T // 2
    NSL = TOKF // 512
    EPS_LN, EPS_RMS = 1e-5, 1e-6
    EE, FFD = 8, 4096

    def din(name, shape, dtype=F32):
        return nc.dram_tensor(name, list(shape), dtype, kind="ExternalInput").ap()

    x = din("x", [T, 1024])
    xr = din("xr", [TOKF, 1024])
    in_w = din("in_w", [1024, IF + IC])
    in_b = din("in_b", [128, 24])
    conv_w = din("conv_w", [128, 16, KC])
    conv_b = din("conv_b", [128, 16])
    xproj_w = din("xproj_w", [IF, 96])
    dt_w = din("dt_w", [64, IC])
    dt_b = din("dt_b", [128, 8])
    A_log = din("A_log", [IC, NST])
    Dvec = din("D", [128, 8])
    out_w = din("out_w", [IC, 1024])
    out_b = din("out_b", [1, 1024])
    norm_scale = din("norm_scale", [128, 8])
    router_wT = din("router_wT", [1024, EE])
    w1T = din("w1T", [EE, 1024, FFD])
    b1 = din("b1", [128, 32, EE])
    w2T = din("w2T", [EE, FFD, 1024])
    b2 = din("b2", [128, 8, EE])

    out_hbm = nc.dram_tensor("out", [TOKF, 1024], F32, kind="ExternalOutput").ap()

    cc_in = nc.dram_tensor("cc_in", [2048, TOKF], F32)
    if dbg_ffn_input:
        rs_out = nc.dram_tensor("rs_out", [1024, TOKF], F32, kind="ExternalInput")
    else:
        rs_out = nc.dram_tensor("rs_out", [1024, TOKF], F32)
    h2n_dram = nc.dram_tensor("h2n_dram", [1024, TOKF], BF16)
    h2_dram = nc.dram_tensor("h2_dram", [TOKF, 1024], F32)
    acc_fm = nc.dram_tensor("acc_fm", [1024, TOKF], F32)

    s_dx = p.sem("s_dx")
    s_dw = p.sem("s_dw")
    s_v = p.sem("s_v")
    s_a = p.sem("s_a")
    s_g = p.sem("s_g")
    s_do = p.sem("s_do")
    s_cc = p.sem("s_cc")
    s_dh2 = p.sem("s_dh2")
    s_dhn = p.sem("s_dhn")
    s_acc = p.sem("s_acc")
    s_dwf = p.sem("s_dwf")
    s_gp = p.sem("s_gp")
    s_out = p.sem("s_out")

    pipeT = BankPipe(p, "pT", n=2)
    pipeM = BankPipe(p, "pM", n=2)
    pipeB = BankPipe(p, "pB", n=2)
    pipeC = BankPipe(p, "pC", n=2)

    ident = p.sb(p.es, "ident", [128, 128], F32)
    iotf = p.sb(p.es, "iotf", [128, 128], F32)
    iotc = p.sb(p.es, "iotc", [128, 1], F32)
    stat = p.sb(p.es, "stat", [128, 4, 8], F32)
    CVALS = [EPS_LN, EPS_RMS, 1.0, -0.5, 0.5, 1.0 / 1024,
             (1.0 + EPS_RMS) / 1024, 1e-11]
    cst = p.sb(p.es, "cst", [128, len(CVALS)], F32)
    for j, v in enumerate(CVALS):
        nc.const_aps.register(F32, v, cst[:, j:j + 1])

    def g_dma(dst, src, inc="s_dw", deps=()):
        p.emit("gpsimd", lambda g, d=dst, s=src: g.dma_start(out=d, in_=s),
               deps=deps, inc=inc, inc_amt=16)

    def s_dma(dst, src, inc="s_dw", deps=()):
        p.emit("sync", lambda s_, d=dst, s=src: s_.dma_start(out=d, in_=s),
               deps=deps, inc=inc, inc_amt=16)

    def emit_ln(xin, xout, ns, rms, xdep, junk, jdeps=()):
        st_ = lambda c: stat[:, 0:ns, c]
        for s in range(ns):
            p.emit("vector", lambda v, s=s: v.reduce_sum(
                stat[:, s, 0:1], xin[:, s, :], axis=AX.X),
                deps=[xdep, G_INIT], inc=s_v, pad=(s == ns - 1))
        SM = p.bar(s_v)
        for s in range(ns):
            p.emit("scalar", lambda a, s=s, junk=junk: a.activation(
                out=junk, in_=xin[:, s, :], func=AF.Square,
                accum_out=stat[:, s, 1:2]), deps=[xdep] + list(jdeps),
                inc=s_a, pad=(s == ns - 1))
        SQ = p.bar(s_a)
        def v_u(v):
            v.tensor_tensor(out=st_(3), in0=st_(0), in1=st_(0), op=ALU.mult)
            v.tensor_scalar(out=st_(2), in0=st_(0), scalar1=1.0 / 1024,
                            scalar2=None, op0=ALU.mult)
            return None
        p.emit("vector", v_u, deps=[SM])
        p.emit("vector", lambda v: p.junk("vector", v))
        def v_w(v):
            return v.scalar_tensor_tensor(out=st_(4), in0=st_(3),
                                          scalar=-1.0 / 1024, in1=st_(1),
                                          op0=ALU.mult, op1=ALU.add)
        p.emit("vector", v_w, deps=[SQ], inc=s_v, pad=True)
        VW = p.bar(s_v)
        if rms:
            sc, bi = (1.0 + EPS_RMS) / 1024, 1e-11
        else:
            sc, bi = 1.0 / 1024, EPS_LN
        def a_sc(a, sc=sc, bi=bi):
            a.activation(out=st_(5), in_=st_(4), func=AF.Ln, scale=sc, bias=bi)
            return a.activation(out=st_(5), in_=st_(5), func=AF.Exp, scale=-0.5)
        p.emit("scalar", a_sc, deps=[VW, G_INIT], inc=s_a, pad=True)
        SCL = p.bar(s_a)
        for s in range(ns):
            p.emit("vector", lambda v, s=s: v.tensor_scalar(
                out=xout[:, s, :], in0=xin[:, s, :], scalar1=stat[:, s, 2:3],
                scalar2=stat[:, s, 5:6], op0=ALU.subtract, op1=ALU.mult),
                deps=[SCL], inc=s_v)
        return p.bar(s_v)

    es_attn = contextlib.ExitStack()
    sel16 = p.sb(es_attn, "sel16", [16, 16 * 128], F32)
    state = p.sb(es_attn, "state", [128, NIT, NST], F32)
    in_w_sb = p.sb(es_attn, "in_w_sb", [128, 8, IF + IC], BF16)
    xproj_w_sb = p.sb(es_attn, "xproj_w_sb", [128, 16, 96], BF16)
    dt_w_sb = p.sb(es_attn, "dt_w_sb", [64, IC], BF16)
    out_w_sb = p.sb(es_attn, "out_w_sb", [128, 8, 1024], BF16)
    in_b_sb = p.sb(es_attn, "in_b_sb", [128, 24], F32)
    conv_b_sb = p.sb(es_attn, "conv_b_sb", [128, 16], F32)
    conv_w_sb = p.sb(es_attn, "conv_w_sb", [128, 16, KC], F32)
    dt_b_sb = p.sb(es_attn, "dt_b_sb", [128, 8], F32)
    A_sb = p.sb(es_attn, "A_sb", [128, 8, NST], F32)
    D_sb = p.sb(es_attn, "D_sb", [128, 8], F32)
    ns_sb = p.sb(es_attn, "ns_sb", [128, 8], F32)
    xt = p.sb(es_attn, "xt", [128, NSUB, 1024], F32)
    xnt = xt
    xn_fm = p.sb(es_attn, "xn_fm", [128, 8, TC], BF16)
    hid = p.sb(es_attn, "hid", [128, NCT, TC + 3], BF16)
    cacc = p.sb(es_attn, "cacc", [128, 2, TC], F32)
    conv_s = p.sb(es_attn, "conv_s", [128, NIT, TC], F32)
    conv_bf = p.sb(es_attn, "conv_bf", [128, NCT, TC], BF16)
    gate_s = p.sb(es_attn, "gate_s", [128, NIT, TC], F32)
    ts_bf = p.sb(es_attn, "ts_bf", [64, TC], BF16)
    B_sbuf = p.sb(es_attn, "B_sbuf", [16, TC], F32)
    C_sbuf = p.sb(es_attn, "C_sbuf", [16, TC], F32)
    delta = p.sb(es_attn, "delta", [128, NIT, TC], F32)
    dh = p.sb(es_attn, "dh", [128, NIT, TC], F32)
    dA_b = [p.sb(es_attn, f"dA_b{i}", [128, NIT, TC + 1], F32)
            for i in range(2)]
    X_b = [p.sb(es_attn, f"X_b{i}", [128, NIT, TC + 1], F32)
           for i in range(2)]
    sout = p.sb(es_attn, "sout", [128, NIT, TC + 1], F32)
    tmpy = p.sb(es_attn, "tmpy", [128, NIT, TC], F32)
    y_acc = p.sb(es_attn, "y_acc", [128, NIT, TC], F32)
    ygate = p.sb(es_attn, "ygate", [128, NIT, TC], BF16)
    o_sb = cacc
    junk = conv_bf[:, 0:4, :].rearrange("p a t -> p (a t)")

    def g_init(g):
        g.iota(iotf[:], pattern=[[1, 128]], base=0, channel_multiplier=0,
               allow_small_or_imprecise_dtypes=True)
        g.iota(iotc[:], pattern=[[0, 1]], base=0, channel_multiplier=1,
               allow_small_or_imprecise_dtypes=True)
        g.iota(sel16[:], pattern=[[1, 16], [0, 128]], base=0, channel_multiplier=0,
               allow_small_or_imprecise_dtypes=True)
        g.memset(state[:], 0.0)
        for j, v in enumerate(CVALS):
            g.memset(cst[:, j:j + 1], v)
        return g.memset(stat[:], 0.0)
    p.emit("gpsimd", g_init, inc=s_g)
    G_INIT = p.bar(s_g)

    def v_init(v):
        v.tensor_scalar(out=ident[:], in0=iotf[:], scalar1=iotc[:], scalar2=None,
                        op0=ALU.is_equal)
        return v.tensor_scalar(out=sel16[:], in0=sel16[:], scalar1=iotc[0:16, :],
                               scalar2=None, op0=ALU.is_equal)
    p.emit("vector", v_init, deps=[G_INIT], inc=s_v)
    V_INIT = p.bar(s_v)


    g_dma(in_w_sb[:], in_w.rearrange("(a p) c -> p a c", p=128))
    g_dma(xproj_w_sb[:], xproj_w.rearrange("(a p) c -> p a c", p=128))
    g_dma(dt_w_sb[:], dt_w)
    g_dma(out_w_sb[:], out_w.rearrange("(a p) c -> p a c", p=128))
    s_dma(in_b_sb[:], in_b)
    s_dma(conv_b_sb[:], conv_b)
    s_dma(conv_w_sb[:], conv_w)
    s_dma(dt_b_sb[:], dt_b)
    s_dma(D_sb[:], Dvec)
    s_dma(ns_sb[:], norm_scale)
    s_dma(A_sb[:], A_log.rearrange("(a p) n -> p a n", p=128))
    W_INIT = p.bar(s_dw)

    def a_initA(a):
        ap = A_sb[:].rearrange("p a n -> p (a n)")
        return a.activation(out=ap, in_=ap, func=AF.Exp)
    p.emit("scalar", a_initA, deps=[W_INIT], inc=s_a)
    def v_negA(v):
        ap = A_sb[:].rearrange("p a n -> p (a n)")
        return v.tensor_scalar(out=ap, in0=ap, scalar1=-1.0, scalar2=None,
                               op0=ALU.mult)
    p.emit("vector", v_negA, deps=[p.bar(s_a)], inc=s_v, pad=True)
    A_READY = p.bar(s_v)

    def g_zeros(g):
        g.memset(dA_b[0][:, :, 0:1], 0.0)
        g.memset(dA_b[1][:, :, 0:1], 0.0)
        return g.memset(hid[:, :, 0:3], 0.0)
    p.emit("gpsimd", g_zeros, inc=s_g)
    HALO0 = p.bar(s_g)

    prev = {}
    for k in range(NCH):
        half = (k * TC) // TOKF
        tloc = k * TC - half * TOKF

        d0 = [prev["xn_v"], prev["sq_a"]] if k else []
        s_dma(xt[:], x[k * TC:(k + 1) * TC, :].rearrange("(a p) h -> p a h", p=128),
              inc=s_dx, deps=d0)
        XRDY = (s_dx, 16 * (k + 1))

        lnw = [prev["xnt_pe"]] if k else []
        if k:
            p.emit("vector", lambda v: v.engine_nop(), deps=[prev["xnt_pe"]],
                   inc=s_v)
        XN = emit_ln(xt, xnt, NSUB, True, XRDY, junk,
                     jdeps=[(s_do, 64 * k)] if k else [])
        prev["xn_v"] = XN
        prev["sq_a"] = p.bar(s_a)

        inproj_pe_guard = prev.get("inproj_pe")
        for ht in range(8):
            def mk_t(t, bank, ht=ht):
                last = None
                for s in range(NSUB):
                    last = t.transpose(bank[:, s * 128:(s + 1) * 128],
                                       xnt[:, s, ht * 128:(ht + 1) * 128], ident[:])
                return last
            pipeT.produce(mk_t, deps=[XN, V_INIT])
            evdeps = [W_INIT] + ([inproj_pe_guard] if inproj_pe_guard else [])
            def ev_t(v, bank, ht=ht):
                return v.tensor_scalar(out=xn_fm[:, ht, :], in0=bank[:, 0:TC],
                                       scalar1=ns_sb[:, ht:ht + 1], scalar2=None,
                                       op0=ALU.mult)
            pipeT.consume("vector", ev_t, deps=evdeps)
        XNFM = p.bar(pipeT.ev_done)
        prev["xnt_pe"] = (pipeT.pe_done, pipeT.uses)

        hide_dep = [prev["conv_v"]] if k else [HALO0]
        gate_dep = [prev["yg_v"]] if k else []
        hid_ct = {}
        for ct in range(24):
            def mk_mm(t, bank, ct=ct):
                last = None
                for ht in range(8):
                    last = t.matmul(bank[:, 0:TC],
                                    in_w_sb[:, ht, ct * 128:(ct + 1) * 128],
                                    xn_fm[:, ht, :], start=(ht == 0), stop=(ht == 7))
                return last
            pipeM.produce(mk_mm, deps=[XNFM, W_INIT])
            if ct < 16:
                def ev_h(a, bank, ct=ct):
                    return a.activation(out=hid[:, ct, 3:3 + TC], in_=bank[:, 0:TC],
                                        func=AF.Identity, bias=in_b_sb[:, ct:ct + 1])
                pipeM.consume("scalar", ev_h, deps=hide_dep + [W_INIT])
                hid_ct[ct] = (pipeM.ev_done, pipeM.uses)
            else:
                def ev_g(a, bank, ct=ct):
                    return a.activation(out=gate_s[:, ct - 16, :], in_=bank[:, 0:TC],
                                        func=AF.Silu, bias=in_b_sb[:, ct:ct + 1])
                pipeM.consume("scalar", ev_g, deps=gate_dep + [W_INIT])
        HID = p.bar(pipeM.ev_done)
        prev["inproj_pe"] = (pipeM.pe_done, pipeM.uses)

        silu_guard = [prev["xproj_pe"]] if k else []
        dh_guard = [prev["yd_v"]] if k else []
        silu_ct = {}
        for ct in range(16):
            pr = ct % 2
            def v_conv(v, ct=ct, pr=pr):
                v.tensor_scalar(out=cacc[:, pr, :], in0=hid[:, ct, 0:TC],
                                scalar1=conv_w_sb[:, ct, 0:1], scalar2=None,
                                op0=ALU.mult)
                last = None
                for kk in range(1, KC):
                    last = v.scalar_tensor_tensor(
                        out=cacc[:, pr, :], in0=hid[:, ct, kk:kk + TC],
                        scalar=conv_w_sb[:, ct, kk:kk + 1],
                        in1=cacc[:, pr, :], op0=ALU.mult, op1=ALU.add)
                return last
            wdeps = [hid_ct[ct], W_INIT]
            if k:
                wdeps.append((s_do, 64 * k))
            if ct >= 2:
                wdeps.append(silu_ct[ct - 2])
            p.emit("vector", v_conv, deps=wdeps, inc=s_v)
            CONVD = p.bar(s_v)
            def a_silu(a, ct=ct, pr=pr):
                last = a.activation(out=conv_bf[:, ct, :], in_=cacc[:, pr, :],
                                    func=AF.Silu, bias=conv_b_sb[:, ct:ct + 1])
                if ct < 8:
                    last = a.activation(out=conv_s[:, ct, :], in_=cacc[:, pr, :],
                                        func=AF.Silu, bias=conv_b_sb[:, ct:ct + 1])
                return last
            p.emit("scalar", a_silu,
                   deps=[CONVD, W_INIT] + silu_guard + (dh_guard if ct < 8 else []),
                   inc=s_a)
            silu_ct[ct] = p.bar(s_a)
        CONV = p.bar(s_a)
        def v_halo(v):
            return v.tensor_copy(hid[:, :, 0:3], hid[:, :, TC:TC + 3])
        p.emit("vector", v_halo, deps=[CONV, p.bar(s_v)], inc=s_v)
        prev["conv_v"] = p.bar(s_v)

        tsev_guard = [prev["dt_pe"]] if k else []
        bcev_guard = [prev["brep_pe"], prev["crep_pe"]] if k else []
        def mk_ts(t, bank):
            last = None
            for ct in range(16):
                last = t.matmul(bank[0:64, 0:TC], xproj_w_sb[:, ct, 0:64],
                                conv_bf[:, ct, :], start=(ct == 0), stop=(ct == 15))
            return last
        pipeM.produce(mk_ts, deps=[CONV])
        def ev_ts(v, bank):
            return v.tensor_copy(ts_bf[:], bank[0:64, 0:TC])
        pipeM.consume("vector", ev_ts, deps=tsev_guard)
        def mk_bc(t, bank):
            last = None
            for ct in range(16):
                t.matmul(bank[0:16, 0:TC], xproj_w_sb[:, ct, 64:80],
                         conv_bf[:, ct, :], start=(ct == 0), stop=(ct == 15))
            for ct in range(16):
                last = t.matmul(bank[32:48, 0:TC], xproj_w_sb[:, ct, 80:96],
                                conv_bf[:, ct, :], start=(ct == 0), stop=(ct == 15))
            return last
        pipeM.produce(mk_bc, deps=[CONV])
        def ev_bc(v, bank):
            v.tensor_copy(B_sbuf[:], bank[0:16, 0:TC])
            return v.tensor_copy(C_sbuf[:], bank[32:48, 0:TC])
        pipeM.consume("vector", ev_bc, deps=bcev_guard, pad=True)
        TSBC = p.bar(pipeM.ev_done)
        prev["xproj_pe"] = (pipeM.pe_done, pipeM.uses)

        dA_guard = [prev["dh_v"]] if k else []
        for it in range(NIT):
            def mk_dt(t, bank, it=it):
                return t.matmul(bank[:, 0:TC], dt_w_sb[:, it * 128:(it + 1) * 128],
                                ts_bf[:], start=True, stop=True)
            pipeM.produce(mk_dt, deps=[TSBC])
            def ev_dt(a, bank, it=it):
                a.activation(out=delta[:, it, :], in_=bank[:, 0:TC], func=AF.Exp,
                             bias=dt_b_sb[:, it:it + 1])
                return a.activation(out=delta[:, it, :], in_=delta[:, it, :],
                                    func=AF.Ln, bias=1.0)
            pipeM.consume("scalar", ev_dt, deps=[W_INIT, G_INIT] + dA_guard)
        DELTA = p.bar(pipeM.ev_done)
        prev["dt_pe"] = (pipeM.pe_done, pipeM.uses)

        for it in range(NIT):
            p.emit("vector", lambda v, it=it: v.tensor_tensor(
                out=dh[:, it, :], in0=delta[:, it, :], in1=conv_s[:, it, :],
                op=ALU.mult), deps=[DELTA, CONV], inc=s_v)
        DH = p.bar(s_v)
        prev["dh_v"] = DH

        scan_done = {}
        gx_cnt = {}
        gy_cnt = {}
        acc_cnt = {}
        cb_cnt = {}
        cc_cnt = {}
        for n in range(16):
            q = n % 2
            def mk_brep(t, bank, n=n):
                return t.matmul(bank[:, 0:TC], sel16[:, n * 128:(n + 1) * 128],
                                B_sbuf[:], start=True, stop=True)
            pipeB.produce(mk_brep, deps=[TSBC, V_INIT])
            bB = pipeB.banks[(pipeB.uses - 1) % 2]
            BRDY = (pipeB.pe_done, pipeB.uses)
            def mk_crep(t, bank, n=n):
                return t.matmul(bank[:, 0:TC], sel16[:, n * 128:(n + 1) * 128],
                                C_sbuf[:], start=True, stop=True)
            pipeC.produce(mk_crep, deps=[TSBC, V_INIT])
            bC = pipeC.banks[(pipeC.uses - 1) % 2]
            CRDY = (pipeC.pe_done, pipeC.uses)
            dadeps = [DELTA, A_READY, G_INIT]
            if n >= 2:
                dadeps.append(scan_done[n - 2])
            elif k:
                dadeps.append(prev["scan_m2"][n])
            for it in range(NIT):
                p.emit("scalar", lambda a, it=it, n=n, q=q: a.activation(
                    out=dA_b[q][:, it, 1:TC + 1], in_=delta[:, it, :],
                    func=AF.Exp, scale=A_sb[:, it, n:n + 1]),
                    deps=dadeps, inc=s_a)
            DA = p.bar(s_a)
            def v_x0(v, n=n, q=q):
                return v.tensor_copy(X_b[q][:, :, 0], state[:, :, n])
            p.emit("vector", v_x0, deps=[DH, G_INIT], inc=s_v)
            def v_X(v, bB=bB, q=q):
                return v.tensor_tensor(
                    out=X_b[q][:, :, 1:TC + 1], in0=dh[:],
                    in1=bB[:, 0:TC].rearrange("p (o t) -> p o t", o=1)
                    .broadcast_to([128, NIT, TC]), op=ALU.mult)
            p.emit("vector", v_X, deps=[BRDY], inc=pipeB.ev_done)
            def v_scan(v, q=q):
                return v.tensor_tensor_scan(
                    out=sout[:].rearrange("p a t -> p (a t)"),
                    data0=dA_b[q][:].rearrange("p a t -> p (a t)"),
                    data1=X_b[q][:].rearrange("p a t -> p (a t)"),
                    initial=0.0, op0=ALU.mult, op1=ALU.add)
            p.emit("vector", v_scan, deps=[DA], inc=s_v)
            scan_done[n] = p.bar(s_v)
            tgt = y_acc if n == 0 else tmpy
            ymdeps = [CRDY]
            if n >= 2:
                ymdeps.append((s_gp, acc_cnt[n - 1]))
            elif k:
                ymdeps.append(prev["acc_m2"][1])
            def v_ym(v, bC=bC, tgt=tgt):
                return v.tensor_tensor(
                    out=tgt[:], in0=sout[:, :, 1:TC + 1],
                    in1=bC[:, 0:TC].rearrange("p (o t) -> p o t", o=1)
                    .broadcast_to([128, NIT, TC]), op=ALU.mult)
            p.emit("vector", v_ym, deps=ymdeps, inc=pipeC.ev_done)
            YMV = (pipeC.ev_done, p.count(pipeC.ev_done))
            if n > 0:
                def g_ya(g):
                    return g.tensor_tensor(out=y_acc[:], in0=y_acc[:],
                                           in1=tmpy[:], op=ALU.add)
                p.emit("gpsimd", g_ya, deps=[YMV], inc=s_gp)
            acc_cnt[n] = p.count(s_gp)
            def v_st(v, n=n):
                return v.tensor_copy(state[:, :, n], sout[:, :, TC])
            p.emit("vector", v_st, inc=s_v)
        prev["scan_m2"] = [scan_done[14], scan_done[15]]
        prev["acc_m2"] = [(s_gp, acc_cnt[14]), (s_gp, acc_cnt[15])]
        prev["brep_pe"] = (pipeB.pe_done, pipeB.uses)
        prev["crep_pe"] = (pipeC.pe_done, pipeC.uses)
        YADD_G = (s_gp, acc_cnt[15])
        SCAN = p.bar(s_v)

        for it in range(NIT):
            def v_yd(v, it=it):
                return v.scalar_tensor_tensor(out=y_acc[:, it, :],
                                              in0=conv_s[:, it, :],
                                              scalar=D_sb[:, it:it + 1],
                                              in1=y_acc[:, it, :],
                                              op0=ALU.mult, op1=ALU.add)
            p.emit("vector", v_yd, deps=[SCAN, W_INIT, YADD_G], inc=s_v)
            ygdeps = [prev["op_pe"]] if k and it == 0 else []
            def v_yg(v, it=it):
                return v.tensor_tensor(out=ygate[:, it, :], in0=y_acc[:, it, :],
                                       in1=gate_s[:, it, :], op=ALU.mult)
            p.emit("vector", v_yg, deps=ygdeps, inc=s_v)
        YG = p.bar(s_v)
        prev["yd_v"] = YG
        prev["yg_v"] = YG

        for hg in range(4):
            oev_guard = [(s_do, 64 * k + 16 * hg)] if (k or hg) else []
            for ho in range(2 * hg, 2 * hg + 2):
                def mk_op(t, bank, ho=ho):
                    last = None
                    for it in range(NIT):
                        last = t.matmul(bank[:, 0:TC],
                                        out_w_sb[:, it, ho * 128:(ho + 1) * 128],
                                        ygate[:, it, :], start=(it == 0),
                                        stop=(it == 7))
                    return last
                pipeM.produce(mk_op, deps=[YG])
                def ev_op(a, bank, ho=ho):
                    return a.activation(out=o_sb[:, ho % 2, :], in_=bank[:, 0:TC],
                                        func=AF.Copy)
                pipeM.consume("scalar", ev_op, deps=oev_guard)
            OUTP = p.bar(pipeM.ev_done)
            def d_out(s_, half=half, tloc=tloc, hg=hg):
                return s_.dma_start(
                    out=cc_in.ap()[half * 1024 + hg * 256:
                                   half * 1024 + (hg + 1) * 256, tloc:tloc + TC]
                        .rearrange("(a p) t -> p a t", p=128),
                    in_=o_sb[:])
            p.emit("sync", d_out, deps=[OUTP], inc=s_do, inc_amt=16)
        prev["op_pe"] = (pipeM.pe_done, pipeM.uses)

        if DBG and k == 0:
            taps = [("stat", stat), ("xt", xt), ("xnt", xnt), ("xn_fm", xn_fm), ("hid", hid),
                    ("conv_bf", conv_bf), ("conv_s", conv_s), ("gate_s", gate_s),
                    ("ts_bf", ts_bf), ("B_sbuf", B_sbuf), ("C_sbuf", C_sbuf),
                    ("delta", delta), ("dh", dh), ("y_acc", y_acc),
                    ("ygate", ygate), ("o_sb", o_sb), ("state", state)]
            s_tap = p.sem("s_tap")
            alldeps = [p.bar(s_v), p.bar(s_a), (s_do, 64), p.bar(pipeM.ev_done),
                       p.bar(s_gp)]
            for nm, tile in taps:
                ap = tile[:]
                od = nc.dram_tensor("dbg_" + nm, list(ap.shape), ap.dtype,
                                    kind="ExternalOutput").ap()
                p.emit("sync", lambda s_, od=od, ap=ap: s_.dma_start(
                    out=od, in_=ap), deps=alldeps, inc="s_tap", inc_amt=16)
            TAPS = p.bar(s_tap)
            for eng in ("vector", "scalar", "gpsimd", "sync"):
                p.emit(eng, lambda en: None, deps=[TAPS])
            p.emit("tensor", lambda t: None, deps=[TAPS])

    ALLOUT = (s_do, 64 * NCH)

    if not dbg_ffn_input:
        def g_rs(g):
            return g.collective_compute(
                "ReduceScatter", ALU.add,
                replica_groups=[[0, 1], [2, 3], [4, 5], [6, 7]],
                ins=[cc_in.ap().opt()], outs=[rs_out.ap().opt()])
        p.emit("gpsimd", g_rs, deps=[ALLOUT], inc=s_cc)
    else:
        p.emit("gpsimd", lambda g: g.engine_nop(), deps=[ALLOUT], inc=s_cc)
    RS = p.bar(s_cc)

    PHB = [RS, ALLOUT, p.bar(s_v), p.bar(s_a), p.bar(pipeM.ev_done),
           p.bar(pipeT.ev_done), p.bar(pipeB.ev_done), p.bar(pipeC.ev_done)]
    es_attn.close()

    es_fp = contextlib.ExitStack()
    w_eT = p.sb(es_fp, "w_eT", [8, TOKF], F32)
    sel8 = p.sb(es_fp, "sel8", [8, 8 * 128], F32)
    b1_sb = p.sb(es_fp, "b1_sb", [128, 32, EE], F32)
    b2_sb = p.sb(es_fp, "b2_sb", [128, 8, EE], F32)
    w1_sb0 = p.sb(es_fp, "w1_sb0", [128, 8, FFD // 2], BF16)
    w2_sb0 = p.sb(es_fp, "w2_sb0", [128, 16, 1024], BF16)

    es_pro = contextlib.ExitStack()
    rw_sb = p.sb(es_pro, "rw_sb", [128, 8, EE], BF16)
    outb_sb = p.sb(es_pro, "outb_sb", [128, 1024], F32)
    obrow = p.sb(es_pro, "obrow", [1, 1024], F32)
    ones1 = p.sb(es_pro, "ones1", [1, 128], F32)
    lgt = p.sb(es_pro, "lgt", [128, 4, EE], F32)
    r8 = p.sb(es_pro, "r8", [128, 12, EE], F32)
    wsc = p.sb(es_pro, "wsc", [128, 24], F32)
    io0 = p.sb(es_pro, "io0", [128, 4, EE], F32)
    io1k = p.sb(es_pro, "io1k", [128, 4, EE], F32)
    wdense = p.sb(es_pro, "wdense", [128, 4, EE], F32)
    xrt = p.sb(es_pro, "xrt", [128, 4, 1024], F32)
    h2t = p.sb(es_pro, "h2t", [128, 4, 1024], F32)
    h2nt = p.sb(es_pro, "h2nt", [128, 4, 1024], F32)
    at_fm = p.sb(es_pro, "at_fm", [128, 8, 512], F32)
    hn_fm = p.sb(es_pro, "hn_fm", [128, 8, 512], BF16)
    junkf = p.sb(es_pro, "junkf", [128, 1024], BF16)

    g_dma(rw_sb[:], router_wT.rearrange("(a p) e -> p a e", p=128), deps=PHB)
    s_dma(b1_sb[:], b1, deps=PHB)
    s_dma(b2_sb[:], b2, deps=PHB)
    s_dma(obrow[:], out_b, deps=PHB)
    FW0 = p.bar(s_dw)

    def g_initf(g):
        g.memset(ones1[:], 1.0)
        g.iota(io0[:], pattern=[[0, 4], [1, 8]], base=0,
               channel_multiplier=0, allow_small_or_imprecise_dtypes=True)
        g.iota(io1k[:], pattern=[[0, 4], [1, 8]], base=1000,
               channel_multiplier=0, allow_small_or_imprecise_dtypes=True)
        return g.iota(sel8[:], pattern=[[1, 8], [0, 128]], base=0,
                      channel_multiplier=0, allow_small_or_imprecise_dtypes=True)
    p.emit("gpsimd", g_initf, deps=PHB, inc=s_g)
    GF = p.bar(s_g)
    def v_self8(v):
        return v.tensor_scalar(out=sel8[:], in0=sel8[:], scalar1=iotc[0:8, :],
                               scalar2=None, op0=ALU.is_equal)
    p.emit("vector", v_self8, deps=[GF] + PHB, inc=s_v)
    SEL8 = p.bar(s_v)

    for hhalf in range(2):
        def mk_ob(t, bank, hh=hhalf):
            return t.matmul(bank[:, 0:512], ones1[:],
                            obrow[0:1, hh * 512:(hh + 1) * 512],
                            start=True, stop=True)
        pipeM.produce(mk_ob, deps=[FW0, GF] + PHB)
        def ev_ob(v, bank, hh=hhalf):
            return v.tensor_copy(outb_sb[:, hh * 512:(hh + 1) * 512],
                                 bank[:, 0:512])
        pipeM.consume("vector", ev_ob, deps=PHB)
    OBREP = p.bar(pipeM.ev_done)

    fprev = {}
    for sl in range(NSL):
        c0 = sl * 512
        xdep = [fprev["h2_v"]] if sl else PHB
        s_dma(xrt[:], xr[c0:c0 + 512, :].rearrange("(a p) h -> p a h", p=128),
              inc=s_dx, deps=xdep)
        adep = [fprev["at_pe"]] if sl else PHB
        s_dma(at_fm[:], rs_out.ap()[:, c0:c0 + 512]
              .rearrange("(a p) t -> p a t", p=128), inc=s_dx, deps=adep + [RS])
        XA = (s_dx, p.count(s_dx))

        for s in range(4):
            p.emit("vector", lambda v, s=s: v.tensor_tensor(
                out=xrt[:, s, :], in0=xrt[:, s, :], in1=outb_sb[:], op=ALU.add),
                deps=[XA, OBREP], inc=s_v)
        XB = p.bar(s_v)

        for tt in range(4):
            for hh in range(2):
                def mk_at(t, bank, tt=tt, hh=hh):
                    last = None
                    for ho in range(4 * hh, 4 * hh + 4):
                        last = t.transpose(
                            bank[:, (ho - 4 * hh) * 128:(ho - 4 * hh + 1) * 128],
                            at_fm[:, ho, tt * 128:(tt + 1) * 128], ident[:])
                    return last
                pipeT.produce(mk_at, deps=[XA, V_INIT] + (PHB if sl == 0 else []))
                def ev_at(v, bank, tt=tt, hh=hh):
                    return v.tensor_tensor(out=h2t[:, tt, hh * 512:(hh + 1) * 512],
                                           in0=bank[:, 0:512],
                                           in1=xrt[:, tt, hh * 512:(hh + 1) * 512],
                                           op=ALU.add)
                h2t_guard = []
                if "h2_dma" in fprev:
                    h2t_guard = [fprev["h2_dma"], fprev["sq2"]]
                pipeT.consume("vector", ev_at, deps=[XB] + h2t_guard)
        fprev["at_pe"] = (pipeT.pe_done, pipeT.uses)
        H2 = p.bar(pipeT.ev_done)

        s_dma(h2_dram.ap()[c0:c0 + 512, :].rearrange("(a p) h -> p a h", p=128),
              h2t[:], inc=s_dh2, deps=[H2])
        fprev["h2_dma"] = (s_dh2, p.count(s_dh2))

        if sl:
            p.emit("vector", lambda v: v.engine_nop(),
                   deps=[fprev["hnt_pe"]], inc=s_v)
        H2N = emit_ln(h2t, h2nt, 4, False, H2, junkf[:])
        fprev["h2_v"] = H2N
        fprev["sq2"] = p.bar(s_a)

        for ho in range(8):
            def mk_hn(t, bank, ho=ho):
                last = None
                for tt in range(4):
                    last = t.transpose(bank[:, tt * 128:(tt + 1) * 128],
                                       h2nt[:, tt, ho * 128:(ho + 1) * 128],
                                       ident[:])
                return last
            pipeT.produce(mk_hn, deps=[H2N])
            fprev["hnt_pe"] = (pipeT.pe_done, pipeT.uses)
            def ev_hn(v, bank, ho=ho):
                return v.tensor_copy(hn_fm[:, ho, :], bank[:, 0:512])
            hn_guard = []
            if "r_pe" in fprev:
                hn_guard = [fprev["r_pe"], fprev["hn_dma"]]
            pipeT.consume("vector", ev_hn, deps=hn_guard)
        HNF = p.bar(pipeT.ev_done)
        s_dma(h2n_dram.ap()[:, c0:c0 + 512].rearrange("(a p) t -> p a t", p=128),
              hn_fm[:], inc=s_dh2, deps=[HNF])
        fprev["hn_dma"] = (s_dh2, p.count(s_dh2))

        def mk_r(t, bank):
            last = None
            for tt in range(4):
                for ht in range(8):
                    last = t.matmul(bank[:, tt * 8:(tt + 1) * 8],
                                    hn_fm[:, ht, tt * 128:(tt + 1) * 128],
                                    rw_sb[:, ht, :], start=(ht == 0),
                                    stop=(ht == 7))
            return last
        pipeM.produce(mk_r, deps=[HNF, FW0])
        fprev["r_pe"] = (pipeM.pe_done, pipeM.uses)
        def ev_r(v, bank):
            return v.tensor_copy(lgt[:].rearrange("p a e -> p (a e)"),
                                 bank[:, 0:32])
        pipeM.consume("vector", ev_r,
                      deps=[fprev["wd_pe"]] if "wd_pe" in fprev else [],
                      pad=True)
        LG = p.bar(pipeM.ev_done)

        def bc(ap):
            return ap.broadcast_to([128, 4, EE])
        lg3 = lgt[:]
        m1 = wsc[:, 0:4].rearrange("p (a o) -> p a o", o=1)
        ix1 = wsc[:, 4:8].rearrange("p (a o) -> p a o", o=1)
        m2 = wsc[:, 8:12].rearrange("p (a o) -> p a o", o=1)
        ix2 = wsc[:, 12:16].rearrange("p (a o) -> p a o", o=1)
        w1c = wsc[:, 16:20].rearrange("p (a o) -> p a o", o=1)
        w2c = wsc[:, 20:24].rearrange("p (a o) -> p a o", o=1)
        r8a = r8[:, 0:4, :]
        r8b = r8[:, 4:8, :]
        r8c = r8[:, 8:12, :]
        RV = [LG, GF]
        def step(fn, eng="vector", deps=()):
            p.emit(eng, fn, deps=list(deps) + RV, inc=s_v if eng == "vector"
                   else s_a, pad=True)
        step(lambda v: v.tensor_reduce(m1, lg3, axis=AX.X, op=ALU.max))
        step(lambda v: v.tensor_tensor(out=r8a, in0=lg3, in1=bc(m1),
                                       op=ALU.is_equal))
        step(lambda v: v.scalar_tensor_tensor(out=r8b, in0=r8a, scalar=-1000.0,
                                              in1=io1k[:], op0=ALU.mult,
                                              op1=ALU.add))
        step(lambda v: v.tensor_reduce(ix1, r8b, axis=AX.X, op=ALU.min))
        step(lambda v: v.tensor_tensor(out=r8a, in0=io0[:], in1=bc(ix1),
                                       op=ALU.is_equal))
        step(lambda v: v.scalar_tensor_tensor(out=r8b, in0=r8a, scalar=-1e30,
                                              in1=lg3, op0=ALU.mult,
                                              op1=ALU.add))
        step(lambda v: v.tensor_reduce(m2, r8b, axis=AX.X, op=ALU.max))
        step(lambda v: v.tensor_tensor(out=r8c, in0=r8b, in1=bc(m2),
                                       op=ALU.is_equal))
        step(lambda v: v.scalar_tensor_tensor(out=r8b, in0=r8c, scalar=-1000.0,
                                              in1=io1k[:], op0=ALU.mult,
                                              op1=ALU.add))
        step(lambda v: v.tensor_reduce(ix2, r8b, axis=AX.X, op=ALU.min))
        step(lambda v: v.tensor_tensor(out=r8c, in0=io0[:], in1=bc(ix2),
                                       op=ALU.is_equal))
        step(lambda v: v.tensor_tensor(out=w1c, in0=m1, in1=m2,
                                       op=ALU.subtract))
        W1D = p.bar(s_v)
        def a_w1(a):
            a.activation(out=w1c, in_=w1c, func=AF.Tanh, scale=0.5)
            return a.activation(out=w1c, in_=w1c, func=AF.Identity, scale=0.5,
                                bias=0.5)
        step(a_w1, eng="scalar", deps=[W1D, G_INIT])
        W1S = p.bar(s_a)
        step(lambda v: v.tensor_scalar(out=w2c, in0=w1c, scalar1=-1.0,
                                       scalar2=1.0, op0=ALU.mult, op1=ALU.add),
             deps=[W1S])
        step(lambda v: v.tensor_tensor(out=r8a, in0=r8a, in1=bc(w1c),
                                       op=ALU.mult))
        step(lambda v: v.tensor_tensor(out=r8c, in0=r8c, in1=bc(w2c),
                                       op=ALU.mult))
        step(lambda v: v.tensor_tensor(out=wdense[:], in0=r8a, in1=r8c,
                                       op=ALU.add))
        WD = p.bar(s_v)

        def mk_wt(t, bank):
            last = None
            for tt in range(4):
                last = t.transpose(bank[0:8, tt * 128:(tt + 1) * 128],
                                   wdense[:, tt, :], ident[:])
            return last
        pipeM.produce(mk_wt, deps=[WD])
        def ev_wt(v, bank, sl=sl):
            return v.tensor_copy(w_eT[:, sl * 512:(sl + 1) * 512], bank[0:8, 0:512])
        pipeM.consume("vector", ev_wt)
        fprev["wd_pe"] = (pipeM.pe_done, pipeM.uses)
    fprev["hn_pe"] = (pipeT.pe_done, pipeT.uses)
    PRO = [p.bar(pipeM.ev_done), p.bar(s_dh2), p.bar(s_v), p.bar(s_a),
           (pipeM.pe_done, pipeM.uses), (pipeT.pe_done, pipeT.uses),
           p.bar(pipeT.ev_done), p.bar(s_dx), p.bar(s_g)]
    WET = p.bar(pipeM.ev_done)
    es_pro.close()
    es_exp = contextlib.ExitStack()
    s_ac2 = p.sem("s_ac2")
    w1_sb = [w1_sb0, p.sb(es_exp, "w1_sb1", [128, 8, FFD // 2], BF16)]
    w2_sb = [w2_sb0, p.sb(es_exp, "w2_sb1", [128, 16, 1024], BF16)]
    h1_sb = p.sb(es_exp, "h1_sb", [128, 16, 512], BF16)
    h2n_sb = [p.sb(es_exp, f"h2n_sb{i}", [128, 8, 512], BF16)
              for i in range(2)]
    tmp_sb = p.sb(es_exp, "tmp_sb", [128, 8, 512], F32)
    wrep_sb = [p.sb(es_exp, f"wrep_sb{i}", [128, 512], F32) for i in range(2)]

    NU = 2 * EE

    def emit_wload(u, deps):
        e, fh = u // 2, u % 2
        pu = u % 2
        g_dma(w1_sb[pu][:],
              w1T[e][:, fh * (FFD // 2):(fh + 1) * (FFD // 2)]
              .rearrange("(a p) f -> p a f", p=128), inc="s_dwf", deps=deps)
        g_dma(w2_sb[pu][:],
              w2T[e][fh * (FFD // 2):(fh + 1) * (FFD // 2), :]
              .rearrange("(a p) h -> p a h", p=128), inc="s_dwf", deps=deps)

    def emit_hnload(u, tc, deps):
        s_dma(h2n_sb[tc % 2][:],
              h2n_dram.ap()[:, tc * 512:(tc + 1) * 512]
              .rearrange("(a p) t -> p a t", p=128),
              inc="s_dhn", deps=deps)

    emit_wload(0, PHB)
    emit_wload(1, PRO)
    emit_hnload(0, 0, PRO + [p.bar(s_dh2)])
    nacc = [0, 0]
    wlast = {}
    for u in range(NU):
        e, fh, pu = u // 2, u % 2, u % 2
        WF = (s_dwf, 32 * (u + 1))
        for tc in range(NSL):
            if not (u == NU - 1 and tc == NSL - 1):
                ntc = (tc + 1) % NSL
                hdep = [fprev["h1_pe"]] if "h1_pe" in fprev else PRO
                emit_hnload(u, ntc, hdep)
            def mk_wr(t, bank, e=e, tc=tc):
                return t.matmul(bank[:, 0:512], sel8[:, e * 128:(e + 1) * 128],
                                w_eT[:, tc * 512:(tc + 1) * 512],
                                start=True, stop=True)
            pipeC.produce(mk_wr, deps=[WET, SEL8])
            def ev_wr(v, bank, tc=tc):
                return v.tensor_copy(wrep_sb[tc % 2][:], bank[:, 0:512])
            pipeC.consume("vector", ev_wr,
                          deps=[fprev["wr_used"]] if "wr_used" in fprev else [])
            WR = p.bar(pipeC.ev_done)
            HN = (s_dhn, 16 * (u * NSL + tc + 1))

            for ft in range(16):
                def mk_h1(t, bank, ft=ft, pu=pu, tc=tc):
                    last = None
                    for ht in range(8):
                        last = t.matmul(bank[:, 0:512],
                                        w1_sb[pu][:, ht, ft * 128:(ft + 1) * 128],
                                        h2n_sb[tc % 2][:, ht, :],
                                        start=(ht == 0), stop=(ht == 7))
                    return last
                pipeM.produce(mk_h1, deps=[WF, HN])
                def ev_h1(a, bank, ft=ft, e=e, fh=fh):
                    return a.activation(out=h1_sb[:, ft, :], in_=bank[:, 0:512],
                                        func=AF.Gelu,
                                        bias=b1_sb[:, fh * 16 + ft, e:e + 1])
                pipeM.consume("scalar", ev_h1,
                              deps=([fprev["o2_pe"]] if "o2_pe" in fprev else [])
                              + [FW0])
            H1 = p.bar(pipeM.ev_done)
            fprev["h1_pe"] = (pipeM.pe_done, pipeM.uses)

            op = None if u == 0 else ALU.add
            for hg in range(2):
                sacc = s_acc if hg == 0 else s_ac2
                for ho in range(hg * 4, hg * 4 + 4):
                    def mk_o2(t, bank, ho=ho, pu=pu):
                        last = None
                        for ft in range(16):
                            last = t.matmul(bank[:, 0:512],
                                            w2_sb[pu][:, ft, ho * 128:(ho + 1) * 128],
                                            h1_sb[:, ft, :], start=(ft == 0),
                                            stop=(ft == 15))
                        return last
                    pipeB.produce(mk_o2, deps=[H1, WF])
                    tmpdep = [(sacc, nacc[hg] * 16)]
                    if fh == 0:
                        def ev_o2(v, bank, ho=ho, tc=tc):
                            return v.tensor_tensor(
                                out=tmp_sb[:, ho, :], in0=bank[:, 0:512],
                                in1=wrep_sb[tc % 2][:], op=ALU.mult)
                    else:
                        def ev_o2(v, bank, ho=ho, e=e, tc=tc):
                            return v.scalar_tensor_tensor(
                                out=tmp_sb[:, ho, :], in0=bank[:, 0:512],
                                scalar=b2_sb[:, ho, e:e + 1],
                                in1=wrep_sb[tc % 2][:],
                                op0=ALU.add, op1=ALU.mult)
                    pipeB.consume("vector", ev_o2, deps=tmpdep + [WR, FW0])
                O2 = p.bar(pipeB.ev_done)
                def d_acc(g, tc=tc, op=op, hg=hg):
                    kw = {} if op is None else dict(accum_op=op)
                    return g.dma_start(
                        out=acc_fm.ap()[hg * 512:(hg + 1) * 512,
                                        tc * 512:(tc + 1) * 512]
                        .rearrange("(a p) t -> p a t", p=128),
                        in_=tmp_sb[:, hg * 4:hg * 4 + 4, :], **kw)
                p.emit("gpsimd", d_acc, deps=[O2, (sacc, nacc[hg] * 16)],
                       inc="s_acc" if hg == 0 else "s_ac2", inc_amt=16)
                nacc[hg] += 1
            fprev["o2_pe"] = (pipeB.pe_done, pipeB.uses)
            fprev["wr_used"] = p.bar(pipeB.ev_done)
        wlast[pu] = [(pipeB.pe_done, pipeB.uses), (pipeM.pe_done, pipeM.uses)]
        if u + 2 < NU:
            emit_wload(u + 2, wlast[pu])

    ACC = [(s_acc, nacc[0] * 16), (s_ac2, nacc[1] * 16)]
    EXP = ACC + [p.bar(pipeB.ev_done), p.bar(pipeC.ev_done), p.bar(s_v),
                 p.bar(s_a), (pipeB.pe_done, pipeB.uses),
                 (pipeM.pe_done, pipeM.uses), p.bar(s_dwf), p.bar(s_dhn)]
    es_exp.close()
    es_epi = contextlib.ExitStack()
    at2 = p.sb(es_epi, "at2", [128, 8, 512], F32)
    ht2 = p.sb(es_epi, "ht2", [128, 4, 1024], F32)
    ot_sb = p.sb(es_epi, "ot_sb", [128, 4, 1024], F32)

    for sl in range(NSL):
        c0 = sl * 512
        adep = [fprev["ep_pe"]] if "ep_pe" in fprev else EXP
        s_dma(at2[:], acc_fm.ap()[:, c0:c0 + 512]
              .rearrange("(a p) t -> p a t", p=128), inc=s_dx,
              deps=adep + ACC)
        hdep = [fprev["ep_v"]] if "ep_v" in fprev else EXP
        s_dma(ht2[:], h2_dram.ap()[c0:c0 + 512, :]
              .rearrange("(a p) h -> p a h", p=128), inc=s_dx,
              deps=hdep + [p.bar(s_dh2)])
        EXA = (s_dx, p.count(s_dx))
        for tt in range(4):
            for hh in range(2):
                def mk_ep(t, bank, tt=tt, hh=hh):
                    last = None
                    for ho in range(4 * hh, 4 * hh + 4):
                        last = t.transpose(
                            bank[:, (ho - 4 * hh) * 128:(ho - 4 * hh + 1) * 128],
                            at2[:, ho, tt * 128:(tt + 1) * 128], ident[:])
                    return last
                pipeT.produce(mk_ep, deps=[EXA] + (EXP if sl == 0 else []))
                def ev_ep(v, bank, tt=tt, hh=hh):
                    return v.tensor_tensor(out=ot_sb[:, tt, hh * 512:(hh + 1) * 512],
                                           in0=bank[:, 0:512],
                                           in1=ht2[:, tt, hh * 512:(hh + 1) * 512],
                                           op=ALU.add)
                pipeT.consume("vector", ev_ep,
                              deps=[EXA, (s_out, 16 * sl)] if sl else [EXA])
        fprev["ep_pe"] = (pipeT.pe_done, pipeT.uses)
        fprev["ep_v"] = p.bar(pipeT.ev_done)
        EOUT = p.bar(pipeT.ev_done)
        s_dma(out_hbm[c0:c0 + 512, :].rearrange("(a p) h -> p a h", p=128),
              ot_sb[:], inc="s_out", deps=[EOUT])

    if DBG:
        for nm, dr in [("cc_in", cc_in), ("rs_out", rs_out), ("h2", h2_dram),
                       ("h2n", h2n_dram), ("acc", acc_fm)]:
            if dbg_ffn_input and nm == "rs_out":
                continue
            ap = dr.ap()
            od = nc.dram_tensor("dbgd_" + nm, list(ap.shape), ap.dtype,
                                kind="ExternalOutput").ap()
            p.emit("sync", lambda s_, od=od, ap=ap: s_.dma_start(out=od, in_=ap),
                   deps=[(s_out, 16 * NSL)], inc="s_out", inc_amt=16)
        p.emit("sync", lambda s_: s_.nop(),
               deps=[("s_out", p.count("s_out"))])

    p.emit("sync", lambda s_: s_.nop(), deps=[(s_out, 16 * NSL)])
    p.emit("gpsimd", lambda g: g.engine_nop(), deps=[(s_out, 16 * NSL)])

    es_epi.close()
    es_fp.close()
    return p


def core_inputs(inputs, c):
    b, half = c // 2, c % 2
    f = np.ascontiguousarray
    own = slice(half * 1024, (half + 1) * 1024)
    oth = slice((1 - half) * 1024, (2 - half) * 1024)
    perm = np.r_[np.arange(own.start, own.stop), np.arange(oth.start, oth.stop)]
    in_w = inputs["L0_attn_in_w"]
    d = {
        "x": f(inputs["L0_attn_x"][b]),
        "xr": f(inputs["L0_attn_x"][b][half * 4096:(half + 1) * 4096]),
        "in_w": f(np.concatenate([in_w[:, perm], in_w[:, 2048 + own.start:2048 + own.stop]], 1)),
        "in_b": col128(np.concatenate([inputs["L0_attn_in_b"][perm],
                                       inputs["L0_attn_in_b"][2048:][own]])),
        "conv_w": f(inputs["L0_attn_conv_w"][:, perm].T.reshape(16, 128, 4)
                    .transpose(1, 0, 2)),
        "conv_b": col128(inputs["L0_attn_conv_b"][perm]),
        "xproj_w": f(inputs["L0_attn_xproj_w"][perm]),
        "dt_w": f(inputs["L0_attn_dt_w"][:, own]),
        "dt_b": col128(inputs["L0_attn_dt_b"][own]),
        "A_log": f(inputs["L0_attn_A_log"][own]),
        "D": col128(inputs["L0_attn_D"][own]),
        "out_w": f(inputs["L0_attn_out_w"][own]),
        "out_b": f(inputs["L0_attn_out_b"][None]),
        "norm_scale": col128(inputs["L0_attn_norm_scale"]),
        "router_wT": f(inputs["L0_ffn_router_w"].T),
        "w1T": f(inputs["L0_ffn_w1"].transpose(0, 2, 1)),
        "b1": f(inputs["L0_ffn_b1"].T.reshape(32, 128, 8).transpose(1, 0, 2)),
        "w2T": f(inputs["L0_ffn_w2"].transpose(0, 2, 1)),
        "b2": f(inputs["L0_ffn_b2"].T.reshape(8, 128, 8).transpose(1, 0, 2)),
    }
    return d


def col128(v):
    v = np.asarray(v)
    return np.ascontiguousarray(v.reshape(-1, 128).T)


_CACHE = {}


def kernel(**inputs):
    from concourse.bass_utils import run_bass_kernel_spmd
    if "nc" not in _CACHE:
        p = build_program(T=8192)
        _CACHE["nc"] = p.build()
    nc = _CACHE["nc"]
    in_maps = [core_inputs(inputs, c) for c in range(8)]
    res = run_bass_kernel_spmd(nc, in_maps, core_ids=list(range(8)))
    out = np.zeros((4, 8192, 1024), np.float32)
    for c in range(8):
        b, half = c // 2, c % 2
        out[b, half * 4096:(half + 1) * 4096] = res.results[c]["out"]
    return out


# version 14
# speedup vs baseline: 1.3170x; 1.0910x over previous
import contextlib
import numpy as np
import concourse.bass as bass
import concourse.mybir as mybir

F32 = mybir.dt.float32
BF16 = mybir.dt.bfloat16
AF = mybir.ActivationFunctionType
ALU = mybir.AluOpType
AX = mybir.AxisListType

ENGINES = ("sync", "scalar", "vector", "gpsimd", "tensor")


class Prog:
    def __init__(self):
        self.nc = bass.Bass("TRN2", target_bir_lowering=False, debug=False)
        self.es = contextlib.ExitStack()
        self.sched = {e: [] for e in ENGINES}
        self.waited = {e: {} for e in ENGINES}
        self.sems = {}
        self.boot = self.es.enter_context(self.nc.semaphore("boot"))
        self.jka = self.es.enter_context(self.nc.sbuf_tensor("jka", [128, 64], mybir.dt.float32))
        self.jkb = self.es.enter_context(self.nc.sbuf_tensor("jkb", [128, 64], mybir.dt.float32))

    def sem(self, name):
        s = self.es.enter_context(self.nc.semaphore(name))
        self.sems[name] = [s, 0]
        return name

    def sb(self, stack, name, shape, dtype):
        return stack.enter_context(self.nc.sbuf_tensor(name, list(shape), dtype))

    def ps(self, name, shape, dtype=F32):
        return self.es.enter_context(self.nc.psum_tensor(name, list(shape), dtype))

    def junk(self, eng, engine):
        if eng == "vector":
            return engine.tensor_copy(self.jkb[:], self.jka[:])
        if eng == "scalar":
            return engine.activation(out=self.jkb[:], in_=self.jka[:],
                                     func=AF.Identity)
        if eng == "gpsimd":
            return engine.tensor_copy(self.jkb[:], self.jka[:])
        raise ValueError(eng)

    def emit(self, eng, fn, deps=(), inc=None, inc_amt=1, pad=False):
        deps = [(s, c) for (s, c) in deps if c > self.waited[eng].get(s, 0)]
        for s, c in deps:
            self.waited[eng][s] = c
        if inc is not None:
            self.sems[inc][1] += inc_amt
        def run(engine, deps=deps, fn=fn, inc=inc, inc_amt=inc_amt, pad=pad):
            for s, c in deps:
                engine.wait_ge(self.sems[s][0], c)
            ret = fn(engine)
            if pad:
                ret = self.junk(eng, engine)
            if inc is not None:
                assert ret is not None
                ret.then_inc(self.sems[inc][0], inc_amt)
        self.sched[eng].append(run)

    def count(self, sem):
        return self.sems[sem][1]

    def bar(self, sem):
        return (sem, self.sems[sem][1])

    def build(self):
        nums = sorted([h.num for h, _ in self.sems.values()] + [self.boot.num])
        lo, hi = min(nums), max(nums) + 1
        rng = range(lo, hi)
        finals = [(h, c) for (h, c) in self.sems.values()]
        block = self.es.enter_context(self.nc.Block())
        for eng in ENGINES:
            def body(engine, eng=eng, rng=rng, finals=finals):
                if eng == "gpsimd":
                    engine.dma_reset(rng)
                    engine.sem_clear(rng)
                    engine.engine_nop().then_inc(self.boot, 1)
                else:
                    engine.wait_ge(self.boot, 1)
                for fn in self.sched[eng]:
                    fn(engine)
                if eng == "gpsimd":
                    engine.wait_ge(self.boot, 5)
                    for h, c in finals:
                        if c > 0:
                            engine.wait_ge(h, c)
                    engine.dma_reset(rng)
                    engine.sem_clear(rng)
                else:
                    engine.nop().then_inc(self.boot, 1)
            getattr(block, eng)(body)
        return self.nc


class BankPipe:

    def __init__(self, p, name, shape=(128, 512), n=2, dtype=F32):
        self.p = p
        self.banks = [p.ps(f"{name}{i}", shape, dtype) for i in range(n)]
        self.pe_done = p.sem(f"{name}_pd")
        self.ev_done = p.sem(f"{name}_ed")
        self.n = n
        self.uses = 0

    def produce(self, fn, deps=()):
        i = self.uses % self.n
        bank = self.banks[i]
        d = list(deps)
        if self.uses >= self.n:
            d.append((self.ev_done, self.uses - self.n + 1))
        self.uses += 1
        self.p.emit("tensor", lambda t, b=bank, fn=fn: fn(t, b), deps=d,
                    inc=self.pe_done)
        return bank

    def consume(self, eng, fn, deps=(), pad=False):
        i = (self.uses - 1) % self.n
        bank = self.banks[i]
        d = list(deps) + [(self.pe_done, self.uses)]
        self.p.emit(eng, lambda e, b=bank, fn=fn: fn(e, b), deps=d,
                    inc=self.ev_done, pad=pad)


def build_program(dbg_ffn_input=False, dbg_taps=False, T=8192):
    DBG = dbg_taps
    """Returns (nc, names). If dbg_ffn_input, rs_out is an ExternalInput."""
    p = Prog()
    nc = p.nc

    TC = 256
    NCH = T // TC
    NSUB = TC // 128
    IC, IF, NST, KC = 1024, 2048, 16, 4
    NIT = IC // 128
    NCT = IF // 128
    TOKF = T // 2
    NSL = TOKF // 512
    EPS_LN, EPS_RMS = 1e-5, 1e-6
    EE, FFD = 8, 4096

    def din(name, shape, dtype=F32):
        return nc.dram_tensor(name, list(shape), dtype, kind="ExternalInput").ap()

    x = din("x", [T, 1024])
    xr = din("xr", [TOKF, 1024])
    in_w = din("in_w", [1024, IF + IC])
    in_b = din("in_b", [128, 24])
    conv_w = din("conv_w", [128, 16, KC])
    conv_b = din("conv_b", [128, 16])
    xproj_w = din("xproj_w", [IF, 96])
    dt_w = din("dt_w", [64, IC])
    dt_b = din("dt_b", [128, 8])
    A_log = din("A_log", [IC, NST])
    Dvec = din("D", [128, 8])
    out_w = din("out_w", [IC, 1024])
    out_b = din("out_b", [1, 1024])
    norm_scale = din("norm_scale", [128, 8])
    router_wT = din("router_wT", [1024, EE])
    w1T = din("w1T", [EE, 1024, FFD])
    b1 = din("b1", [128, 32, EE])
    w2T = din("w2T", [EE, FFD, 1024])
    b2 = din("b2", [128, 8, EE])

    out_hbm = nc.dram_tensor("out", [TOKF, 1024], F32, kind="ExternalOutput").ap()

    cc_in = nc.dram_tensor("cc_in", [2048, TOKF], F32)
    if dbg_ffn_input:
        rs_out = nc.dram_tensor("rs_out", [1024, TOKF], F32, kind="ExternalInput")
    else:
        rs_out = nc.dram_tensor("rs_out", [1024, TOKF], F32)
    h2n_dram = nc.dram_tensor("h2n_dram", [1024, TOKF], BF16)
    h2_dram = nc.dram_tensor("h2_dram", [TOKF, 1024], F32)
    acc_fm = nc.dram_tensor("acc_fm", [1024, TOKF], F32)

    s_dx = p.sem("s_dx")
    s_dw = p.sem("s_dw")
    s_v = p.sem("s_v")
    s_a = p.sem("s_a")
    s_g = p.sem("s_g")
    s_do = p.sem("s_do")
    s_cc = p.sem("s_cc")
    s_dh2 = p.sem("s_dh2")
    s_dhn = p.sem("s_dhn")
    s_acc = p.sem("s_acc")
    s_dwf = p.sem("s_dwf")
    s_gp = p.sem("s_gp")
    s_out = p.sem("s_out")

    pipeT = BankPipe(p, "pT", n=2)
    pipeM = BankPipe(p, "pM", n=2)
    pipeB = BankPipe(p, "pB", n=2)
    pipeC = BankPipe(p, "pC", n=2)

    ident = p.sb(p.es, "ident", [128, 128], F32)
    iotf = p.sb(p.es, "iotf", [128, 128], F32)
    iotc = p.sb(p.es, "iotc", [128, 1], F32)
    stat = p.sb(p.es, "stat", [128, 4, 8], F32)
    CVALS = [EPS_LN, EPS_RMS, 1.0, -0.5, 0.5, 1.0 / 1024,
             (1.0 + EPS_RMS) / 1024, 1e-11]
    cst = p.sb(p.es, "cst", [128, len(CVALS)], F32)
    for j, v in enumerate(CVALS):
        nc.const_aps.register(F32, v, cst[:, j:j + 1])

    def g_dma(dst, src, inc="s_dw", deps=()):
        p.emit("gpsimd", lambda g, d=dst, s=src: g.dma_start(out=d, in_=s),
               deps=deps, inc=inc, inc_amt=16)

    def s_dma(dst, src, inc="s_dw", deps=()):
        p.emit("sync", lambda s_, d=dst, s=src: s_.dma_start(out=d, in_=s),
               deps=deps, inc=inc, inc_amt=16)

    def emit_ln(xin, xout, ns, rms, xdep, junk, jdeps=()):
        st_ = lambda c: stat[:, 0:ns, c]
        for s in range(ns):
            p.emit("vector", lambda v, s=s: v.reduce_sum(
                stat[:, s, 0:1], xin[:, s, :], axis=AX.X),
                deps=[xdep, G_INIT], inc=s_v, pad=(s == ns - 1))
        SM = p.bar(s_v)
        for s in range(ns):
            p.emit("scalar", lambda a, s=s, junk=junk: a.activation(
                out=junk, in_=xin[:, s, :], func=AF.Square,
                accum_out=stat[:, s, 1:2]), deps=[xdep] + list(jdeps),
                inc=s_a, pad=(s == ns - 1))
        SQ = p.bar(s_a)
        def v_u(v):
            v.tensor_tensor(out=st_(3), in0=st_(0), in1=st_(0), op=ALU.mult)
            v.tensor_scalar(out=st_(2), in0=st_(0), scalar1=1.0 / 1024,
                            scalar2=None, op0=ALU.mult)
            return None
        p.emit("vector", v_u, deps=[SM])
        p.emit("vector", lambda v: p.junk("vector", v))
        def v_w(v):
            return v.scalar_tensor_tensor(out=st_(4), in0=st_(3),
                                          scalar=-1.0 / 1024, in1=st_(1),
                                          op0=ALU.mult, op1=ALU.add)
        p.emit("vector", v_w, deps=[SQ], inc=s_v, pad=True)
        VW = p.bar(s_v)
        if rms:
            sc, bi = (1.0 + EPS_RMS) / 1024, 1e-11
        else:
            sc, bi = 1.0 / 1024, EPS_LN
        def a_sc(a, sc=sc, bi=bi):
            a.activation(out=st_(5), in_=st_(4), func=AF.Ln, scale=sc, bias=bi)
            return a.activation(out=st_(5), in_=st_(5), func=AF.Exp, scale=-0.5)
        p.emit("scalar", a_sc, deps=[VW, G_INIT], inc=s_a, pad=True)
        SCL = p.bar(s_a)
        for s in range(ns):
            p.emit("vector", lambda v, s=s: v.tensor_scalar(
                out=xout[:, s, :], in0=xin[:, s, :], scalar1=stat[:, s, 2:3],
                scalar2=stat[:, s, 5:6], op0=ALU.subtract, op1=ALU.mult),
                deps=[SCL], inc=s_v)
        return p.bar(s_v)

    es_attn = contextlib.ExitStack()
    sel16 = p.sb(es_attn, "sel16", [16, 16 * 128], F32)
    state = p.sb(es_attn, "state", [128, NIT, NST], F32)
    in_w_sb = p.sb(es_attn, "in_w_sb", [128, 8, IF + IC], BF16)
    xproj_w_sb = p.sb(es_attn, "xproj_w_sb", [128, 16, 96], BF16)
    dt_w_sb = p.sb(es_attn, "dt_w_sb", [64, IC], BF16)
    out_w_sb = p.sb(es_attn, "out_w_sb", [128, 8, 1024], BF16)
    in_b_sb = p.sb(es_attn, "in_b_sb", [128, 24], F32)
    conv_b_sb = p.sb(es_attn, "conv_b_sb", [128, 16], F32)
    conv_w_sb = p.sb(es_attn, "conv_w_sb", [128, 16, KC], F32)
    dt_b_sb = p.sb(es_attn, "dt_b_sb", [128, 8], F32)
    A_sb = p.sb(es_attn, "A_sb", [128, 8, NST], F32)
    D_sb = p.sb(es_attn, "D_sb", [128, 8], F32)
    ns_sb = p.sb(es_attn, "ns_sb", [128, 8], F32)
    xt = p.sb(es_attn, "xt", [128, NSUB, 1024], F32)
    xnt = xt
    xn_fm = p.sb(es_attn, "xn_fm", [128, 8, TC], BF16)
    hid = p.sb(es_attn, "hid", [128, NCT, TC + 3], BF16)
    cacc = p.sb(es_attn, "cacc", [128, 2, TC], F32)
    conv_s = p.sb(es_attn, "conv_s", [128, NIT, TC], F32)
    conv_bf = p.sb(es_attn, "conv_bf", [128, NCT, TC], BF16)
    gate_s = p.sb(es_attn, "gate_s", [128, NIT, TC], F32)
    ts_bf = p.sb(es_attn, "ts_bf", [64, TC], BF16)
    B_sbuf = p.sb(es_attn, "B_sbuf", [16, TC], F32)
    C_sbuf = p.sb(es_attn, "C_sbuf", [16, TC], F32)
    delta = p.sb(es_attn, "delta", [128, NIT, TC], F32)
    dh = p.sb(es_attn, "dh", [128, NIT, TC], F32)
    dA_b = [p.sb(es_attn, f"dA_b{i}", [128, NIT, TC + 1], F32)
            for i in range(2)]
    X_b = [p.sb(es_attn, f"X_b{i}", [128, NIT, TC + 1], F32)
           for i in range(2)]
    sout = p.sb(es_attn, "sout", [128, NIT, TC + 1], F32)
    tmpy = p.sb(es_attn, "tmpy", [128, NIT, TC], F32)
    y_acc = p.sb(es_attn, "y_acc", [128, NIT, TC], F32)
    ygate = p.sb(es_attn, "ygate", [128, NIT, TC], BF16)
    o_sb = cacc
    junk = conv_bf[:, 0:4, :].rearrange("p a t -> p (a t)")

    def g_init(g):
        g.iota(iotf[:], pattern=[[1, 128]], base=0, channel_multiplier=0,
               allow_small_or_imprecise_dtypes=True)
        g.iota(iotc[:], pattern=[[0, 1]], base=0, channel_multiplier=1,
               allow_small_or_imprecise_dtypes=True)
        g.iota(sel16[:], pattern=[[1, 16], [0, 128]], base=0, channel_multiplier=0,
               allow_small_or_imprecise_dtypes=True)
        g.memset(state[:], 0.0)
        for j, v in enumerate(CVALS):
            g.memset(cst[:, j:j + 1], v)
        return g.memset(stat[:], 0.0)
    p.emit("gpsimd", g_init, inc=s_g)
    G_INIT = p.bar(s_g)

    def v_init(v):
        v.tensor_scalar(out=ident[:], in0=iotf[:], scalar1=iotc[:], scalar2=None,
                        op0=ALU.is_equal)
        return v.tensor_scalar(out=sel16[:], in0=sel16[:], scalar1=iotc[0:16, :],
                               scalar2=None, op0=ALU.is_equal)
    p.emit("vector", v_init, deps=[G_INIT], inc=s_v)
    V_INIT = p.bar(s_v)


    g_dma(in_w_sb[:], in_w.rearrange("(a p) c -> p a c", p=128))
    g_dma(xproj_w_sb[:], xproj_w.rearrange("(a p) c -> p a c", p=128))
    g_dma(dt_w_sb[:], dt_w)
    g_dma(out_w_sb[:], out_w.rearrange("(a p) c -> p a c", p=128))
    s_dma(in_b_sb[:], in_b)
    s_dma(conv_b_sb[:], conv_b)
    s_dma(conv_w_sb[:], conv_w)
    s_dma(dt_b_sb[:], dt_b)
    s_dma(D_sb[:], Dvec)
    s_dma(ns_sb[:], norm_scale)
    s_dma(A_sb[:], A_log.rearrange("(a p) n -> p a n", p=128))
    W_INIT = p.bar(s_dw)

    def a_initA(a):
        ap = A_sb[:].rearrange("p a n -> p (a n)")
        return a.activation(out=ap, in_=ap, func=AF.Exp)
    p.emit("scalar", a_initA, deps=[W_INIT], inc=s_a)
    def v_negA(v):
        ap = A_sb[:].rearrange("p a n -> p (a n)")
        return v.tensor_scalar(out=ap, in0=ap, scalar1=-1.0, scalar2=None,
                               op0=ALU.mult)
    p.emit("vector", v_negA, deps=[p.bar(s_a)], inc=s_v, pad=True)
    A_READY = p.bar(s_v)

    def g_zeros(g):
        g.memset(dA_b[0][:, :, 0:1], 0.0)
        g.memset(dA_b[1][:, :, 0:1], 0.0)
        return g.memset(hid[:, :, 0:3], 0.0)
    p.emit("gpsimd", g_zeros, inc=s_g)
    HALO0 = p.bar(s_g)

    prev = {}

    def emit_tail(tk, half, tloc, SCAN, YADD_G):
        for it in range(NIT):
            def v_yd(v, it=it):
                return v.scalar_tensor_tensor(out=y_acc[:, it, :],
                                              in0=conv_s[:, it, :],
                                              scalar=D_sb[:, it:it + 1],
                                              in1=y_acc[:, it, :],
                                              op0=ALU.mult, op1=ALU.add)
            p.emit("vector", v_yd, deps=[SCAN, W_INIT, YADD_G], inc=s_v)
            ygdeps = [prev["op_pe"]] if tk and it == 0 else []
            def v_yg(v, it=it):
                return v.tensor_tensor(out=ygate[:, it, :], in0=y_acc[:, it, :],
                                       in1=gate_s[:, it, :], op=ALU.mult)
            p.emit("vector", v_yg, deps=ygdeps, inc=s_v)
        YG = p.bar(s_v)
        prev["yd_v"] = YG
        prev["yg_v"] = YG
        for hg in range(4):
            oev_guard = [(s_do, 64 * tk + 16 * hg)] if (tk or hg) else []
            for ho in range(2 * hg, 2 * hg + 2):
                def mk_op(t, bank, ho=ho):
                    last = None
                    for it in range(NIT):
                        last = t.matmul(bank[:, 0:TC],
                                        out_w_sb[:, it, ho * 128:(ho + 1) * 128],
                                        ygate[:, it, :], start=(it == 0),
                                        stop=(it == 7))
                    return last
                pipeM.produce(mk_op, deps=[YG])
                def ev_op(a, bank, ho=ho):
                    return a.activation(out=o_sb[:, ho % 2, :], in_=bank[:, 0:TC],
                                        func=AF.Copy)
                pipeM.consume("scalar", ev_op, deps=oev_guard)
            OUTP = p.bar(pipeM.ev_done)
            def d_out(s_, half=half, tloc=tloc, hg=hg):
                return s_.dma_start(
                    out=cc_in.ap()[half * 1024 + hg * 256:
                                   half * 1024 + (hg + 1) * 256, tloc:tloc + TC]
                        .rearrange("(a p) t -> p a t", p=128),
                    in_=o_sb[:])
            p.emit("sync", d_out, deps=[OUTP], inc=s_do, inc_amt=16)
        prev["op_pe"] = (pipeM.pe_done, pipeM.uses)

    for k in range(NCH):
        half = (k * TC) // TOKF
        tloc = k * TC - half * TOKF

        d0 = [prev["xn_v"], prev["sq_a"]] if k else []
        s_dma(xt[:], x[k * TC:(k + 1) * TC, :].rearrange("(a p) h -> p a h", p=128),
              inc=s_dx, deps=d0)
        XRDY = (s_dx, 16 * (k + 1))

        lnw = [prev["xnt_pe"]] if k else []
        if k:
            p.emit("vector", lambda v: v.engine_nop(), deps=[prev["xnt_pe"]],
                   inc=s_v)
        XN = emit_ln(xt, xnt, NSUB, True, XRDY, junk)
        prev["xn_v"] = XN
        prev["sq_a"] = p.bar(s_a)

        if k:
            emit_tail(*prev.pop("tail"))

        inproj_pe_guard = prev.get("inproj_pe")
        for ht in range(8):
            def mk_t(t, bank, ht=ht):
                last = None
                for s in range(NSUB):
                    last = t.transpose(bank[:, s * 128:(s + 1) * 128],
                                       xnt[:, s, ht * 128:(ht + 1) * 128], ident[:])
                return last
            pipeT.produce(mk_t, deps=[XN, V_INIT])
            evdeps = [W_INIT] + ([inproj_pe_guard] if inproj_pe_guard else [])
            def ev_t(v, bank, ht=ht):
                return v.tensor_scalar(out=xn_fm[:, ht, :], in0=bank[:, 0:TC],
                                       scalar1=ns_sb[:, ht:ht + 1], scalar2=None,
                                       op0=ALU.mult)
            pipeT.consume("vector", ev_t, deps=evdeps)
        XNFM = p.bar(pipeT.ev_done)
        prev["xnt_pe"] = (pipeT.pe_done, pipeT.uses)

        hide_dep = [prev["conv_v"]] if k else [HALO0]
        gate_dep = [prev["yg_v"]] if k else []
        hid_ct = {}
        for ct in range(24):
            def mk_mm(t, bank, ct=ct):
                last = None
                for ht in range(8):
                    last = t.matmul(bank[:, 0:TC],
                                    in_w_sb[:, ht, ct * 128:(ct + 1) * 128],
                                    xn_fm[:, ht, :], start=(ht == 0), stop=(ht == 7))
                return last
            pipeM.produce(mk_mm, deps=[XNFM, W_INIT])
            if ct < 16:
                def ev_h(a, bank, ct=ct):
                    return a.activation(out=hid[:, ct, 3:3 + TC], in_=bank[:, 0:TC],
                                        func=AF.Identity, bias=in_b_sb[:, ct:ct + 1])
                pipeM.consume("scalar", ev_h, deps=hide_dep + [W_INIT])
                hid_ct[ct] = (pipeM.ev_done, pipeM.uses)
            else:
                def ev_g(a, bank, ct=ct):
                    return a.activation(out=gate_s[:, ct - 16, :], in_=bank[:, 0:TC],
                                        func=AF.Silu, bias=in_b_sb[:, ct:ct + 1])
                pipeM.consume("scalar", ev_g, deps=gate_dep + [W_INIT])
        HID = p.bar(pipeM.ev_done)
        prev["inproj_pe"] = (pipeM.pe_done, pipeM.uses)

        silu_guard = [prev["xproj_pe"]] if k else []
        dh_guard = [prev["yd_v"]] if k else []
        silu_ct = {}
        for ct in range(16):
            pr = ct % 2
            def v_conv(v, ct=ct, pr=pr):
                v.tensor_scalar(out=cacc[:, pr, :], in0=hid[:, ct, 0:TC],
                                scalar1=conv_w_sb[:, ct, 0:1], scalar2=None,
                                op0=ALU.mult)
                last = None
                for kk in range(1, KC):
                    last = v.scalar_tensor_tensor(
                        out=cacc[:, pr, :], in0=hid[:, ct, kk:kk + TC],
                        scalar=conv_w_sb[:, ct, kk:kk + 1],
                        in1=cacc[:, pr, :], op0=ALU.mult, op1=ALU.add)
                return last
            wdeps = [hid_ct[ct], W_INIT]
            if k:
                wdeps.append((s_do, 64 * k))
            if ct >= 2:
                wdeps.append(silu_ct[ct - 2])
            p.emit("vector", v_conv, deps=wdeps, inc=s_v)
            CONVD = p.bar(s_v)
            def a_silu(a, ct=ct, pr=pr):
                last = a.activation(out=conv_bf[:, ct, :], in_=cacc[:, pr, :],
                                    func=AF.Silu, bias=conv_b_sb[:, ct:ct + 1])
                if ct < 8:
                    last = a.activation(out=conv_s[:, ct, :], in_=cacc[:, pr, :],
                                        func=AF.Silu, bias=conv_b_sb[:, ct:ct + 1])
                return last
            p.emit("scalar", a_silu,
                   deps=[CONVD, W_INIT] + silu_guard + (dh_guard if ct < 8 else []),
                   inc=s_a)
            silu_ct[ct] = p.bar(s_a)
        CONV = p.bar(s_a)
        def v_halo(v):
            return v.tensor_copy(hid[:, :, 0:3], hid[:, :, TC:TC + 3])
        p.emit("vector", v_halo, deps=[CONV, p.bar(s_v)], inc=s_v)
        prev["conv_v"] = p.bar(s_v)

        tsev_guard = [prev["dt_pe"]] if k else []
        bcev_guard = [prev["brep_pe"], prev["crep_pe"]] if k else []
        def mk_ts(t, bank):
            last = None
            for ct in range(16):
                last = t.matmul(bank[0:64, 0:TC], xproj_w_sb[:, ct, 0:64],
                                conv_bf[:, ct, :], start=(ct == 0), stop=(ct == 15))
            return last
        pipeM.produce(mk_ts, deps=[CONV])
        def ev_ts(v, bank):
            return v.tensor_copy(ts_bf[:], bank[0:64, 0:TC])
        pipeM.consume("vector", ev_ts, deps=tsev_guard)
        def mk_bc(t, bank):
            last = None
            for ct in range(16):
                t.matmul(bank[0:16, 0:TC], xproj_w_sb[:, ct, 64:80],
                         conv_bf[:, ct, :], start=(ct == 0), stop=(ct == 15))
            for ct in range(16):
                last = t.matmul(bank[32:48, 0:TC], xproj_w_sb[:, ct, 80:96],
                                conv_bf[:, ct, :], start=(ct == 0), stop=(ct == 15))
            return last
        pipeM.produce(mk_bc, deps=[CONV])
        def ev_bc(v, bank):
            v.tensor_copy(B_sbuf[:], bank[0:16, 0:TC])
            return v.tensor_copy(C_sbuf[:], bank[32:48, 0:TC])
        pipeM.consume("vector", ev_bc, deps=bcev_guard, pad=True)
        TSBC = p.bar(pipeM.ev_done)
        prev["xproj_pe"] = (pipeM.pe_done, pipeM.uses)

        dA_guard = [prev["dh_v"]] if k else []
        for it in range(NIT):
            def mk_dt(t, bank, it=it):
                return t.matmul(bank[:, 0:TC], dt_w_sb[:, it * 128:(it + 1) * 128],
                                ts_bf[:], start=True, stop=True)
            pipeM.produce(mk_dt, deps=[TSBC])
            def ev_dt(a, bank, it=it):
                a.activation(out=delta[:, it, :], in_=bank[:, 0:TC], func=AF.Exp,
                             bias=dt_b_sb[:, it:it + 1])
                return a.activation(out=delta[:, it, :], in_=delta[:, it, :],
                                    func=AF.Ln, bias=1.0)
            pipeM.consume("scalar", ev_dt, deps=[W_INIT, G_INIT] + dA_guard)
        DELTA = p.bar(pipeM.ev_done)
        prev["dt_pe"] = (pipeM.pe_done, pipeM.uses)

        for it in range(NIT):
            p.emit("vector", lambda v, it=it: v.tensor_tensor(
                out=dh[:, it, :], in0=delta[:, it, :], in1=conv_s[:, it, :],
                op=ALU.mult), deps=[DELTA, CONV], inc=s_v)
        DH = p.bar(s_v)
        prev["dh_v"] = DH

        scan_done = {}
        gx_cnt = {}
        gy_cnt = {}
        acc_cnt = {}
        cb_cnt = {}
        cc_cnt = {}
        for n in range(16):
            q = n % 2
            def mk_brep(t, bank, n=n):
                return t.matmul(bank[:, 0:TC], sel16[:, n * 128:(n + 1) * 128],
                                B_sbuf[:], start=True, stop=True)
            pipeB.produce(mk_brep, deps=[TSBC, V_INIT])
            bB = pipeB.banks[(pipeB.uses - 1) % 2]
            BRDY = (pipeB.pe_done, pipeB.uses)
            def mk_crep(t, bank, n=n):
                return t.matmul(bank[:, 0:TC], sel16[:, n * 128:(n + 1) * 128],
                                C_sbuf[:], start=True, stop=True)
            pipeC.produce(mk_crep, deps=[TSBC, V_INIT])
            bC = pipeC.banks[(pipeC.uses - 1) % 2]
            CRDY = (pipeC.pe_done, pipeC.uses)
            dadeps = [DELTA, A_READY, G_INIT]
            if n >= 2:
                dadeps.append(scan_done[n - 2])
            elif k:
                dadeps.append(prev["scan_m2"][n])
            for it in range(NIT):
                p.emit("scalar", lambda a, it=it, n=n, q=q: a.activation(
                    out=dA_b[q][:, it, 1:TC + 1], in_=delta[:, it, :],
                    func=AF.Exp, scale=A_sb[:, it, n:n + 1]),
                    deps=dadeps, inc=s_a)
            DA = p.bar(s_a)
            def v_x0(v, n=n, q=q):
                return v.tensor_copy(X_b[q][:, :, 0], state[:, :, n])
            p.emit("vector", v_x0, deps=[DH, G_INIT], inc=s_v)
            def v_X(v, bB=bB, q=q):
                return v.tensor_tensor(
                    out=X_b[q][:, :, 1:TC + 1], in0=dh[:],
                    in1=bB[:, 0:TC].rearrange("p (o t) -> p o t", o=1)
                    .broadcast_to([128, NIT, TC]), op=ALU.mult)
            p.emit("vector", v_X, deps=[BRDY], inc=pipeB.ev_done)
            def v_scan(v, q=q):
                return v.tensor_tensor_scan(
                    out=sout[:].rearrange("p a t -> p (a t)"),
                    data0=dA_b[q][:].rearrange("p a t -> p (a t)"),
                    data1=X_b[q][:].rearrange("p a t -> p (a t)"),
                    initial=0.0, op0=ALU.mult, op1=ALU.add)
            p.emit("vector", v_scan, deps=[DA], inc=s_v)
            scan_done[n] = p.bar(s_v)
            tgt = y_acc if n == 0 else tmpy
            ymdeps = [CRDY]
            if n >= 2:
                ymdeps.append((s_gp, acc_cnt[n - 1]))
            elif k:
                ymdeps.append(prev["acc_m2"][1])
            def v_ym(v, bC=bC, tgt=tgt):
                return v.tensor_tensor(
                    out=tgt[:], in0=sout[:, :, 1:TC + 1],
                    in1=bC[:, 0:TC].rearrange("p (o t) -> p o t", o=1)
                    .broadcast_to([128, NIT, TC]), op=ALU.mult)
            p.emit("vector", v_ym, deps=ymdeps, inc=pipeC.ev_done)
            YMV = (pipeC.ev_done, p.count(pipeC.ev_done))
            if n > 0:
                def g_ya(g):
                    return g.tensor_tensor(out=y_acc[:], in0=y_acc[:],
                                           in1=tmpy[:], op=ALU.add)
                p.emit("gpsimd", g_ya, deps=[YMV], inc=s_gp)
            acc_cnt[n] = p.count(s_gp)
            def v_st(v, n=n):
                return v.tensor_copy(state[:, :, n], sout[:, :, TC])
            p.emit("vector", v_st, inc=s_v)
        prev["scan_m2"] = [scan_done[14], scan_done[15]]
        prev["acc_m2"] = [(s_gp, acc_cnt[14]), (s_gp, acc_cnt[15])]
        prev["brep_pe"] = (pipeB.pe_done, pipeB.uses)
        prev["crep_pe"] = (pipeC.pe_done, pipeC.uses)
        YADD_G = (s_gp, acc_cnt[15])
        SCAN = p.bar(s_v)
        prev["tail"] = (k, half, tloc, SCAN, YADD_G)

        if DBG and k == 0:
            taps = [("stat", stat), ("xt", xt), ("xnt", xnt), ("xn_fm", xn_fm), ("hid", hid),
                    ("conv_bf", conv_bf), ("conv_s", conv_s), ("gate_s", gate_s),
                    ("ts_bf", ts_bf), ("B_sbuf", B_sbuf), ("C_sbuf", C_sbuf),
                    ("delta", delta), ("dh", dh), ("y_acc", y_acc),
                    ("ygate", ygate), ("o_sb", o_sb), ("state", state)]
            s_tap = p.sem("s_tap")
            alldeps = [p.bar(s_v), p.bar(s_a), (s_do, 64), p.bar(pipeM.ev_done),
                       p.bar(s_gp)]
            for nm, tile in taps:
                ap = tile[:]
                od = nc.dram_tensor("dbg_" + nm, list(ap.shape), ap.dtype,
                                    kind="ExternalOutput").ap()
                p.emit("sync", lambda s_, od=od, ap=ap: s_.dma_start(
                    out=od, in_=ap), deps=alldeps, inc="s_tap", inc_amt=16)
            TAPS = p.bar(s_tap)
            for eng in ("vector", "scalar", "gpsimd", "sync"):
                p.emit(eng, lambda en: None, deps=[TAPS])
            p.emit("tensor", lambda t: None, deps=[TAPS])

    emit_tail(*prev.pop("tail"))
    ALLOUT = (s_do, 64 * NCH)

    if not dbg_ffn_input:
        def g_rs(g):
            return g.collective_compute(
                "ReduceScatter", ALU.add,
                replica_groups=[[0, 1], [2, 3], [4, 5], [6, 7]],
                ins=[cc_in.ap().opt()], outs=[rs_out.ap().opt()])
        p.emit("gpsimd", g_rs, deps=[ALLOUT], inc=s_cc)
    else:
        p.emit("gpsimd", lambda g: g.engine_nop(), deps=[ALLOUT], inc=s_cc)
    RS = p.bar(s_cc)

    PHB = [RS, ALLOUT, p.bar(s_v), p.bar(s_a), p.bar(pipeM.ev_done),
           p.bar(pipeT.ev_done), p.bar(pipeB.ev_done), p.bar(pipeC.ev_done)]
    es_attn.close()

    es_fp = contextlib.ExitStack()
    w_eT = p.sb(es_fp, "w_eT", [8, TOKF], F32)
    sel8 = p.sb(es_fp, "sel8", [8, 8 * 128], F32)
    b1_sb = p.sb(es_fp, "b1_sb", [128, 32, EE], F32)
    b2_sb = p.sb(es_fp, "b2_sb", [128, 8, EE], F32)
    w1_sb0 = p.sb(es_fp, "w1_sb0", [128, 8, FFD // 2], BF16)
    w2_sb0 = p.sb(es_fp, "w2_sb0", [128, 16, 1024], BF16)

    es_pro = contextlib.ExitStack()
    rw_sb = p.sb(es_pro, "rw_sb", [128, 8, EE], BF16)
    outb_sb = p.sb(es_pro, "outb_sb", [128, 1024], F32)
    obrow = p.sb(es_pro, "obrow", [1, 1024], F32)
    ones1 = p.sb(es_pro, "ones1", [1, 128], F32)
    lgt = p.sb(es_pro, "lgt", [128, 4, EE], F32)
    r8 = p.sb(es_pro, "r8", [128, 12, EE], F32)
    wsc = p.sb(es_pro, "wsc", [128, 24], F32)
    io0 = p.sb(es_pro, "io0", [128, 4, EE], F32)
    io1k = p.sb(es_pro, "io1k", [128, 4, EE], F32)
    wdense = p.sb(es_pro, "wdense", [128, 4, EE], F32)
    xrt = p.sb(es_pro, "xrt", [128, 4, 1024], F32)
    h2t = p.sb(es_pro, "h2t", [128, 4, 1024], F32)
    h2nt = p.sb(es_pro, "h2nt", [128, 4, 1024], F32)
    at_fm = p.sb(es_pro, "at_fm", [128, 8, 512], F32)
    hn_fm = p.sb(es_pro, "hn_fm", [128, 8, 512], BF16)
    junkf = p.sb(es_pro, "junkf", [128, 1024], BF16)

    g_dma(rw_sb[:], router_wT.rearrange("(a p) e -> p a e", p=128), deps=PHB)
    s_dma(b1_sb[:], b1, deps=PHB)
    s_dma(b2_sb[:], b2, deps=PHB)
    s_dma(obrow[:], out_b, deps=PHB)
    FW0 = p.bar(s_dw)

    def g_initf(g):
        g.memset(ones1[:], 1.0)
        g.iota(io0[:], pattern=[[0, 4], [1, 8]], base=0,
               channel_multiplier=0, allow_small_or_imprecise_dtypes=True)
        g.iota(io1k[:], pattern=[[0, 4], [1, 8]], base=1000,
               channel_multiplier=0, allow_small_or_imprecise_dtypes=True)
        return g.iota(sel8[:], pattern=[[1, 8], [0, 128]], base=0,
                      channel_multiplier=0, allow_small_or_imprecise_dtypes=True)
    p.emit("gpsimd", g_initf, deps=PHB, inc=s_g)
    GF = p.bar(s_g)
    def v_self8(v):
        return v.tensor_scalar(out=sel8[:], in0=sel8[:], scalar1=iotc[0:8, :],
                               scalar2=None, op0=ALU.is_equal)
    p.emit("vector", v_self8, deps=[GF] + PHB, inc=s_v)
    SEL8 = p.bar(s_v)

    for hhalf in range(2):
        def mk_ob(t, bank, hh=hhalf):
            return t.matmul(bank[:, 0:512], ones1[:],
                            obrow[0:1, hh * 512:(hh + 1) * 512],
                            start=True, stop=True)
        pipeM.produce(mk_ob, deps=[FW0, GF] + PHB)
        def ev_ob(v, bank, hh=hhalf):
            return v.tensor_copy(outb_sb[:, hh * 512:(hh + 1) * 512],
                                 bank[:, 0:512])
        pipeM.consume("vector", ev_ob, deps=PHB)
    OBREP = p.bar(pipeM.ev_done)

    fprev = {}
    for sl in range(NSL):
        c0 = sl * 512
        xdep = [fprev["h2_v"]] if sl else PHB
        s_dma(xrt[:], xr[c0:c0 + 512, :].rearrange("(a p) h -> p a h", p=128),
              inc=s_dx, deps=xdep)
        adep = [fprev["at_pe"]] if sl else PHB
        s_dma(at_fm[:], rs_out.ap()[:, c0:c0 + 512]
              .rearrange("(a p) t -> p a t", p=128), inc=s_dx, deps=adep + [RS])
        XA = (s_dx, p.count(s_dx))

        for s in range(4):
            p.emit("vector", lambda v, s=s: v.tensor_tensor(
                out=xrt[:, s, :], in0=xrt[:, s, :], in1=outb_sb[:], op=ALU.add),
                deps=[XA, OBREP], inc=s_v)
        XB = p.bar(s_v)

        for tt in range(4):
            for hh in range(2):
                def mk_at(t, bank, tt=tt, hh=hh):
                    last = None
                    for ho in range(4 * hh, 4 * hh + 4):
                        last = t.transpose(
                            bank[:, (ho - 4 * hh) * 128:(ho - 4 * hh + 1) * 128],
                            at_fm[:, ho, tt * 128:(tt + 1) * 128], ident[:])
                    return last
                pipeT.produce(mk_at, deps=[XA, V_INIT] + (PHB if sl == 0 else []))
                def ev_at(v, bank, tt=tt, hh=hh):
                    return v.tensor_tensor(out=h2t[:, tt, hh * 512:(hh + 1) * 512],
                                           in0=bank[:, 0:512],
                                           in1=xrt[:, tt, hh * 512:(hh + 1) * 512],
                                           op=ALU.add)
                h2t_guard = []
                if "h2_dma" in fprev:
                    h2t_guard = [fprev["h2_dma"], fprev["sq2"]]
                pipeT.consume("vector", ev_at, deps=[XB] + h2t_guard)
        fprev["at_pe"] = (pipeT.pe_done, pipeT.uses)
        H2 = p.bar(pipeT.ev_done)

        s_dma(h2_dram.ap()[c0:c0 + 512, :].rearrange("(a p) h -> p a h", p=128),
              h2t[:], inc=s_dh2, deps=[H2])
        fprev["h2_dma"] = (s_dh2, p.count(s_dh2))

        if sl:
            p.emit("vector", lambda v: v.engine_nop(),
                   deps=[fprev["hnt_pe"]], inc=s_v)
        H2N = emit_ln(h2t, h2nt, 4, False, H2, junkf[:])
        fprev["h2_v"] = H2N
        fprev["sq2"] = p.bar(s_a)

        for ho in range(8):
            def mk_hn(t, bank, ho=ho):
                last = None
                for tt in range(4):
                    last = t.transpose(bank[:, tt * 128:(tt + 1) * 128],
                                       h2nt[:, tt, ho * 128:(ho + 1) * 128],
                                       ident[:])
                return last
            pipeT.produce(mk_hn, deps=[H2N])
            fprev["hnt_pe"] = (pipeT.pe_done, pipeT.uses)
            def ev_hn(v, bank, ho=ho):
                return v.tensor_copy(hn_fm[:, ho, :], bank[:, 0:512])
            hn_guard = []
            if "r_pe" in fprev:
                hn_guard = [fprev["r_pe"], fprev["hn_dma"]]
            pipeT.consume("vector", ev_hn, deps=hn_guard)
        HNF = p.bar(pipeT.ev_done)
        s_dma(h2n_dram.ap()[:, c0:c0 + 512].rearrange("(a p) t -> p a t", p=128),
              hn_fm[:], inc=s_dh2, deps=[HNF])
        fprev["hn_dma"] = (s_dh2, p.count(s_dh2))

        def mk_r(t, bank):
            last = None
            for tt in range(4):
                for ht in range(8):
                    last = t.matmul(bank[:, tt * 8:(tt + 1) * 8],
                                    hn_fm[:, ht, tt * 128:(tt + 1) * 128],
                                    rw_sb[:, ht, :], start=(ht == 0),
                                    stop=(ht == 7))
            return last
        pipeM.produce(mk_r, deps=[HNF, FW0])
        fprev["r_pe"] = (pipeM.pe_done, pipeM.uses)
        def ev_r(v, bank):
            return v.tensor_copy(lgt[:].rearrange("p a e -> p (a e)"),
                                 bank[:, 0:32])
        pipeM.consume("vector", ev_r,
                      deps=[fprev["wd_pe"]] if "wd_pe" in fprev else [],
                      pad=True)
        LG = p.bar(pipeM.ev_done)

        def bc(ap):
            return ap.broadcast_to([128, 4, EE])
        lg3 = lgt[:]
        m1 = wsc[:, 0:4].rearrange("p (a o) -> p a o", o=1)
        ix1 = wsc[:, 4:8].rearrange("p (a o) -> p a o", o=1)
        m2 = wsc[:, 8:12].rearrange("p (a o) -> p a o", o=1)
        ix2 = wsc[:, 12:16].rearrange("p (a o) -> p a o", o=1)
        w1c = wsc[:, 16:20].rearrange("p (a o) -> p a o", o=1)
        w2c = wsc[:, 20:24].rearrange("p (a o) -> p a o", o=1)
        r8a = r8[:, 0:4, :]
        r8b = r8[:, 4:8, :]
        r8c = r8[:, 8:12, :]
        RV = [LG, GF]
        def step(fn, eng="vector", deps=()):
            p.emit(eng, fn, deps=list(deps) + RV, inc=s_v if eng == "vector"
                   else s_a, pad=True)
        step(lambda v: v.tensor_reduce(m1, lg3, axis=AX.X, op=ALU.max))
        step(lambda v: v.tensor_tensor(out=r8a, in0=lg3, in1=bc(m1),
                                       op=ALU.is_equal))
        step(lambda v: v.scalar_tensor_tensor(out=r8b, in0=r8a, scalar=-1000.0,
                                              in1=io1k[:], op0=ALU.mult,
                                              op1=ALU.add))
        step(lambda v: v.tensor_reduce(ix1, r8b, axis=AX.X, op=ALU.min))
        step(lambda v: v.tensor_tensor(out=r8a, in0=io0[:], in1=bc(ix1),
                                       op=ALU.is_equal))
        step(lambda v: v.scalar_tensor_tensor(out=r8b, in0=r8a, scalar=-1e30,
                                              in1=lg3, op0=ALU.mult,
                                              op1=ALU.add))
        step(lambda v: v.tensor_reduce(m2, r8b, axis=AX.X, op=ALU.max))
        step(lambda v: v.tensor_tensor(out=r8c, in0=r8b, in1=bc(m2),
                                       op=ALU.is_equal))
        step(lambda v: v.scalar_tensor_tensor(out=r8b, in0=r8c, scalar=-1000.0,
                                              in1=io1k[:], op0=ALU.mult,
                                              op1=ALU.add))
        step(lambda v: v.tensor_reduce(ix2, r8b, axis=AX.X, op=ALU.min))
        step(lambda v: v.tensor_tensor(out=r8c, in0=io0[:], in1=bc(ix2),
                                       op=ALU.is_equal))
        step(lambda v: v.tensor_tensor(out=w1c, in0=m1, in1=m2,
                                       op=ALU.subtract))
        W1D = p.bar(s_v)
        def a_w1(a):
            a.activation(out=w1c, in_=w1c, func=AF.Tanh, scale=0.5)
            return a.activation(out=w1c, in_=w1c, func=AF.Identity, scale=0.5,
                                bias=0.5)
        step(a_w1, eng="scalar", deps=[W1D, G_INIT])
        W1S = p.bar(s_a)
        step(lambda v: v.tensor_scalar(out=w2c, in0=w1c, scalar1=-1.0,
                                       scalar2=1.0, op0=ALU.mult, op1=ALU.add),
             deps=[W1S])
        step(lambda v: v.tensor_tensor(out=r8a, in0=r8a, in1=bc(w1c),
                                       op=ALU.mult))
        step(lambda v: v.tensor_tensor(out=r8c, in0=r8c, in1=bc(w2c),
                                       op=ALU.mult))
        step(lambda v: v.tensor_tensor(out=wdense[:], in0=r8a, in1=r8c,
                                       op=ALU.add))
        WD = p.bar(s_v)

        def mk_wt(t, bank):
            last = None
            for tt in range(4):
                last = t.transpose(bank[0:8, tt * 128:(tt + 1) * 128],
                                   wdense[:, tt, :], ident[:])
            return last
        pipeM.produce(mk_wt, deps=[WD])
        def ev_wt(v, bank, sl=sl):
            return v.tensor_copy(w_eT[:, sl * 512:(sl + 1) * 512], bank[0:8, 0:512])
        pipeM.consume("vector", ev_wt)
        fprev["wd_pe"] = (pipeM.pe_done, pipeM.uses)
    fprev["hn_pe"] = (pipeT.pe_done, pipeT.uses)
    PRO = [p.bar(pipeM.ev_done), p.bar(s_dh2), p.bar(s_v), p.bar(s_a),
           (pipeM.pe_done, pipeM.uses), (pipeT.pe_done, pipeT.uses),
           p.bar(pipeT.ev_done), p.bar(s_dx), p.bar(s_g)]
    WET = p.bar(pipeM.ev_done)
    es_pro.close()
    es_exp = contextlib.ExitStack()
    s_ac2 = p.sem("s_ac2")
    w1_sb = [w1_sb0, p.sb(es_exp, "w1_sb1", [128, 8, FFD // 2], BF16)]
    w2_sb = [w2_sb0, p.sb(es_exp, "w2_sb1", [128, 16, 1024], BF16)]
    h1_sb = p.sb(es_exp, "h1_sb", [128, 16, 512], BF16)
    h2n_sb = [p.sb(es_exp, f"h2n_sb{i}", [128, 8, 512], BF16)
              for i in range(2)]
    tmp_sb = p.sb(es_exp, "tmp_sb", [128, 8, 512], F32)
    wrep_sb = [p.sb(es_exp, f"wrep_sb{i}", [128, 512], F32) for i in range(2)]

    NU = 2 * EE

    def emit_wload(u, deps):
        e, fh = u // 2, u % 2
        pu = u % 2
        g_dma(w1_sb[pu][:],
              w1T[e][:, fh * (FFD // 2):(fh + 1) * (FFD // 2)]
              .rearrange("(a p) f -> p a f", p=128), inc="s_dwf", deps=deps)
        g_dma(w2_sb[pu][:],
              w2T[e][fh * (FFD // 2):(fh + 1) * (FFD // 2), :]
              .rearrange("(a p) h -> p a h", p=128), inc="s_dwf", deps=deps)

    def emit_hnload(u, tc, deps):
        s_dma(h2n_sb[tc % 2][:],
              h2n_dram.ap()[:, tc * 512:(tc + 1) * 512]
              .rearrange("(a p) t -> p a t", p=128),
              inc="s_dhn", deps=deps)

    emit_wload(0, PHB)
    emit_wload(1, PRO)
    emit_hnload(0, 0, PRO + [p.bar(s_dh2)])
    nacc = [0, 0]
    wlast = {}
    for u in range(NU):
        e, fh, pu = u // 2, u % 2, u % 2
        WF = (s_dwf, 32 * (u + 1))
        for tc in range(NSL):
            if not (u == NU - 1 and tc == NSL - 1):
                ntc = (tc + 1) % NSL
                hdep = [fprev["h1_pe"]] if "h1_pe" in fprev else PRO
                emit_hnload(u, ntc, hdep)
            def mk_wr(t, bank, e=e, tc=tc):
                return t.matmul(bank[:, 0:512], sel8[:, e * 128:(e + 1) * 128],
                                w_eT[:, tc * 512:(tc + 1) * 512],
                                start=True, stop=True)
            pipeC.produce(mk_wr, deps=[WET, SEL8])
            def ev_wr(v, bank, tc=tc):
                return v.tensor_copy(wrep_sb[tc % 2][:], bank[:, 0:512])
            pipeC.consume("vector", ev_wr,
                          deps=[fprev["wr_used"]] if "wr_used" in fprev else [])
            WR = p.bar(pipeC.ev_done)
            HN = (s_dhn, 16 * (u * NSL + tc + 1))

            for ft in range(16):
                def mk_h1(t, bank, ft=ft, pu=pu, tc=tc):
                    last = None
                    for ht in range(8):
                        last = t.matmul(bank[:, 0:512],
                                        w1_sb[pu][:, ht, ft * 128:(ft + 1) * 128],
                                        h2n_sb[tc % 2][:, ht, :],
                                        start=(ht == 0), stop=(ht == 7))
                    return last
                pipeM.produce(mk_h1, deps=[WF, HN])
                def ev_h1(a, bank, ft=ft, e=e, fh=fh):
                    return a.activation(out=h1_sb[:, ft, :], in_=bank[:, 0:512],
                                        func=AF.Gelu,
                                        bias=b1_sb[:, fh * 16 + ft, e:e + 1])
                pipeM.consume("scalar", ev_h1,
                              deps=([fprev["o2_pe"]] if "o2_pe" in fprev else [])
                              + [FW0])
            H1 = p.bar(pipeM.ev_done)
            fprev["h1_pe"] = (pipeM.pe_done, pipeM.uses)

            op = None if u == 0 else ALU.add
            for hg in range(2):
                sacc = s_acc if hg == 0 else s_ac2
                for ho in range(hg * 4, hg * 4 + 4):
                    def mk_o2(t, bank, ho=ho, pu=pu):
                        last = None
                        for ft in range(16):
                            last = t.matmul(bank[:, 0:512],
                                            w2_sb[pu][:, ft, ho * 128:(ho + 1) * 128],
                                            h1_sb[:, ft, :], start=(ft == 0),
                                            stop=(ft == 15))
                        return last
                    pipeB.produce(mk_o2, deps=[H1, WF])
                    tmpdep = [(sacc, nacc[hg] * 16)]
                    if fh == 0:
                        def ev_o2(v, bank, ho=ho, tc=tc):
                            return v.tensor_tensor(
                                out=tmp_sb[:, ho, :], in0=bank[:, 0:512],
                                in1=wrep_sb[tc % 2][:], op=ALU.mult)
                    else:
                        def ev_o2(v, bank, ho=ho, e=e, tc=tc):
                            return v.scalar_tensor_tensor(
                                out=tmp_sb[:, ho, :], in0=bank[:, 0:512],
                                scalar=b2_sb[:, ho, e:e + 1],
                                in1=wrep_sb[tc % 2][:],
                                op0=ALU.add, op1=ALU.mult)
                    pipeB.consume("vector", ev_o2, deps=tmpdep + [WR, FW0])
                O2 = p.bar(pipeB.ev_done)
                def d_acc(g, tc=tc, op=op, hg=hg):
                    kw = {} if op is None else dict(accum_op=op)
                    return g.dma_start(
                        out=acc_fm.ap()[hg * 512:(hg + 1) * 512,
                                        tc * 512:(tc + 1) * 512]
                        .rearrange("(a p) t -> p a t", p=128),
                        in_=tmp_sb[:, hg * 4:hg * 4 + 4, :], **kw)
                p.emit("gpsimd", d_acc, deps=[O2, (sacc, nacc[hg] * 16)],
                       inc="s_acc" if hg == 0 else "s_ac2", inc_amt=16)
                nacc[hg] += 1
            fprev["o2_pe"] = (pipeB.pe_done, pipeB.uses)
            fprev["wr_used"] = p.bar(pipeB.ev_done)
        wlast[pu] = [(pipeB.pe_done, pipeB.uses), (pipeM.pe_done, pipeM.uses)]
        if u + 2 < NU:
            emit_wload(u + 2, wlast[pu])

    ACC = [(s_acc, nacc[0] * 16), (s_ac2, nacc[1] * 16)]
    EXP = ACC + [p.bar(pipeB.ev_done), p.bar(pipeC.ev_done), p.bar(s_v),
                 p.bar(s_a), (pipeB.pe_done, pipeB.uses),
                 (pipeM.pe_done, pipeM.uses), p.bar(s_dwf), p.bar(s_dhn)]
    es_exp.close()
    es_epi = contextlib.ExitStack()
    at2 = p.sb(es_epi, "at2", [128, 8, 512], F32)
    ht2 = p.sb(es_epi, "ht2", [128, 4, 1024], F32)
    ot_sb = p.sb(es_epi, "ot_sb", [128, 4, 1024], F32)

    for sl in range(NSL):
        c0 = sl * 512
        adep = [fprev["ep_pe"]] if "ep_pe" in fprev else EXP
        s_dma(at2[:], acc_fm.ap()[:, c0:c0 + 512]
              .rearrange("(a p) t -> p a t", p=128), inc=s_dx,
              deps=adep + ACC)
        hdep = [fprev["ep_v"]] if "ep_v" in fprev else EXP
        s_dma(ht2[:], h2_dram.ap()[c0:c0 + 512, :]
              .rearrange("(a p) h -> p a h", p=128), inc=s_dx,
              deps=hdep + [p.bar(s_dh2)])
        EXA = (s_dx, p.count(s_dx))
        for tt in range(4):
            for hh in range(2):
                def mk_ep(t, bank, tt=tt, hh=hh):
                    last = None
                    for ho in range(4 * hh, 4 * hh + 4):
                        last = t.transpose(
                            bank[:, (ho - 4 * hh) * 128:(ho - 4 * hh + 1) * 128],
                            at2[:, ho, tt * 128:(tt + 1) * 128], ident[:])
                    return last
                pipeT.produce(mk_ep, deps=[EXA] + (EXP if sl == 0 else []))
                def ev_ep(v, bank, tt=tt, hh=hh):
                    return v.tensor_tensor(out=ot_sb[:, tt, hh * 512:(hh + 1) * 512],
                                           in0=bank[:, 0:512],
                                           in1=ht2[:, tt, hh * 512:(hh + 1) * 512],
                                           op=ALU.add)
                pipeT.consume("vector", ev_ep,
                              deps=[EXA, (s_out, 16 * sl)] if sl else [EXA])
        fprev["ep_pe"] = (pipeT.pe_done, pipeT.uses)
        fprev["ep_v"] = p.bar(pipeT.ev_done)
        EOUT = p.bar(pipeT.ev_done)
        s_dma(out_hbm[c0:c0 + 512, :].rearrange("(a p) h -> p a h", p=128),
              ot_sb[:], inc="s_out", deps=[EOUT])

    if DBG:
        for nm, dr in [("cc_in", cc_in), ("rs_out", rs_out), ("h2", h2_dram),
                       ("h2n", h2n_dram), ("acc", acc_fm)]:
            if dbg_ffn_input and nm == "rs_out":
                continue
            ap = dr.ap()
            od = nc.dram_tensor("dbgd_" + nm, list(ap.shape), ap.dtype,
                                kind="ExternalOutput").ap()
            p.emit("sync", lambda s_, od=od, ap=ap: s_.dma_start(out=od, in_=ap),
                   deps=[(s_out, 16 * NSL)], inc="s_out", inc_amt=16)
        p.emit("sync", lambda s_: s_.nop(),
               deps=[("s_out", p.count("s_out"))])

    p.emit("sync", lambda s_: s_.nop(), deps=[(s_out, 16 * NSL)])
    p.emit("gpsimd", lambda g: g.engine_nop(), deps=[(s_out, 16 * NSL)])

    es_epi.close()
    es_fp.close()
    return p


def core_inputs(inputs, c):
    b, half = c // 2, c % 2
    f = np.ascontiguousarray
    own = slice(half * 1024, (half + 1) * 1024)
    oth = slice((1 - half) * 1024, (2 - half) * 1024)
    perm = np.r_[np.arange(own.start, own.stop), np.arange(oth.start, oth.stop)]
    in_w = inputs["L0_attn_in_w"]
    d = {
        "x": f(inputs["L0_attn_x"][b]),
        "xr": f(inputs["L0_attn_x"][b][half * 4096:(half + 1) * 4096]),
        "in_w": f(np.concatenate([in_w[:, perm], in_w[:, 2048 + own.start:2048 + own.stop]], 1)),
        "in_b": col128(np.concatenate([inputs["L0_attn_in_b"][perm],
                                       inputs["L0_attn_in_b"][2048:][own]])),
        "conv_w": f(inputs["L0_attn_conv_w"][:, perm].T.reshape(16, 128, 4)
                    .transpose(1, 0, 2)),
        "conv_b": col128(inputs["L0_attn_conv_b"][perm]),
        "xproj_w": f(inputs["L0_attn_xproj_w"][perm]),
        "dt_w": f(inputs["L0_attn_dt_w"][:, own]),
        "dt_b": col128(inputs["L0_attn_dt_b"][own]),
        "A_log": f(inputs["L0_attn_A_log"][own]),
        "D": col128(inputs["L0_attn_D"][own]),
        "out_w": f(inputs["L0_attn_out_w"][own]),
        "out_b": f(inputs["L0_attn_out_b"][None]),
        "norm_scale": col128(inputs["L0_attn_norm_scale"]),
        "router_wT": f(inputs["L0_ffn_router_w"].T),
        "w1T": f(inputs["L0_ffn_w1"].transpose(0, 2, 1)),
        "b1": f(inputs["L0_ffn_b1"].T.reshape(32, 128, 8).transpose(1, 0, 2)),
        "w2T": f(inputs["L0_ffn_w2"].transpose(0, 2, 1)),
        "b2": f(inputs["L0_ffn_b2"].T.reshape(8, 128, 8).transpose(1, 0, 2)),
    }
    return d


def col128(v):
    v = np.asarray(v)
    return np.ascontiguousarray(v.reshape(-1, 128).T)


_CACHE = {}


def kernel(**inputs):
    from concourse.bass_utils import run_bass_kernel_spmd
    if "nc" not in _CACHE:
        p = build_program(T=8192)
        _CACHE["nc"] = p.build()
    nc = _CACHE["nc"]
    in_maps = [core_inputs(inputs, c) for c in range(8)]
    res = run_bass_kernel_spmd(nc, in_maps, core_ids=list(range(8)))
    out = np.zeros((4, 8192, 1024), np.float32)
    for c in range(8):
        b, half = c // 2, c % 2
        out[b, half * 4096:(half + 1) * 4096] = res.results[c]["out"]
    return out
